# version 01
# baseline (speedup 1.0000x reference)
import sys
sys.path.insert(0, '/opt/trn_rl_repo')
import contextlib
import numpy as np
import ml_dtypes
import concourse.bass as bass
import concourse.mybir as mybir
from concourse.alu_op_type import AluOpType as OP
from concourse.bass_utils import run_bass_kernel_spmd

F32 = mybir.dt.float32
BF16 = mybir.dt.bfloat16
AF = mybir.ActivationFunctionType
BF = ml_dtypes.bfloat16

B, L, DM = 2, 4096, 2048
TPC = 1024
NCORE = 8

ENGINES = ("tensor", "vector", "scalar", "gpsimd", "sync")


class SemC:
    def __init__(self, h):
        self.h = h
        self.n = 0


class Prog:
    def __init__(self, name="prog"):
        self.nc = bass.Bass("TRN2", target_bir_lowering=False)
        self.q = {e: [] for e in ENGINES}
        self.stack = contextlib.ExitStack()
        self._semi = 0
        self._shared = {}
        self.proj_prev = None
        self._tp = None

    def shared(self, key, fn):
        if key not in self._shared:
            self._shared[key] = fn()
        return self._shared[key]

    def main_pool(self):
        return self.shared("mainpool", lambda: self.psum("mainpool", [128, 4, 512], F32))

    def tp_pool(self):
        return self.shared("tppool", lambda: self.psum("tppool", [128, 4, 1024], BF16))

    def get_tp(self):
        if self._tp is None:
            self._tp = Transposer(self, "g")
        return self._tp

    def wbuf(self):
        return self.shared("wbuf", lambda: self.sbuf("wbuf", [128, 2, 45, 128], BF16))

    def sem(self, name=None):
        h = self.stack.enter_context(self.nc.semaphore(name or f"s{self._semi}"))
        self._semi += 1
        return SemC(h)

    def sbuf(self, name, shape, dtype):
        return self.stack.enter_context(self.nc.sbuf_tensor(name, list(shape), dtype))

    def psum(self, name, shape, dtype=F32):
        return self.stack.enter_context(self.nc.psum_tensor(name, list(shape), dtype))

    def dram(self, name, shape, dtype, kind=None):
        if kind:
            return self.nc.dram_tensor(name, list(shape), dtype, kind=kind)
        return self.nc.dram_tensor(name, list(shape), dtype)

    def emit(self, eng, fn):
        self.q[eng].append(fn)

    def finish(self):
        with self.nc.Block() as block:
            for e in ENGINES:
                fns = self.q[e]
                if not fns:
                    continue
                def runner(engine, fns=fns):
                    for f in fns:
                        f(engine)
                getattr(block, e)(runner)
        self.stack.close()
        return self.nc


def dma(p, qeng, out, in_, sem):
    sem.n += 16
    def f(e, out=out, in_=in_, s=sem.h):
        e.dma_start(out=out, in_=in_).then_inc(s, 16)
    p.emit(qeng, f)
    return sem.n


def wait(p, eng, sem, val=None):
    v = sem.n if val is None else val
    if v > 0:
        p.emit(eng, lambda e, s=sem.h, v=v: e.wait_ge(s, v))


def inc_last(sem, k=1):
    sem.n += k
    return sem.n


class Transposer:
    def __init__(self, p, name, dtype=BF16, nslots=4):
        self.p = p
        self.pool = p.tp_pool()
        self.s_mm = p.sem()
        self.s_ev = p.sem()
        self.nslots = nslots
        self.i = 0

    def emit(self, src_ap, dst_ap, ident, evac_eng="scalar", func=None, scale=None):
        p = self.p
        slot = self.i % self.nslots
        need = self.i - self.nslots + 1
        mmv = inc_last(self.s_mm)
        def ft(e, slot=slot, src=src_ap, need=need):
            if need > 0:
                e.wait_ge(self.s_ev.h, need)
            e.transpose(self.pool[:, slot, 0:128], src, ident).then_inc(self.s_mm.h, 1)
        p.emit("tensor", ft)
        inc_last(self.s_ev)
        def fe(e, slot=slot, dst=dst_ap, mmv=mmv, func=func, scale=scale):
            e.wait_ge(self.s_mm.h, mmv)
            if func is not None:
                kw = {}
                if scale is not None:
                    kw["scale"] = scale
                e.activation(out=dst, in_=self.pool[:, slot, 0:128], func=func,
                             **kw).then_inc(self.s_ev.h, 1)
            else:
                e.copy(out=dst, in_=self.pool[:, slot, 0:128]).then_inc(self.s_ev.h, 1)
        p.emit(evac_eng, fe)
        self.i += 1

    def drain(self, engines):
        for e in engines:
            wait(self.p, e, self.s_ev)


def transpose_mat(p, tp, src_tile_fn, dst_tile_fn, R, C, ident, evac_eng="scalar",
                  func=None, scale_fn=None):
    for rc in range(R // 128):
        for cc in range(C // 128):
            sc = scale_fn(rc, cc) if scale_fn else None
            tp.emit(src_tile_fn(rc, cc), dst_tile_fn(rc, cc), ident,
                    evac_eng=evac_eng, func=func, scale=sc)


def proj(p, name, K, M, T, w_dram, x_ap, epilog, epilog_eng="scalar",
         nbanks=4, tile_t=512, wq="gpsimd", nwbuf=2, w_pre=None, wbuf_ap=None):
    Kt, Mt = K // 128, M // 128
    ntc = (T + tile_t - 1) // tile_t
    ps = p.main_pool()
    if p.proj_prev is not None:
        wait(p, "tensor", p.proj_prev)
        wait(p, wq, p.proj_prev)
    s_mm = p.sem(f"{name}_mm")
    s_free = p.sem(f"{name}_free")
    s_w = [p.sem(f"{name}_w{i}") for i in range(nwbuf)]
    wb = wbuf_ap if wbuf_ap is not None else p.wbuf()
    assert wbuf_ap is not None or Kt <= 45
    for m in range(Mt):
        buf = m % nwbuf
        if m >= nwbuf:
            need = (m - nwbuf + 1) * ntc
            p.emit(wq, lambda e, t=need, s=s_mm.h: e.wait_ge(s, t))
        src = w_dram[:, m * 128:(m + 1) * 128].rearrange("(kt q) m -> q kt m", q=128)
        dma(p, wq, wb[:, buf, 0:Kt, :], src, s_w[buf])
    jobs = 0
    for m in range(Mt):
        buf = m % nwbuf
        wwait = 16 * (m // nwbuf + 1)
        for tc in range(ntc):
            tw = min(tile_t, T - tc * tile_t)
            bank = jobs % nbanks
            freewait = jobs - nbanks + 1
            mmv = inc_last(s_mm)
            def fmm(e, tw=tw, bank=bank, buf=buf, wwait=wwait, freewait=freewait,
                    tc=tc):
                e.wait_ge(s_w[buf].h, wwait)
                if freewait > 0:
                    e.wait_ge(s_free.h, freewait)
                inst = None
                for k in range(Kt):
                    inst = e.matmul(ps[:, bank, 0:tw], wb[:, buf, k, :],
                                    x_ap(k)[:, tc * tile_t:tc * tile_t + tw],
                                    start=(k == 0), stop=(k == Kt - 1))
                inst.then_inc(s_mm.h, 1)
            p.emit("tensor", fmm)
            inc_last(s_free)
            def fep(e, m=m, tc=tc, tw=tw, bank=bank, mmv=mmv):
                e.wait_ge(s_mm.h, mmv)
                last = epilog(e, m, tc, ps[:, bank, 0:tw])
                last.then_inc(s_free.h, 1)
            p.emit(epilog_eng, fep)
            jobs += 1
    p.proj_prev = s_free
    return s_free


def barrier(p, engines):
    ss = {e: p.sem() for e in engines}
    for e in engines:
        ss[e].n += 1
        p.emit(e, lambda en, s=ss[e].h: en.sem_inc(s, 1))
    for e in engines:
        for e2 in engines:
            if e2 != e:
                wait(p, e, ss[e2])


def ln_tile(p, x_ap, xln_ap, mv_ap, bn_ap, rstd_ap, nf, eps=1e-5, out2=None):
    def fv(e):
        nch = (nf + 511) // 512
        for c in range(nch):
            w = min(512, nf - c * 512)
            e.bn_stats(out=bn_ap[:, c * 6:(c + 1) * 6], in_=x_ap[:, c * 512:c * 512 + w])
        e.bn_aggr(out=mv_ap, in_=bn_ap[:, 0:6 * nch])
    p.emit("vector", fv)
    def fs(e):
        e.activation(out=rstd_ap, in_=mv_ap[:, 1:2], func=AF.Ln, bias=float(eps))
        e.activation(out=rstd_ap, in_=rstd_ap, func=AF.Exp, scale=-0.5)
    p.emit("scalar", fs)
    def fv2(e):
        e.tensor_scalar(out=xln_ap, in0=x_ap, scalar1=mv_ap[:, 0:1], scalar2=rstd_ap,
                        op0=OP.subtract, op1=OP.mult)
        if out2 is not None:
            e.tensor_scalar(out=out2, in0=x_ap, scalar1=mv_ap[:, 0:1], scalar2=rstd_ap,
                            op0=OP.subtract, op1=OP.mult)
    p.emit("vector", fv2)



def eng_fence(p, eng, sem, scr):
    inc_last(sem)
    if eng == "vector":
        p.emit(eng, lambda e, s=sem.h: e.memset(scr, 0.0).then_inc(s, 1))
    elif eng == "scalar":
        p.emit(eng, lambda e, s=sem.h: e.activation(out=scr, in_=scr, func=AF.Copy).then_inc(s, 1))
    elif eng == "gpsimd":
        p.emit(eng, lambda e, s=sem.h: e.memset(scr, 0.0).then_inc(s, 1))
    else:
        raise ValueError(eng)


def emit_ln8(p, src_ap_fn, stat, s_ph, s_sc, out_cb, eps=1e-5, nf=2048, ntiles=8,
             sqscr=None):
    inv = 1.0 / nf
    cols = {}
    for i in range(ntiles):
        c0 = 160 + (i % 8) * 6
        cols[i] = tuple(stat[:, c0 + j:c0 + j + 1] for j in range(6))
    def fs(e):
        def spacer():
            if ntiles < 4:
                e.activation(out=sqscr[:, 0:512], in_=sqscr[:, 0:512], func=AF.Square)
        for i in range(ntiles):
            sm, mean, ssq, m2, em, rstd = cols[i]
            e.activation(out=sqscr, in_=src_ap_fn(i), func=AF.Copy, accum_out=sm)
            e.activation(out=sqscr, in_=src_ap_fn(i), func=AF.Square, accum_out=ssq)
        spacer()
        for i in range(ntiles):
            sm, mean, ssq, m2, em, rstd = cols[i]
            e.activation(out=mean, in_=sm, func=AF.Copy, scale=inv)
        spacer()
        for i in range(ntiles):
            sm, mean, ssq, m2, em, rstd = cols[i]
            e.activation(out=m2, in_=mean, func=AF.Square)
        spacer()
        for i in range(ntiles):
            sm, mean, ssq, m2, em, rstd = cols[i]
            e.activation(out=em, in_=m2, func=AF.Copy, scale=-1.0, bias=float(eps))
        spacer()
        for i in range(ntiles):
            sm, mean, ssq, m2, em, rstd = cols[i]
            e.activation(out=rstd, in_=ssq, func=AF.Ln, scale=inv, bias=em)
        spacer()
        inst = None
        for i in range(ntiles):
            sm, mean, ssq, m2, em, rstd = cols[i]
            inst = e.activation(out=rstd, in_=rstd, func=AF.Exp, scale=-0.5)
        inst.then_inc(s_sc.h, 1)
    p.emit("scalar", fs)
    scv = inc_last(s_sc)
    for i in range(ntiles):
        sm, mean, ssq, m2, em, rstd = cols[i]
        def fv3(e, i=i, mean=mean, rstd=rstd, scv=scv):
            e.wait_ge(s_sc.h, scv)
            out_cb(e, i, mean, rstd)
        p.emit("vector", fv3)


def emit_ln_transpose(p, tp, ntiles, src_fn, scr, dstfm, idb, stat, s_ph, s_sc,
                      s_ln, eps=1e-5, nf=2048, sqscr=None):
    nct = nf // 128
    slot_state = {"done": {}}
    def cb(e, i, mean, rstd):
        prev_mm = slot_state["done"].get(i - 2, 0)
        if prev_mm > 0:
            e.wait_ge(tp.s_mm.h, prev_mm)
        e.tensor_scalar(out=scr[:, i % 2, :], in0=src_fn(i)[:, :], scalar1=mean,
                        scalar2=rstd, op0=OP.subtract, op1=OP.mult).then_inc(s_ln.h, 1)
    emit_ln8(p, src_fn, stat, s_ph, s_sc, cb, eps, nf=nf, ntiles=ntiles, sqscr=sqscr)
    for i in range(ntiles):
        inc_last(s_ln)
        wait(p, "tensor", s_ln, s_ln.n)
        for ct in range(nct):
            tp.emit(scr[:, i % 2, ct * 128:(ct + 1) * 128],
                    dstfm[:, ct, i * 128:(i + 1) * 128], idb[:, :], evac_eng="scalar")
        slot_state["done"][i] = tp.s_mm.n



def build_l0tail():
    p = Prog("l0tail")
    d = {}
    for nm, shp, dt in [("x_in", [TPC, DM], F32), ("yf_in", [TPC, 4096], BF16),
                        ("ssq_in", [TPC, 4], F32), ("wout", [4096, DM], F32),
                        ("normw", [128, 32], F32), ("gwt", [DM, 5632], F32),
                        ("uwt", [DM, 5632], F32), ("dwt", [5632, DM], F32),
                        ("gs", [128, 44], F32), ("us", [128, 44], F32),
                        ("ds", [128, 16], F32), ("identb", [128, 128], BF16),
                        ("cst", [128, 8], F32)]:
        d[nm] = p.dram(nm, shp, dt, "ExternalInput").ap()
    x3_o = p.dram("x3_o", [TPC, DM], F32, "ExternalOutput").ap()
    x3lnf_o = p.dram("x3lnf_o", [DM, TPC], BF16, "ExternalOutput").ap()
    x2_spill = p.dram("x2_spill", [TPC, DM], F32)

    AR = p.sbuf("AR", [128, 81920], BF16)
    def Ab(off, n):
        return AR[:, off:off + n]
    def Af(off, n):
        return AR[:, off:off + 2 * n].bitcast(F32)

    idb = p.sbuf("idb", [128, 128], BF16)
    small = p.sbuf("small", [128, 160], F32)
    NW, GS, US, DS = 0, 32, 76, 120
    stat = p.sbuf("stat", [128, 256], F32)
    sqscr = p.sbuf("sqscr", [128, 2048], F32)

    s_ld = p.sem(); s_ph = p.sem(); s_sc = p.sem(); s_v = p.sem(); s_out = p.sem()

    cst = p.sbuf("cst_sb", [128, 8], F32)
    dma(p, "sync", cst[:, :], d["cst"], s_ld)
    dma(p, "sync", idb[:, :], d["identb"], s_ld)
    dma(p, "sync", small[:, NW:NW + 32], d["normw"], s_ld)
    dma(p, "sync", small[:, GS:GS + 44], d["gs"], s_ld)
    dma(p, "sync", small[:, US:US + 44], d["us"], s_ld)
    dma(p, "sync", small[:, DS:DS + 16], d["ds"], s_ld)
    yf = Ab(0, 32768).rearrange("q (t f) -> q t f", t=8)
    dma(p, "sync", yf, d["yf_in"].rearrange("(t q) f -> q t f", q=128), s_ld)
    dma(p, "sync", stat[:, 0:32].rearrange("q (t c) -> q t c", t=8),
        d["ssq_in"].rearrange("(t q) c -> q t c", q=128), s_ld)

    wait(p, "vector", s_ld)
    wait(p, "scalar", s_ld)
    def f_red(e):
        inst = None
        for i in range(8):
            inst = e.reduce_sum(out=stat[:, 40 + i:41 + i], in_=stat[:, 4 * i:4 * i + 4],
                                axis=mybir.AxisListType.X)
        inst.then_inc(s_ph.h, 1)
    p.emit("vector", f_red)
    ph1 = inc_last(s_ph)
    def f_rr(e):
        e.wait_ge(s_ph.h, ph1)
        e.activation(out=stat[:, 56:64], in_=stat[:, 40:48], func=AF.Ln,
                     scale=1.0 / 4096.0, bias=cst[:, 0:1])
        e.activation(out=stat[:, 56:64], in_=stat[:, 56:64], func=AF.Exp,
                     scale=-0.5).then_inc(s_sc.h, 1)
    p.emit("scalar", f_rr)
    sc1 = inc_last(s_sc)
    def f_yhat(e):
        e.wait_ge(s_sc.h, sc1)
        for i in range(8):
            inst = e.tensor_scalar(out=yf[:, i, :], in0=yf[:, i, :],
                                   scalar1=stat[:, 56 + i:57 + i], scalar2=None,
                                   op0=OP.mult)
        inst.then_inc(s_v.h, 1)
    p.emit("vector", f_yhat)
    v1 = inc_last(s_v)

    yfm = Ab(32768, 32768).rearrange("q (f t) -> q f t", f=32)
    tp = p.get_tp()
    wait(p, "tensor", s_v, v1)
    wait(p, "scalar", s_ld)
    for rt in range(8):
        for ct in range(32):
            tp.emit(yf[:, rt, ct * 128:(ct + 1) * 128],
                    yfm[:, ct, rt * 128:(rt + 1) * 128], idb[:, :],
                    evac_eng="scalar", func=AF.Copy,
                    scale=small[:, NW + ct:NW + ct + 1])
    tp.drain(["tensor"])

    o1 = Ab(0, 16384).rearrange("q (f t) -> q f t", f=16)
    def ep_o1(e, m, tc, ps_ap):
        return e.copy(out=o1[:, m, tc * 512:tc * 512 + ps_ap.shape[-1]], in_=ps_ap)
    pr1 = proj(p, "wout", 4096, DM, TPC, d["wout"], lambda k: yfm[:, k, :], ep_o1)

    o1tm = Ab(16384, 16384).rearrange("q (t f) -> q t f", t=8)
    tp2 = p.get_tp()
    wait(p, "tensor", pr1)
    for rt in range(16):
        for ct in range(8):
            tp2.emit(o1[:, rt, ct * 128:(ct + 1) * 128],
                     o1tm[:, ct, rt * 128:(rt + 1) * 128], idb[:, :], evac_eng="scalar")
    xb = Af(32768, 16384).rearrange("q (t f) -> q t f", t=8)
    wait(p, "sync", pr1)
    dma(p, "sync", xb, d["x_in"].rearrange("(t q) f -> q t f", q=128), s_ld)
    wait(p, "vector", s_ld)
    tp2.drain(["vector"])

    lnscr = Ab(0, 4096).rearrange("q (a f) -> q a f", a=2)
    x2lnfm = Ab(4096, 16384).rearrange("q (f t) -> q f t", f=16)
    tp3 = p.get_tp()
    s_res = p.sem()
    def cb_res(e, i, mean, rstd):
        e.tensor_scalar(out=lnscr[:, i % 2, :], in0=o1tm[:, i, :], scalar1=mean,
                        scalar2=rstd, op0=OP.subtract, op1=OP.mult)
        e.tensor_tensor(out=xb[:, i, :], in0=xb[:, i, :], in1=lnscr[:, i % 2, :],
                        op=OP.add).then_inc(s_res.h, 1)
        inc_last(s_res, 0)
    emit_ln8(p, lambda i: o1tm[:, i, :], stat, s_ph, s_sc, cb_res, 1e-5, sqscr=sqscr[:, :])
    s_res.n += 8
    s_ln2 = p.sem()
    wait(p, "scalar", s_res)
    emit_ln_transpose(p, tp3, 8, lambda i: xb[:, i, :], lnscr, x2lnfm, idb,
                      stat, s_ph, s_sc, s_ln2, 1e-5, sqscr=sqscr[:, :])
    wait(p, "gpsimd", s_res)
    s_spill = p.sem()
    dma(p, "gpsimd", x2_spill.ap().rearrange("(t q) f -> q t f", q=128), xb, s_spill)
    tp3.drain(["tensor"])

    hdn = Ab(20480, 45056).rearrange("q (f t) -> q f t", f=44)
    gscr = p.sbuf("gscr", [128, 2, 512], BF16)
    wait(p, "scalar", s_spill)
    wait(p, "vector", s_spill)
    def ffs(e, m, tc, ps1):
        return e.activation(out=gscr[:, (2 * m + tc) % 2, 0:ps1.shape[-1]], in_=ps1,
                            func=AF.Silu, scale=small[:, GS + m:GS + m + 1])
    def ffv(e, m, tc, ps1, ps2):
        tw = ps2.shape[-1]
        e.tensor_scalar(out=hdn[:, m, tc * 512:tc * 512 + tw], in0=ps2,
                        scalar1=small[:, US + m:US + m + 1], scalar2=None, op0=OP.mult)
        return e.tensor_tensor(out=hdn[:, m, tc * 512:tc * 512 + tw],
                               in0=hdn[:, m, tc * 512:tc * 512 + tw],
                               in1=gscr[:, (2 * m + tc) % 2, 0:tw], op=OP.mult)
    proj2(p, "gu", DM, 5632, TPC, d["gwt"], d["uwt"], lambda k: x2lnfm[:, k, :],
          (ffs, ffv))

    o2 = Ab(65536, 16384).rearrange("q (f t) -> q f t", f=16)
    def ep_o2(e, m, tc, ps_ap):
        return e.activation(out=o2[:, m, tc * 512:tc * 512 + ps_ap.shape[-1]],
                            in_=ps_ap, func=AF.Copy, scale=small[:, DS + m:DS + m + 1])
    pr3 = proj(p, "down", 5632, DM, TPC, d["dwt"], lambda k: hdn[:, k, :], ep_o2)

    o2tm = Ab(0, 16384).rearrange("q (t f) -> q t f", t=8)
    tp4 = p.get_tp()
    wait(p, "tensor", pr3)
    for rt in range(16):
        for ct in range(8):
            tp4.emit(o2[:, rt, ct * 128:(ct + 1) * 128],
                     o2tm[:, ct, rt * 128:(rt + 1) * 128], idb[:, :], evac_eng="scalar")
    xb2 = Af(20480, 16384).rearrange("q (t f) -> q t f", t=8)
    wait(p, "sync", pr3)
    wait(p, "sync", s_spill)
    dma(p, "sync", xb2, x2_spill.ap().rearrange("(t q) f -> q t f", q=128), s_ld)
    wait(p, "vector", s_ld)
    tp4.drain(["vector"])
    lnscr2 = Ab(16384, 4096).rearrange("q (a f) -> q a f", a=2)
    x3lnfm = Ab(53248, 16384).rearrange("q (f t) -> q f t", f=16)
    tp5 = p.get_tp()
    s_res2 = p.sem(); s_ln3 = p.sem()
    def cb_res2(e, i, mean, rstd):
        e.tensor_scalar(out=lnscr2[:, i % 2, :], in0=o2tm[:, i, :], scalar1=mean,
                        scalar2=rstd, op0=OP.subtract, op1=OP.mult)
        e.tensor_tensor(out=xb2[:, i, :], in0=xb2[:, i, :], in1=lnscr2[:, i % 2, :],
                        op=OP.add).then_inc(s_res2.h, 1)
    emit_ln8(p, lambda i: o2tm[:, i, :], stat, s_ph, s_sc, cb_res2, 1e-5, sqscr=sqscr[:, :])
    s_res2.n += 8
    wait(p, "scalar", s_res2)
    for i in range(8):
        def fdma(e, i=i):
            e.wait_ge(s_res2.h, i + 1)
            e.dma_start(out=x3_o.rearrange("(t q) f -> q t f", q=128)[:, i, :],
                        in_=xb2[:, i, :]).then_inc(s_out.h, 16)
        p.emit("sync", fdma)
        s_out.n += 16
    emit_ln_transpose(p, tp5, 8, lambda i: xb2[:, i, :], lnscr2, x3lnfm, idb,
                      stat, s_ph, s_sc, s_ln3, 1e-5, sqscr=sqscr[:, :])
    tp5.drain(["sync"])
    dma(p, "sync", x3lnf_o.rearrange("(ft q) t -> q ft t", q=128),
        x3lnfm[:, :, :], s_out)
    wait(p, "sync", s_out)
    return p.finish()


CST = np.tile(np.array([1e-5, 1e-6, 1.0, 0.0, 0.5, 2.0, 0.0, 0.0],
                        dtype=np.float32), (128, 1))


def host_l0tail_inmaps(inputs, yf_full, ssq_full, x_base):
    identb = np.eye(128, dtype=BF)
    wout = np.ascontiguousarray(inputs["L0_attn_W_out"], dtype=np.float32)
    normw = np.ascontiguousarray(inputs["L0_attn_norm_w"].reshape(32, 128).T)
    gwt = np.ascontiguousarray(inputs["L0_ffn_gate_w"].T)
    uwt = np.ascontiguousarray(inputs["L0_ffn_up_w"].T)
    dwt = np.ascontiguousarray(inputs["L0_ffn_down_w"].T)
    gs = np.ascontiguousarray(inputs["L0_ffn_gate_s"].reshape(44, 128).T)
    us = np.ascontiguousarray(inputs["L0_ffn_up_s"].reshape(44, 128).T)
    ds = np.ascontiguousarray(inputs["L0_ffn_down_s"].reshape(16, 128).T)
    yfr = yf_full.reshape(NCORE, TPC, 4096)
    sqr = ssq_full.reshape(NCORE, TPC, 4)
    xr = x_base.reshape(NCORE, TPC, DM)
    maps = []
    for c in range(NCORE):
        maps.append(dict(x_in=np.ascontiguousarray(xr[c]),
                         yf_in=np.ascontiguousarray(yfr[c]),
                         ssq_in=np.ascontiguousarray(sqr[c]),
                         wout=wout, normw=normw, gwt=gwt, uwt=uwt, dwt=dwt,
                         gs=gs, us=us, ds=ds, identb=identb, cst=CST))
    return maps


def proj2(p, name, K, M, T, w1_dram, w2_dram, x_ap, epilog,
          tile_t=512, wq="gpsimd", nwbuf=2):
    Kt, Mt = K // 128, M // 128
    ntc = (T + tile_t - 1) // tile_t
    ps = p.main_pool()
    if p.proj_prev is not None:
        wait(p, "tensor", p.proj_prev)
        wait(p, wq, p.proj_prev)
    s_mm = p.sem(f"{name}_mm"); s_free = p.sem(f"{name}_free"); s_scd = p.sem(f"{name}_scd")
    s_w = [p.sem(f"{name}_w{i}") for i in range(nwbuf)]
    wb = p.wbuf()
    assert 2 * Kt <= 45
    for m in range(Mt):
        buf = m % nwbuf
        if m >= nwbuf:
            need = (m - nwbuf + 1) * ntc
            p.emit(wq, lambda e, t=need, s=s_mm.h: e.wait_ge(s, t))
        for wi, wd in enumerate((w1_dram, w2_dram)):
            src = wd[:, m * 128:(m + 1) * 128].rearrange("(kt q) m -> q kt m", q=128)
            dma(p, wq, wb[:, buf, wi * Kt:(wi + 1) * Kt, :], src, s_w[buf])
    jobs = 0
    fs_cb, fv_cb = epilog
    for m in range(Mt):
        buf = m % nwbuf
        wwait = 32 * (m // nwbuf + 1)
        for tc in range(ntc):
            tw = min(tile_t, T - tc * tile_t)
            pair = jobs % 2
            freewait = jobs - 2 + 1
            mmv = inc_last(s_mm)
            def fmm(e, tw=tw, pair=pair, buf=buf, wwait=wwait, freewait=freewait, tc=tc):
                e.wait_ge(s_w[buf].h, wwait)
                if freewait > 0:
                    e.wait_ge(s_free.h, freewait)
                inst = None
                for wi in range(2):
                    for k in range(Kt):
                        inst = e.matmul(ps[:, 2 * pair + wi, 0:tw], wb[:, buf, wi * Kt + k, :],
                                        x_ap(k)[:, tc * tile_t:tc * tile_t + tw],
                                        start=(k == 0), stop=(k == Kt - 1))
                inst.then_inc(s_mm.h, 1)
            p.emit("tensor", fmm)
            scv = inc_last(s_scd)
            gfree = jobs - 1
            def ffs(e, m=m, tc=tc, tw=tw, pair=pair, mmv=mmv, gfree=gfree):
                e.wait_ge(s_mm.h, mmv)
                if gfree > 0:
                    e.wait_ge(s_free.h, gfree)
                last = fs_cb(e, m, tc, ps[:, 2 * pair, 0:tw])
                last.then_inc(s_scd.h, 1)
            p.emit("scalar", ffs)
            inc_last(s_free)
            def ffv(e, m=m, tc=tc, tw=tw, pair=pair, scv=scv):
                e.wait_ge(s_scd.h, scv)
                last = fv_cb(e, m, tc, ps[:, 2 * pair, 0:tw], ps[:, 2 * pair + 1, 0:tw])
                last.then_inc(s_free.h, 1)
            p.emit("vector", ffv)
            jobs += 1
    p.proj_prev = s_free
    return s_free


def build_l1tail():
    p = Prog("l1tail")
    d = {}
    for nm, shp, dt in [("x3_in", [TPC, DM], F32), ("ytil_in", [4096, TPC], BF16),
                        ("wout1", [4096, DM], F32), ("w1t", [DM, 8192], F32),
                        ("w3t", [DM, 8192], F32), ("w2t", [8192, DM], F32),
                        ("identb", [128, 128], BF16), ("h128", [128, 128], BF16)]:
        d[nm] = p.dram(nm, shp, dt, "ExternalInput").ap()
    xout = p.dram("xout", [TPC, DM], F32, "ExternalOutput").ap()
    x4_spill = p.dram("x4_spill", [TPC, DM], F32)

    AR = p.sbuf("AR", [128, 81920], BF16)
    def Ab(off, n):
        return AR[:, off:off + n]
    def Af(off, n):
        return AR[:, off:off + 2 * n].bitcast(F32)
    idb = p.sbuf("idb", [128, 128], BF16)
    h128 = p.sbuf("h128s", [128, 128], BF16)
    stat = p.sbuf("stat", [128, 256], F32)
    sqscr = p.sbuf("sqscr", [128, 2048], F32)
    spare = p.sbuf("spare", [128, 512], BF16)
    gscr = p.sbuf("gscr", [128, 2, 512], BF16)

    s_ld = p.sem(); s_ph = p.sem(); s_sc = p.sem(); s_out = p.sem()

    dma(p, "sync", idb[:, :], d["identb"], s_ld)
    dma(p, "sync", h128[:, :], d["h128"], s_ld)
    ytil = Ab(0, 32768).rearrange("q (f t) -> q f t", f=32)
    dma(p, "sync", ytil, d["ytil_in"].rearrange("(f q) t -> q f t", q=128), s_ld)
    wait(p, "tensor", s_ld)

    o1 = Ab(49152, 16384).rearrange("q (f t) -> q f t", f=16)
    def ep_o1(e, m, tc, ps_ap):
        return e.copy(out=o1[:, m, tc * 512:tc * 512 + ps_ap.shape[-1]], in_=ps_ap)
    pr1 = proj(p, "wo1", 4096, DM, TPC, d["wout1"], lambda k: ytil[:, k, :], ep_o1)

    o1tm = Ab(65536, 16384).rearrange("q (t f) -> q t f", t=8)
    tp = p.get_tp()
    wait(p, "tensor", pr1)
    for rt in range(16):
        for ct in range(8):
            tp.emit(o1[:, rt, ct * 128:(ct + 1) * 128],
                    o1tm[:, ct, rt * 128:(rt + 1) * 128], idb[:, :], evac_eng="scalar")
    xb = Af(0, 16384).rearrange("q (t f) -> q t f", t=8)
    wait(p, "sync", pr1)
    dma(p, "sync", xb, d["x3_in"].rearrange("(t q) f -> q t f", q=128), s_ld)
    wait(p, "vector", s_ld)
    wait(p, "scalar", s_ld)
    tp.drain(["vector", "scalar"])
    lnscr = Ab(49152, 4096).rearrange("q (a f) -> q a f", a=2)
    x4lnfm = Ab(32768, 16384).rearrange("q (f t) -> q f t", f=16)
    s_res = p.sem()
    def cb_res(e, i, mean, rstd):
        e.tensor_scalar(out=lnscr[:, i % 2, :], in0=o1tm[:, i, :], scalar1=mean,
                        scalar2=rstd, op0=OP.subtract, op1=OP.mult)
        e.tensor_tensor(out=xb[:, i, :], in0=xb[:, i, :], in1=lnscr[:, i % 2, :],
                        op=OP.add).then_inc(s_res.h, 1)
    emit_ln8(p, lambda i: o1tm[:, i, :], stat, s_ph, s_sc, cb_res, 1e-5, sqscr=sqscr[:, :])
    s_res.n += 8
    wait(p, "scalar", s_res)
    s_ln2 = p.sem()
    emit_ln_transpose(p, tp, 8, lambda i: xb[:, i, :], lnscr, x4lnfm, idb,
                      stat, s_ph, s_sc, s_ln2, 1e-5, sqscr=sqscr[:, :])
    wait(p, "gpsimd", s_res)
    s_spill = p.sem()
    dma(p, "gpsimd", x4_spill.ap().rearrange("(t q) f -> q t f", q=128), xb, s_spill)
    tp.drain(["tensor"])

    o2 = Ab(49152, 16384).rearrange("q (f t) -> q f t", f=16)
    wb2 = Ab(65536, 16384).rearrange("q (b k m) -> q b k m", b=2, k=64)
    s_h128 = p.sem("s_h128")
    s_bfly = p.sem("s_bfly")
    wait(p, "scalar", s_spill)
    wait(p, "vector", s_spill)
    for half in range(2):
        t0c = half * 512
        hdn = Ab(0, 32768).rearrange("q (f t) -> q f t", f=64)
        def ffs(e, m, tc, ps1):
            return e.activation(out=gscr[:, (2 * m + tc) % 2, 0:ps1.shape[-1]],
                                in_=ps1, func=AF.Silu)
        def ffv(e, m, tc, ps1, ps2, hdn=hdn):
            tw = ps2.shape[-1]
            return e.tensor_tensor(out=hdn[:, m, 0:tw], in0=ps2,
                                   in1=gscr[:, (2 * m + tc) % 2, 0:tw], op=OP.mult)
        proj2(p, f"gu{half}", DM, 8192, 512, d["w1t"][:, :], d["w3t"][:, :],
              lambda k, t0c=t0c: x4lnfm[:, k, t0c:t0c + 512], (ffs, ffv), tile_t=512)
        ps = p.main_pool()
        wait(p, "tensor", p.proj_prev)
        s_mm1 = p.sem(f"h128mm{half}"); s_ev1 = p.sem(f"h128ev{half}")
        for a in range(64):
            bank = a % 4
            mmv = inc_last(s_mm1)
            need = a - 4 + 1
            def fmm(e, a=a, bank=bank, need=need, s_ev1=s_ev1, s_mm1=s_mm1, hdn=hdn):
                if need > 0:
                    e.wait_ge(s_ev1.h, need)
                e.matmul(ps[:, bank, 0:512], h128[:, :], hdn[:, a, :],
                         start=True, stop=True).then_inc(s_mm1.h, 1)
            p.emit("tensor", fmm)
            inc_last(s_ev1)
            def fev(e, a=a, bank=bank, mmv=mmv, s_ev1=s_ev1, s_mm1=s_mm1, hdn=hdn):
                e.wait_ge(s_mm1.h, mmv)
                e.activation(out=hdn[:, a, :], in_=ps[:, bank, 0:512], func=AF.Copy,
                             scale=1.4142135623730951).then_inc(s_ev1.h, 1)
            p.emit("scalar", fev)
        p.proj_prev = s_ev1
        pos = list(range(64))
        spare_slot = 64
        def slot_ap(s, hdn=hdn):
            if s == 64:
                return spare[:, :]
            return hdn[:, s, :]
        wait(p, "vector", s_ev1)
        nbf = 0
        for st in range(6):
            h = 1 << st
            for base in range(0, 64, 2 * h):
                for off in range(h):
                    i, j = base + off, base + off + h
                    A, Bs = pos[i], pos[j]
                    e_diff = spare_slot
                    def fb(e, A=A, Bs=Bs, e_diff=e_diff, hdn=hdn):
                        e.tensor_tensor(out=slot_ap(e_diff), in0=slot_ap(A),
                                        in1=slot_ap(Bs), op=OP.subtract)
                        e.tensor_tensor(out=slot_ap(A), in0=slot_ap(A),
                                        in1=slot_ap(Bs), op=OP.add)
                    p.emit("vector", fb)
                    nbf += 1
                    pos[j] = e_diff
                    spare_slot = Bs
        def fbl(e):
            e.memset(spare[:, 0:1], 0.0)
        inc_last(s_bfly)
        def fbl2(e):
            e.memset(stat[:, 255:256], 0.0)
            e.tensor_scalar(out=stat[:, 255:256], in0=stat[:, 255:256], scalar1=1.0,
                            scalar2=None, op0=OP.mult).then_inc(s_bfly.h, 1)
        p.emit("vector", fbl2)
        def ep_o2(e, m, tc, ps_ap, t0c=t0c):
            return e.copy(out=o2[:, m, t0c:t0c + ps_ap.shape[-1]], in_=ps_ap)
        wait(p, "tensor", s_bfly)
        pos_fixed = list(pos)
        proj(p, f"w2_{half}", 8192, DM, 512, d["w2t"],
             lambda k, pf=pos_fixed: slot_ap(pf[k]), ep_o2, wbuf_ap=wb2)

    o2tm = Ab(0, 16384).rearrange("q (t f) -> q t f", t=8)
    wait(p, "tensor", p.proj_prev)
    for rt in range(16):
        for ct in range(8):
            tp.emit(o2[:, rt, ct * 128:(ct + 1) * 128],
                    o2tm[:, ct, rt * 128:(rt + 1) * 128], idb[:, :], evac_eng="scalar")
    xb2 = Af(16384, 16384).rearrange("q (t f) -> q t f", t=8)
    wait(p, "sync", p.proj_prev)
    wait(p, "sync", s_spill)
    dma(p, "sync", xb2, x4_spill.ap().rearrange("(t q) f -> q t f", q=128), s_ld)
    wait(p, "vector", s_ld)
    wait(p, "scalar", s_ld)
    tp.drain(["vector", "scalar"])
    lnscr2 = Ab(49152, 4096).rearrange("q (a f) -> q a f", a=2)
    s_res2 = p.sem()
    def cb_res2(e, i, mean, rstd):
        e.tensor_scalar(out=lnscr2[:, i % 2, :], in0=o2tm[:, i, :], scalar1=mean,
                        scalar2=rstd, op0=OP.subtract, op1=OP.mult)
        e.tensor_tensor(out=xb2[:, i, :], in0=xb2[:, i, :], in1=lnscr2[:, i % 2, :],
                        op=OP.add).then_inc(s_res2.h, 1)
    emit_ln8(p, lambda i: o2tm[:, i, :], stat, s_ph, s_sc, cb_res2, 1e-5, sqscr=sqscr[:, :])
    s_res2.n += 8
    for i in range(8):
        def fdma(e, i=i):
            e.wait_ge(s_res2.h, i + 1)
            e.dma_start(out=xout.rearrange("(t q) f -> q t f", q=128)[:, i, :],
                        in_=xb2[:, i, :]).then_inc(s_out.h, 16)
        p.emit("sync", fdma)
        s_out.n += 16
    wait(p, "sync", s_out)
    return p.finish()


def hadamard(n):
    h = np.array([[1.0]], dtype=np.float32)
    while h.shape[0] < n:
        h = np.block([[h, h], [h, -h]])
    return h


def host_l1tail_inmaps(inputs, ytil_full, x3_full):
    identb = np.eye(128, dtype=BF)
    h128 = (hadamard(128) * (2.0 ** -7)).astype(BF)
    wout1 = np.ascontiguousarray(inputs["L1_attn_out_proj_w"], dtype=np.float32)
    w1t = np.ascontiguousarray(inputs["L1_ffn_W1"].T)
    w3t = np.ascontiguousarray(inputs["L1_ffn_W3"].T)
    w2t = np.ascontiguousarray(inputs["L1_ffn_W2"].T)
    maps = []
    for c in range(NCORE):
        maps.append(dict(
            x3_in=np.ascontiguousarray(x3_full[c * TPC:(c + 1) * TPC]),
            ytil_in=np.ascontiguousarray(ytil_full[:, c * TPC:(c + 1) * TPC]),
            wout1=wout1, w1t=w1t, w3t=w3t, w2t=w2t, identb=identb, h128=h128))
    return maps


def build_l1pre():
    p = Prog("l1pre")
    d = {}
    for nm, shp, dt in [("xlnf", [DM, 8192], BF16), ("wh", [DM, 512], F32),
                        ("cw", [128, 4, 4], F32), ("cb", [128, 4], F32),
                        ("wxp", [512, 160], F32)]:
        d[nm] = p.dram(nm, shp, dt, "ExternalInput").ap()
    hs_o = p.dram("hs_o", [512, 8192], BF16, "ExternalOutput").ap()
    part_o = p.dram("part_o", [8192, 160], F32, "ExternalOutput").ap()

    s_ld = p.sem(); s_out = p.sem(); s_conv = p.sem(); s_hs = p.sem()
    cwb = p.sbuf("cwb", [128, 4, 4], F32)
    cbb = p.sbuf("cbb", [128, 4], F32)
    wxpb = p.sbuf("wxpb", [128, 4, 160], BF16)
    s_ldg = p.sem()
    dma(p, "sync", cwb[:, :, :], d["cw"], s_ld)
    dma(p, "sync", cbb[:, :], d["cb"], s_ld)
    dma(p, "gpsimd", wxpb[:, :, :],
        d["wxp"].rearrange("(kt q) c -> q kt c", q=128), s_ldg)

    hid = p.sbuf("hid", [128, 4, 2, 4099], BF16)
    hs = p.sbuf("hs", [128, 4, 8192], BF16)
    xch = p.sbuf("xch", [128, 2, 16, 512], BF16)
    cs = p.sbuf("cs", [128, 4096], F32)
    partb = p.sbuf("partb", [128, 16, 160], F32)

    def fz(e):
        for mt in range(4):
            for b in range(2):
                e.memset(hid[:, mt, b, 0:3], 0.0)
        e.memset(cs[:, 0:1], 0.0)
    p.emit("vector", fz)

    ps = p.main_pool()
    s_mm = p.sem("inp_mm"); s_ev = p.sem("inp_ev"); s_x = [p.sem(f"x{i}") for i in range(2)]
    whb = p.sbuf("whb", [128, 16, 4, 128], BF16)
    dma(p, "gpsimd", whb[:, :, :, :],
        d["wh"].rearrange("(kt q) (mt m) -> q kt mt m", q=128, m=128), s_ldg)
    jobs = 0
    for tc in range(16):
        buf = tc % 2
        if tc >= 2:
            need = (tc - 1) * 4
            p.emit("sync", lambda e, t=need, s=s_mm.h: e.wait_ge(s, t))
        dma(p, "sync", xch[:, buf, :, :],
            d["xlnf"][:, tc * 512:(tc + 1) * 512].rearrange("(kt q) t -> q kt t", q=128),
            s_x[buf])
        b, l0 = divmod(tc * 512, 4096)
        for mt in range(4):
            bank = jobs % 4
            mmv = inc_last(s_mm)
            freewait = jobs - 4 + 1
            xwait = 16 * (tc // 2 + 1)
            def fmm(e, mt=mt, bank=bank, buf=buf, xwait=xwait, freewait=freewait,
                    first=(jobs == 0)):
                e.wait_ge(s_x[buf].h, xwait)
                if freewait > 0:
                    e.wait_ge(s_ev.h, freewait)
                if first:
                    e.wait_ge(s_ldg.h, 32)
                inst = None
                for k in range(16):
                    inst = e.matmul(ps[:, bank, 0:512], whb[:, k, mt, :],
                                    xch[:, buf, k, :], start=(k == 0), stop=(k == 15))
                inst.then_inc(s_mm.h, 1)
            p.emit("tensor", fmm)
            inc_last(s_ev)
            def fev(e, mt=mt, bank=bank, mmv=mmv, b=b, l0=l0):
                e.wait_ge(s_mm.h, mmv)
                e.copy(out=hid[:, mt, b, 3 + l0:3 + l0 + 512],
                       in_=ps[:, bank, 0:512]).then_inc(s_ev.h, 1)
            p.emit("scalar", fev)
            jobs += 1
    wait(p, "vector", s_ev)
    wait(p, "vector", s_ld)
    wait(p, "scalar", s_ld)
    for mt in range(4):
        for b in range(2):
            cv = inc_last(s_conv)
            def fconv(e, mt=mt, b=b):
                e.tensor_scalar(out=cs[:, :], in0=hid[:, mt, b, 0:4096],
                                scalar1=cwb[:, mt, 0:1], scalar2=None, op0=OP.mult)
                inst = None
                for w in range(1, 4):
                    inst = e.scalar_tensor_tensor(out=cs[:, :], in0=hid[:, mt, b, w:w + 4096],
                                                  scalar=cwb[:, mt, w:w + 1], in1=cs[:, :],
                                                  op0=OP.mult, op1=OP.add)
                inst.then_inc(s_conv.h, 1)
            p.emit("vector", fconv)
            hv = inc_last(s_hs)
            def fsilu(e, mt=mt, b=b, cv=cv):
                e.wait_ge(s_conv.h, cv)
                e.activation(out=hs[:, mt, b * 4096:(b + 1) * 4096], in_=cs[:, :],
                             func=AF.Silu, bias=cbb[:, mt:mt + 1]).then_inc(s_hs.h, 1)
            p.emit("scalar", fsilu)
            wait(p, "vector", s_hs, hv)
    wait(p, "sync", s_hs)
    dma(p, "sync", hs_o.rearrange("(mt q) t -> q mt t", q=128), hs[:, :, :], s_out)
    s_mm2 = p.sem("xp_mm"); s_ev2 = p.sem("xp_ev"); s_pout = p.sem("xp_out")
    wait(p, "tensor", s_hs)
    wait(p, "tensor", s_ldg, 32)
    for tt in range(64):
        slot = tt % 16
        mmv = inc_last(s_mm2)
        def fmm2(e, tt=tt, slot=slot):
            if tt >= 16:
                e.wait_ge(s_pout.h, 16 * (tt - 15))
            inst = None
            for k in range(4):
                inst = e.matmul(ps[:, tt % 4, 0:160], hs[:, k, tt * 128:(tt + 1) * 128],
                                wxpb[:, k, :], start=(k == 0), stop=(k == 3))
            inst.then_inc(s_mm2.h, 1)
        p.emit("tensor", fmm2)
        inc_last(s_ev2)
        def fev2(e, tt=tt, slot=slot, mmv=mmv):
            e.wait_ge(s_mm2.h, mmv)
            e.copy(out=partb[:, slot, :], in_=ps[:, tt % 4, 0:160]).then_inc(s_ev2.h, 1)
        p.emit("scalar", fev2)
        def fdma(e, tt=tt, slot=slot, ev=s_ev2.n):
            e.wait_ge(s_ev2.h, ev)
            e.dma_start(out=part_o.rearrange("(tt q) c -> q tt c", q=128)[:, tt, :],
                        in_=partb[:, slot, :]).then_inc(s_pout.h, 16)
        p.emit("sync", fdma)
        s_pout.n += 16
    wait(p, "sync", s_pout)
    wait(p, "sync", s_out)
    return p.finish()


def host_l1pre_inmaps(inputs, xlnf_full):
    wfull = inputs["L1_attn_in_proj_w"]
    cwf = inputs["L1_attn_conv_w"][:, 0, :]
    cbf = inputs["L1_attn_conv_b"]
    xpf = inputs["L1_attn_x_proj_w"]
    maps = []
    for ds in range(NCORE):
        r0 = 512 * ds
        maps.append(dict(
            xlnf=xlnf_full,
            wh=np.ascontiguousarray(wfull[:, r0:r0 + 512], dtype=np.float32),
            cw=np.ascontiguousarray(cwf[r0:r0 + 512].reshape(4, 128, 4).transpose(1, 0, 2)),
            cb=np.ascontiguousarray(cbf[r0:r0 + 512].reshape(4, 128).T),
            wxp=np.ascontiguousarray(xpf[r0:r0 + 512], dtype=np.float32)))
    return maps


def build_l1mid():
    p = Prog("l1mid")
    d = {}
    for nm, shp, dt in [("part8", [8, TPC, 160], F32), ("dtw", [128, 4096], F32),
                        ("dtb", [128, 32], F32), ("identb", [128, 128], BF16),
                        ("cst", [128, 8], F32)]:
        d[nm] = p.dram(nm, shp, dt, "ExternalInput").ap()
    dt_o = p.dram("dt_o", [4096, TPC], F32, "ExternalOutput").ap()
    bt_o = p.dram("bt_o", [16, TPC], F32, "ExternalOutput").ap()
    ct_o = p.dram("ct_o", [16, TPC], F32, "ExternalOutput").ap()

    s_ld = p.sem(); s_ldg = p.sem(); s_ph = p.sem(); s_sc = p.sem(); s_out = p.sem()
    pb = p.sbuf("pb", [128, 8, 8, 160], F32)
    ssm = p.sbuf("ssm", [128, 8, 160], F32)
    cst = p.sbuf("cstb", [128, 8], F32)
    idb = p.sbuf("idb", [128, 128], BF16)
    stat = p.sbuf("stat", [128, 256], F32)
    sqscr = p.sbuf("sqscr", [128, 128], F32)
    dtwb = p.sbuf("dtwb", [128, 32, 128], BF16)
    dtbb = p.sbuf("dtbb", [128, 32], F32)
    dtrn = p.sbuf("dtrn", [128, 8, 128], BF16)
    bcn = p.sbuf("bcn", [128, 8, 128], F32)
    dtrfm = p.sbuf("dtrfm", [128, 1024], BF16)
    bct = p.sbuf("bct", [16, 2, 1024], F32)
    dto = p.sbuf("dto", [128, 32, 1024], F32)
    lscr = p.sbuf("lscr", [128, 2, 512], F32)

    for j in range(8):
        dma(p, "sync", pb[:, :, j, :],
            d["part8"][j, :, :].rearrange("(tt q) c -> q tt c", q=128), s_ld)
    dma(p, "sync", cst[:, :], d["cst"], s_ld)
    dma(p, "sync", idb[:, :], d["identb"], s_ld)
    dma(p, "sync", dtbb[:, :], d["dtb"], s_ld)
    dma(p, "gpsimd", dtwb[:, :, :], d["dtw"].rearrange("q (mt m) -> q mt m", m=128), s_ldg)

    wait(p, "vector", s_ld)
    wait(p, "scalar", s_ld)
    s_sum = p.sem("sum")
    def fpad(e):
        e.memset(bcn[:, :, :], 0.0)
    p.emit("vector", fpad)
    def fsum(e):
        inst = None
        for tt in range(8):
            e.tensor_tensor(out=ssm[:, tt, :], in0=pb[:, tt, 0, :], in1=pb[:, tt, 1, :],
                            op=OP.add)
            for j in range(2, 8):
                inst = e.tensor_tensor(out=ssm[:, tt, :], in0=ssm[:, tt, :],
                                       in1=pb[:, tt, j, :], op=OP.add)
        inst.then_inc(s_sum.h, 1)
    p.emit("vector", fsum)
    s_sum.n += 1
    wait(p, "scalar", s_sum)
    s_rms = p.sem("rms")
    def frms(e):
        for tt in range(8):
            e.activation(out=sqscr[:, 0:128], in_=ssm[:, tt, 0:128], func=AF.Square,
                         accum_out=stat[:, tt:tt + 1])
            e.activation(out=sqscr[:, 0:16], in_=ssm[:, tt, 128:144], func=AF.Square,
                         accum_out=stat[:, 8 + tt:9 + tt])
            e.activation(out=sqscr[:, 0:16], in_=ssm[:, tt, 144:160], func=AF.Square,
                         accum_out=stat[:, 16 + tt:17 + tt])
        for g, n in ((0, 128.0), (8, 16.0), (16, 16.0)):
            for tt in range(8):
                e.activation(out=stat[:, 24 + g + tt:25 + g + tt],
                             in_=stat[:, g + tt:g + tt + 1], func=AF.Ln,
                             scale=1.0 / n, bias=cst[:, 1:2])
        inst = None
        for c in range(24):
            inst = e.activation(out=stat[:, 24 + c:25 + c], in_=stat[:, 24 + c:25 + c],
                                func=AF.Exp, scale=-0.5)
        inst.then_inc(s_rms.h, 1)
    p.emit("scalar", frms)
    s_rms.n += 1
    wait(p, "vector", s_rms)
    s_n = p.sem("norm")
    def fnorm(e):
        inst = None
        for tt in range(8):
            e.tensor_scalar(out=dtrn[:, tt, :], in0=ssm[:, tt, 0:128],
                            scalar1=stat[:, 24 + tt:25 + tt], scalar2=None, op0=OP.mult)
            e.tensor_scalar(out=bcn[:, tt, 0:16], in0=ssm[:, tt, 128:144],
                            scalar1=stat[:, 32 + tt:33 + tt], scalar2=None, op0=OP.mult)
            inst = e.tensor_scalar(out=bcn[:, tt, 32:48], in0=ssm[:, tt, 144:160],
                                   scalar1=stat[:, 40 + tt:41 + tt], scalar2=None,
                                   op0=OP.mult)
        inst.then_inc(s_n.h, 1)
    p.emit("vector", fnorm)
    s_n.n += 1
    tp = p.get_tp()
    wait(p, "tensor", s_n)
    for tt in range(8):
        tp.emit(dtrn[:, tt, :], dtrfm[:, tt * 128:(tt + 1) * 128], idb[:, :])
    identf = p.sbuf("identf", [128, 128], F32)
    s_id = p.sem("id")
    def fid(e):
        e.copy(out=identf[:, :], in_=idb[:, :]).then_inc(s_id.h, 1)
    p.emit("scalar", fid)
    s_id.n += 1
    ps = p.main_pool()
    s_bt = p.sem("bt"); s_bte = p.sem("bte")
    wait(p, "tensor", s_id)
    for tt in range(8):
        bank = tt % 2
        mmv = inc_last(s_bt)
        def fT(e, tt=tt, bank=bank):
            if tt >= 2:
                e.wait_ge(s_bte.h, 2 * (tt - 1))
            e.transpose(ps[0:128, bank, 0:128], bcn[:, tt, 0:128],
                        identf[:, :]).then_inc(s_bt.h, 1)
        p.emit("tensor", fT)
        inc_last(s_bte, 2)
        def fTe(e, tt=tt, bank=bank, mmv=mmv):
            e.wait_ge(s_bt.h, mmv)
            e.copy(out=bct[:, 0, tt * 128:(tt + 1) * 128],
                   in_=ps[0:16, bank, 0:128]).then_inc(s_bte.h, 1)
            e.copy(out=bct[:, 1, tt * 128:(tt + 1) * 128],
                   in_=ps[32:48, bank, 0:128]).then_inc(s_bte.h, 1)
        p.emit("scalar", fTe)
    wait(p, "sync", s_bte)
    wait(p, "tensor", tp.s_ev)
    s_mm3 = p.sem("dt_mm"); s_ev3 = p.sem("dt_ev"); s_v3 = p.sem("dt_v")
    wait(p, "tensor", s_ldg)
    jobs = 0
    for m in range(32):
        for tc in range(2):
            bank = jobs % 4
            mmv = inc_last(s_mm3)
            def fmm3(e, m=m, tc=tc, bank=bank, jobs=jobs, bte=s_bte.n):
                if jobs < 4:
                    e.wait_ge(s_bte.h, bte)
                if jobs >= 4:
                    e.wait_ge(s_v3.h, jobs - 3)
                e.matmul(ps[:, bank, 0:512], dtwb[:, m, :],
                         dtrfm[:, tc * 512:(tc + 1) * 512],
                         start=True, stop=True).then_inc(s_mm3.h, 1)
            p.emit("tensor", fmm3)
            scv = inc_last(s_ev3)
            def fev3(e, m=m, tc=tc, bank=bank, mmv=mmv, jobs=jobs):
                e.wait_ge(s_mm3.h, mmv)
                slot = jobs % 2
                e.activation(out=dto[:, m, tc * 512:(tc + 1) * 512], in_=ps[:, bank, 0:512],
                             func=AF.Relu, bias=dtbb[:, m:m + 1])
                e.activation(out=lscr[:, slot, :], in_=ps[:, bank, 0:512], func=AF.Abs,
                             bias=dtbb[:, m:m + 1])
                e.activation(out=lscr[:, slot, :], in_=lscr[:, slot, :], func=AF.Exp,
                             scale=-1.0)
                e.activation(out=lscr[:, slot, :], in_=lscr[:, slot, :], func=AF.Ln,
                             bias=cst[:, 2:3]).then_inc(s_ev3.h, 1)
            p.emit("scalar", fev3)
            inc_last(s_v3)
            def fv3(e, m=m, tc=tc, scv=scv, jobs=jobs):
                e.wait_ge(s_ev3.h, scv)
                slot = jobs % 2
                e.tensor_tensor(out=dto[:, m, tc * 512:(tc + 1) * 512],
                                in0=dto[:, m, tc * 512:(tc + 1) * 512],
                                in1=lscr[:, slot, :], op=OP.add).then_inc(s_v3.h, 1)
            p.emit("vector", fv3)
            jobs += 1
    wait(p, "sync", s_v3)
    dma(p, "sync", dt_o.rearrange("(mt q) t -> q mt t", q=128), dto[:, :, :], s_out)
    dma(p, "sync", bt_o, bct[0:16, 0, :], s_out)
    dma(p, "sync", ct_o, bct[0:16, 1, :], s_out)
    wait(p, "sync", s_out)
    return p.finish()


def host_l1mid_inmaps(inputs, parts):
    dtw = np.ascontiguousarray(inputs["L1_attn_dt_proj_w"], dtype=np.float32)
    dtb = np.ascontiguousarray(inputs["L1_attn_dt_proj_b"].reshape(32, 128).T)
    identb = np.eye(128, dtype=BF)
    maps = []
    for c in range(NCORE):
        p8 = np.stack([parts[j][c * TPC:(c + 1) * TPC] for j in range(8)], axis=0)
        maps.append(dict(part8=np.ascontiguousarray(p8), dtw=dtw, dtb=dtb,
                         identb=identb, cst=CST))
    return maps


def build_l1scan():
    p = Prog("l1scan")
    d = {}
    for nm, shp, dt in [("hs_in", [512, 8192], BF16), ("dt_in", [512, 8192], F32),
                        ("bt_in", [16, 8192], F32), ("ct_in", [16, 8192], F32),
                        ("alog", [128, 4, 16], F32), ("dp", [128, 4], F32),
                        ("wg", [DM, 512], F32), ("xlnf", [DM, 8192], BF16)]:
        d[nm] = p.dram(nm, shp, dt, "ExternalInput").ap()
    ytil_o = p.dram("ytil_o", [512, 8192], BF16, "ExternalOutput").ap()

    s_ld = p.sem(); s_ldg = p.sem(); s_out = p.sem()
    hs = p.sbuf("hs", [128, 4, 8192], BF16)
    Ab_ = p.sbuf("Ab", [128, 4, 16], F32)
    dpb = p.sbuf("dpb", [128, 4], F32)
    wgb = p.sbuf("wgb", [128, 16, 4, 128], BF16)
    dtb_ = p.sbuf("dtb", [128, 4096], F32)
    dtu = p.sbuf("dtu", [128, 4096], F32)
    bbc = p.sbuf("bbc", [128, 1, 2048], F32)
    cbc = p.sbuf("cbc", [128, 1, 2048], F32)
    asc = p.sbuf("asc", [128, 2048], F32)
    bsc = p.sbuf("bsc", [128, 2048], F32)
    ssc = p.sbuf("ssc", [128, 1, 2048], F32)
    carry = p.sbuf("carry", [128, 16], F32)
    tmp = p.sbuf("tmp", [128, 2048], F32)
    yb = p.sbuf("yb", [128, 2048], F32)
    gsl = p.sbuf("gsl", [128, 2048], BF16)
    xch = p.sbuf("xch", [128, 1, 16, 512], BF16)
    ytb = p.sbuf("ytb", [128, 2048], BF16)

    dma(p, "sync", Ab_[:, :, :], d["alog"], s_ld)
    dma(p, "sync", dpb[:, :], d["dp"], s_ld)
    dma(p, "sync", hs[:, :, :], d["hs_in"].rearrange("(mt q) t -> q mt t", q=128), s_ld)
    dma(p, "gpsimd", wgb[:, :, :, :],
        d["wg"].rearrange("(kt q) (mt m) -> q kt mt m", q=128, m=128), s_ldg)
    s_A = p.sem("A")
    wait(p, "scalar", s_ld)
    def fA(e):
        e.activation(out=Ab_[:, :, :], in_=Ab_[:, :, :], func=AF.Exp, scale=1.0)
        e.activation(out=Ab_[:, :, :], in_=Ab_[:, :, :], func=AF.Copy,
                     scale=-1.0).then_inc(s_A.h, 1)
    p.emit("scalar", fA)
    s_A.n += 1
    wait(p, "vector", s_ld)

    ps = p.main_pool()
    s_dt = p.sem("dt")
    s_v = p.sem("v")
    s_s = p.sem("s")
    s_b = [p.sem(f"bb{i}") for i in range(2)]
    s_g = p.sem("g")
    s_gm = p.sem("gm")
    s_yo = p.sem("yo")
    nv = 0
    ns = 0
    it = 0
    for dtile in range(4):
        for b in range(2):
            wait(p, "sync", s_s)
            dwait = dma(p, "sync", dtb_[:, :],
                        d["dt_in"][dtile * 128:(dtile + 1) * 128,
                                   b * 4096:(b + 1) * 4096], s_dt)
            nv = inc_last(s_v)
            def fdtu(e, dtile=dtile, b=b, dwait=dwait):
                e.wait_ge(s_dt.h, dwait)
                e.tensor_tensor(out=dtu[:, :], in0=dtb_[:, :],
                                in1=hs[:, dtile, b * 4096:(b + 1) * 4096],
                                op=OP.mult).then_inc(s_v.h, 1)
            p.emit("vector", fdtu)
            for half in range(2):
                off = half * 2048
                gt0 = b * 4096 + off
                for n in range(16):
                    par = 0
                    vprev = s_v.n
                    p.emit("sync", lambda e, t=vprev: e.wait_ge(s_v.h, t))
                    bw = dma(p, "sync", bbc[:, par, :],
                             d["bt_in"][n:n + 1, gt0:gt0 + 2048].broadcast_to((128, 2048)),
                             s_b[par])
                    cw = dma(p, "sync", cbc[:, par, :],
                             d["ct_in"][n:n + 1, gt0:gt0 + 2048].broadcast_to((128, 2048)),
                             s_b[par])
                    prev_scan = s_v.n - 1
                    ns = inc_last(s_s)
                    def fa(e, dtile=dtile, n=n, off=off, nv=nv, ns=ns, prev_scan=prev_scan):
                        e.wait_ge(s_v.h, max(nv, prev_scan))
                        e.activation(out=asc[:, :], in_=dtb_[:, off:off + 2048],
                                     func=AF.Exp,
                                     scale=Ab_[:, dtile, n:n + 1]).then_inc(s_s.h, 1)
                    p.emit("scalar", fa)
                    nv_prev_scan = s_v.n
                    def fb(e, par=par, off=off, bw=bw, ns=ns):
                        e.wait_ge(s_b[par].h, bw + 16)
                        e.tensor_tensor(out=bsc[:, :], in0=dtu[:, off:off + 2048],
                                        in1=bbc[:, par, :], op=OP.mult)
                        e.wait_ge(s_s.h, ns)
                    p.emit("vector", fb)
                    init = 0.0 if half == 0 else carry[:, n:n + 1]
                    nv = inc_last(s_v, 2)
                    def fscan(e, init=init, par=par, n=n, half=half):
                        e.tensor_tensor_scan(out=ssc[:, 0, :], data0=asc[:, :],
                                             data1=bsc[:, :], initial=init,
                                             op0=OP.mult, op1=OP.add)
                        if half == 0:
                            e.tensor_copy(out=carry[:, n:n + 1], in_=ssc[:, 0, 2047:2048])
                        e.tensor_scalar(out=ssc[:, 0, 0:1], in0=ssc[:, 0, 0:1],
                                        scalar1=1.0, scalar2=None,
                                        op0=OP.mult).then_inc(s_v.h, 1)
                        e.tensor_tensor(out=tmp[:, :], in0=ssc[:, 0, :],
                                        in1=cbc[:, par, :], op=OP.mult)
                        if n == 0:
                            e.tensor_copy(out=yb[:, :], in_=tmp[:, :])
                            e.engine_nop() if False else None
                            inst = e.tensor_scalar(out=yb[:, 0:1], in0=yb[:, 0:1],
                                                   scalar1=1.0, scalar2=None, op0=OP.mult)
                        else:
                            inst = e.tensor_tensor(out=yb[:, :], in0=yb[:, :],
                                                   in1=tmp[:, :], op=OP.add)
                        inst.then_inc(s_v.h, 1)
                    p.emit("vector", fscan)
                for tc in range(4):
                    xbuf = 0
                    gprev = s_gm.n
                    p.emit("sync", lambda e, t=gprev: e.wait_ge(s_gm.h, t) if t > 0 else None)
                    xw = dma(p, "sync", xch[:, xbuf, :, :],
                             d["xlnf"][:, gt0 + tc * 512:gt0 + (tc + 1) * 512]
                             .rearrange("(kt q) t -> q kt t", q=128), s_ld)
                    bank = tc % 2
                    gmv = inc_last(s_gm)
                    def fgm(e, dtile=dtile, tc=tc, xbuf=xbuf, bank=bank, xw=xw,
                            it=it, gmv=gmv):
                        e.wait_ge(s_ld.h, xw)
                        if gmv > 2:
                            e.wait_ge(s_g.h, gmv - 2)
                        if gmv == 1:
                            e.wait_ge(s_ldg.h, 16)
                        inst = None
                        for k in range(16):
                            inst = e.matmul(ps[:, bank, 0:512], wgb[:, k, dtile, :],
                                            xch[:, xbuf, k, :], start=(k == 0),
                                            stop=(k == 15))
                        inst.then_inc(s_gm.h, 1)
                    p.emit("tensor", fgm)
                    inc_last(s_g)
                    def fge(e, tc=tc, bank=bank, gmv=gmv):
                        e.wait_ge(s_gm.h, gmv)
                        e.activation(out=gsl[:, tc * 512:(tc + 1) * 512],
                                     in_=ps[:, bank, 0:512],
                                     func=AF.Silu).then_inc(s_g.h, 1)
                    p.emit("scalar", fge)
                yv = inc_last(s_v)
                gev = s_g.n
                def fcomp(e, dtile=dtile, b=b, off=off, gev=gev):
                    e.scalar_tensor_tensor(out=tmp[:, :],
                                           in0=hs[:, dtile, b * 4096 + off:b * 4096 + off + 2048],
                                           scalar=dpb[:, dtile:dtile + 1], in1=yb[:, :],
                                           op0=OP.mult, op1=OP.add)
                    e.wait_ge(s_g.h, gev)
                    e.tensor_tensor(out=ytb[:, :], in0=tmp[:, :], in1=gsl[:, :],
                                    op=OP.mult).then_inc(s_v.h, 1)
                p.emit("vector", fcomp)
                def fout(e, dtile=dtile, gt0=gt0, yv=yv):
                    e.wait_ge(s_v.h, yv)
                    e.dma_start(out=ytil_o[dtile * 128:(dtile + 1) * 128,
                                           gt0:gt0 + 2048],
                                in_=ytb[:, :]).then_inc(s_yo.h, 16)
                p.emit("sync", fout)
                s_yo.n += 16
                yo_now = s_yo.n
                def fg2(e, yo_now=yo_now):
                    e.wait_ge(s_yo.h, yo_now)
                p.emit("vector", fg2)
                it += 1
    wait(p, "sync", s_yo)
    return p.finish()


def host_l1scan_inmaps(inputs, hs_full, dt_full, bt, ct, xlnf_full):
    wfull = inputs["L1_attn_in_proj_w"]
    alog = inputs["L1_attn_A_log"]
    dpar = inputs["L1_attn_D_param"]
    maps = []
    for ds in range(NCORE):
        r0 = 512 * ds
        maps.append(dict(
            hs_in=np.ascontiguousarray(hs_full[r0:r0 + 512]),
            dt_in=np.ascontiguousarray(dt_full[r0:r0 + 512]),
            bt_in=bt, ct_in=ct,
            alog=np.ascontiguousarray(alog[r0:r0 + 512].reshape(4, 128, 16).transpose(1, 0, 2)),
            dp=np.ascontiguousarray(dpar[r0:r0 + 512].reshape(4, 128).T),
            wg=np.ascontiguousarray(wfull[:, 4096 + r0:4096 + r0 + 512], dtype=np.float32),
            xlnf=xlnf_full))
    return maps


def build_l0attn(debug=False):
    p = Prog("l0attn")
    d = {}
    for nm, shp, dt in [("x_in", [4096, DM], F32), ("wz", [DM, 1024], F32),
                        ("wx", [DM, 1024], F32), ("wbc", [DM, 256], F32),
                        ("wdt", [DM, 16], F32), ("cwx", [128, 8, 4], F32),
                        ("cbx", [128, 8], F32), ("cwbc", [128, 2, 4], F32),
                        ("cbbc", [128, 2], F32), ("dtbv", [16, 1], F32),
                        ("alog", [16, 1], F32), ("dpar", [1, 16], F32),
                        ("mneg", [128, 128], F32), ("oneh", [128, 1], F32),
                        ("identb", [128, 128], BF16), ("identf", [128, 128], F32),
                        ("cst", [128, 8], F32)]:
        d[nm] = p.dram(nm, shp, dt, "ExternalInput").ap()
    yf_o = p.dram("yf_o", [4096, 1024], BF16, "ExternalOutput").ap()
    ssq_o = p.dram("ssq_o", [128, 32], F32, "ExternalOutput").ap()
    kk = "ExternalOutput" if debug else None
    acf_h = p.dram("acf", [1, 65536], F32, kk)
    acf = acf_h.ap()
    smp_h = p.dram("smp", [1, 1024], F32, kk)
    smp = smp_h.ap()
    zsp_h = p.dram("zsp", [1024, 4096], BF16, kk)
    zsp = zsp_h.ap()
    ysp_h = p.dram("ysp", [4096, 1024], BF16, kk)
    ysp = ysp_h.ap()
    xssp_h = p.dram("xssp", [1024, 4096], BF16, kk)
    xssp = xssp_h.ap()

    cwxb = p.sbuf("cwxb", [128, 8, 4], F32)
    cbxb = p.sbuf("cbxb", [128, 8], F32)
    cwbcb = p.sbuf("cwbcb", [128, 2, 4], F32)
    cbbcb = p.sbuf("cbbcb", [128, 2], F32)
    dtbb = p.sbuf("dtbb", [16, 1], F32)
    alogb = p.sbuf("alogb", [16, 2], F32)
    mnegb = p.sbuf("mnegb", [128, 128], F32)
    onehb = p.sbuf("onehb", [128, 1], F32)
    idb = p.sbuf("idb", [128, 128], BF16)
    idf = p.sbuf("idf", [128, 128], F32)
    cstb = p.sbuf("cstb", [128, 8], F32)
    stat = p.sbuf("stat", [128, 256], F32)
    drep = p.sbuf("drep", [128, 16], F32)
    sampb = p.sbuf("sampb", [1, 1024], F32)
    ssqb = p.sbuf("ssqb", [128, 32], F32)
    dtpre = p.sbuf("dtpre", [16, 4096], F32)

    AR = p.sbuf("AR", [128, 93440], BF16)
    def Ab(off, n):
        return AR[:, off:off + n]
    def Af(off, n):
        return AR[:, off:off + 2 * n].bitcast(F32)

    s_ld = p.sem("ld"); s_out = p.sem("out")
    for nm, dst in [("cwx", cwxb[:, :, :]), ("cbx", cbxb[:, :]),
                    ("cwbc", cwbcb[:, :, :]), ("cbbc", cbbcb[:, :]),
                    ("dtbv", dtbb[:, :]), ("alog", alogb[:, 0:1]),
                    ("mneg", mnegb[:, :]), ("oneh", onehb[:, :]),
                    ("identb", idb[:, :]), ("identf", idf[:, :]),
                    ("cst", cstb[:, :])]:
        dma(p, "sync", dst, d[nm], s_ld)
    dma(p, "sync", drep[:, :], d["dpar"].broadcast_to((128, 16)), s_ld)

    sqscr = Af(76800, 2048)
    xlq = Ab(0, 16384).rearrange("z (k t) -> z k t", k=16)
    xcv = Ab(16384, 32800).rearrange("z (f t) -> z f t", f=8)
    bccv = Ab(49184, 8200).rearrange("z (f t) -> z f t", f=2)
    xring = Af(57384, 4096).rearrange("z (r t) -> z r t", r=2)
    lnscr = Ab(65576, 4096).rearrange("z (r t) -> z r t", r=2)
    zring = Ab(69672, 2048).rearrange("z (r t) -> z r t", r=4)
    wb = Ab(71720, 4096).rearrange("z (r k m) -> z r k m", r=2, k=16)
    ps = p.main_pool()
    tp = p.get_tp()

    s_x = p.sem("x"); s_ph = p.sem("ph"); s_sc = p.sem("sc"); s_ln = p.sem("ln")
    s_w = [p.sem(f"w{i}") for i in range(2)]
    s_mm = p.sem("mm"); s_ev = p.sem("ev"); s_zo = p.sem("zo")
    wait(p, "vector", s_ld)
    wait(p, "scalar", s_ld)
    def fz0(e):
        e.memset(Ab(16384, 41000), 0.0)
    p.emit("vector", fz0)
    wjobs = ([("z", d["wz"], m, 128) for m in range(8)]
             + [("x", d["wx"], m, 128) for m in range(8)]
             + [("bc", d["wbc"], m, 128) for m in range(2)]
             + [("dt", d["wdt"], 0, 16)])
    mwi = 0
    job = 0
    for q in range(4):
        for i in range(8):
            tt = q * 8 + i
            prev_ln = s_ln.n - 1
            if prev_ln > 0:
                p.emit("sync", lambda e, t=prev_ln: e.wait_ge(s_ln.h, t))
            xw = dma(p, "sync", xring[:, i % 2, :],
                     d["x_in"][tt * 128:(tt + 1) * 128, :], s_x)
            wait(p, "vector", s_x, xw)
            wait(p, "scalar", s_x, xw)
            def cb(e, ii, mean, rstd, i=i):
                e.tensor_scalar(out=lnscr[:, i % 2, :], in0=xring[:, i % 2, :],
                                scalar1=mean, scalar2=rstd, op0=OP.subtract,
                                op1=OP.mult).then_inc(s_ln.h, 1)
            emit_ln8(p, lambda ii, i=i: xring[:, i % 2, :], stat, s_ph, s_sc, cb,
                     1e-5, ntiles=1, sqscr=sqscr[:, :])
            inc_last(s_ln)
            wait(p, "tensor", s_ln, s_ln.n)
            for ct in range(16):
                tp.emit(lnscr[:, i % 2, ct * 128:(ct + 1) * 128],
                        xlq[:, ct, i * 128:(i + 1) * 128], idb[:, :], evac_eng="scalar")
        tp.drain(["tensor"])
        for (kind, wap, m, mcols) in wjobs:
            buf = mwi % 2
            if mwi >= 2:
                need = (mwi - 1) * 2
                p.emit("gpsimd", lambda e, t=need, s=s_mm.h: e.wait_ge(s, t))
            dma(p, "gpsimd", wb[:, buf, 0:16, 0:mcols],
                wap[:, m * mcols:(m + 1) * mcols]
                .rearrange("(kt z) c -> z kt c", z=128), s_w[buf])
            wwait = 16 * (mwi // 2 + 1)
            for tc in range(2):
                bank = job % 4
                mmv = inc_last(s_mm)
                freewait = job - 4 + 1
                prows = 128 if mcols == 128 else 16
                def fmm(e, buf=buf, bank=bank, wwait=wwait, freewait=freewait,
                        mcols=mcols, tc=tc, prows=prows):
                    e.wait_ge(s_w[buf].h, wwait)
                    if freewait > 0:
                        e.wait_ge(s_ev.h, freewait)
                    inst = None
                    for k in range(16):
                        inst = e.matmul(ps[0:prows, bank, 0:512],
                                        wb[:, buf, k, 0:mcols],
                                        xlq[:, k, tc * 512:(tc + 1) * 512],
                                        start=(k == 0), stop=(k == 15))
                    inst.then_inc(s_mm.h, 1)
                p.emit("tensor", fmm)
                col0 = q * 1024 + tc * 512
                inc_last(s_ev)
                if kind == "z":
                    zi = job
                    def fevz(e, bank=bank, mmv=mmv, zi=zi, zon=s_zo.n):
                        e.wait_ge(s_mm.h, mmv)
                        if zon >= 64:
                            e.wait_ge(s_zo.h, zon - 48)
                        e.copy(out=zring[:, zi % 4, :],
                               in_=ps[:, bank, 0:512]).then_inc(s_ev.h, 1)
                    p.emit("scalar", fevz)
                    evn = s_ev.n
                    def fdz(e, m=m, col0=col0, evn=evn, zi=zi):
                        e.wait_ge(s_ev.h, evn)
                        e.dma_start(out=zsp[m * 128:(m + 1) * 128, col0:col0 + 512],
                                    in_=zring[:, zi % 4, :]).then_inc(s_zo.h, 16)
                    p.emit("sync", fdz)
                    s_zo.n += 16
                elif kind == "x":
                    def fev(e, m=m, bank=bank, mmv=mmv, col0=col0):
                        e.wait_ge(s_mm.h, mmv)
                        e.copy(out=xcv[:, m, 3 + col0:3 + col0 + 512],
                               in_=ps[:, bank, 0:512]).then_inc(s_ev.h, 1)
                    p.emit("scalar", fev)
                elif kind == "bc":
                    def fev(e, m=m, bank=bank, mmv=mmv, col0=col0):
                        e.wait_ge(s_mm.h, mmv)
                        e.copy(out=bccv[:, m, 3 + col0:3 + col0 + 512],
                               in_=ps[:, bank, 0:512]).then_inc(s_ev.h, 1)
                    p.emit("scalar", fev)
                else:
                    def fev(e, bank=bank, mmv=mmv, col0=col0):
                        e.wait_ge(s_mm.h, mmv)
                        e.copy(out=dtpre[:, col0:col0 + 512],
                               in_=ps[0:16, bank, 0:512]).then_inc(s_ev.h, 1)
                    p.emit("scalar", fev)
                job += 1
            mwi += 1
    cs = Af(57384, 4096)
    xsr = Ab(0, 8192).rearrange("z (r t) -> z r t", r=2)
    bcs = Ab(8192, 8192).rearrange("z (f t) -> z f t", f=2)
    s_conv = p.sem("conv"); s_hs = p.sem("hs"); s_xso = p.sem("xso")
    wait(p, "vector", s_ev)
    wait(p, "scalar", s_ev)
    for grp in range(2):
        n_ft = 8 if grp == 0 else 2
        for ft in range(n_ft):
            src_t, cvb, cbv = ((xcv, cwxb, cbxb) if grp == 0 else (bccv, cwbcb, cbbcb))
            cv = inc_last(s_conv)
            def fconv(e, ft=ft, src_t=src_t, cvb=cvb, xon=s_xso.n):
                if xon >= 32:
                    e.wait_ge(s_xso.h, xon - 16)
                e.tensor_scalar(out=cs[:, :], in0=src_t[:, ft, 0:4096],
                                scalar1=cvb[:, ft, 0:1], scalar2=None, op0=OP.mult)
                inst = None
                for w in range(1, 4):
                    inst = e.scalar_tensor_tensor(out=cs[:, :],
                                                  in0=src_t[:, ft, w:w + 4096],
                                                  scalar=cvb[:, ft, w:w + 1],
                                                  in1=cs[:, :], op0=OP.mult, op1=OP.add)
                inst.then_inc(s_conv.h, 1)
            p.emit("vector", fconv)
            hv = inc_last(s_hs)
            if grp == 0:
                def fsilu(e, ft=ft, cv=cv, cbv=cbv):
                    e.wait_ge(s_conv.h, cv)
                    e.activation(out=xsr[:, ft % 2, :], in_=cs[:, :], func=AF.Silu,
                                 bias=cbv[:, ft:ft + 1]).then_inc(s_hs.h, 1)
                p.emit("scalar", fsilu)
                def fxo(e, ft=ft, hv=hv):
                    e.wait_ge(s_hs.h, hv)
                    e.dma_start(out=xssp[ft * 128:(ft + 1) * 128, :],
                                in_=xsr[:, ft % 2, :]).then_inc(s_xso.h, 16)
                p.emit("sync", fxo)
                s_xso.n += 16
            else:
                def fsilu(e, ft=ft, cv=cv, cbv=cbv):
                    e.wait_ge(s_conv.h, cv)
                    e.activation(out=bcs[:, ft, :], in_=cs[:, :], func=AF.Silu,
                                 bias=cbv[:, ft:ft + 1]).then_inc(s_hs.h, 1)
                p.emit("scalar", fsilu)
            wait(p, "vector", s_hs, hv)
    dtf = Af(16384, 4096)[0:16, :]
    zer = Af(24576, 4096)[0:16, :]
    acum = Af(32768, 4096)[0:16, :]
    stkT = Af(40960, 1024).rearrange("z (t c) -> z t c", t=32)
    negAc = Af(43008, 512).rearrange("z (t c) -> z t c", t=32)
    odT = Af(44032, 512).rearrange("z (t c) -> z t c", t=32)
    nsprep = Af(45056, 512)
    dchrep = Af(46080, 512)
    s_c = p.sem("cc"); s_cs = p.sem("ccs")
    s_ev_total = s_ev.n
    def fsp(e):
        e.wait_ge(s_ev.h, s_ev_total)
        e.activation(out=alogb[:, 1:2], in_=alogb[:, 0:1], func=AF.Exp)
        e.activation(out=alogb[:, 1:2], in_=alogb[:, 1:2], func=AF.Copy, scale=-1.0)
        e.activation(out=dtf[:, :], in_=dtpre[:, :], func=AF.Relu, bias=dtbb[:, 0:1])
        e.activation(out=zer[:, :], in_=dtpre[:, :], func=AF.Abs, bias=dtbb[:, 0:1])
        e.activation(out=zer[:, :], in_=zer[:, :], func=AF.Exp, scale=-1.0)
        e.activation(out=zer[:, :], in_=zer[:, :], func=AF.Ln,
                     bias=cstb[0:16, 2:3]).then_inc(s_cs.h, 1)
    p.emit("scalar", fsp)
    cs1 = inc_last(s_cs)
    c1 = inc_last(s_c)
    def fdt(e, cs1=cs1):
        e.wait_ge(s_cs.h, cs1)
        e.tensor_tensor(out=dtf[:, :], in0=dtf[:, :], in1=zer[:, :], op=OP.add)
        e.tensor_scalar(out=acum[:, :], in0=dtf[:, :], scalar1=alogb[:, 1:2],
                        scalar2=None, op0=OP.mult)
        e.memset(zer[:, :], 0.0)
        e.memset(sampb[:, :], 0.0)
        e.memset(sqscr[:, 0:512], 0.0)
        e.tensor_tensor_scan(out=acum[:, :], data0=acum[:, :], data1=zer[:, :],
                             initial=0.0, op0=OP.add, op1=OP.add)
        for _ in range(6):
            e.tensor_copy(out=sqscr[:, 1024:2048], in_=sqscr[:, 0:1024])
        e.tensor_copy(out=sqscr[:, 1024:2048],
                      in_=sqscr[:, 0:1024]).then_inc(s_c.h, 1)
    p.emit("vector", fdt)
    s_st = p.sem("st"); s_ste = p.sem("ste")
    wait(p, "tensor", s_c, c1)
    for tt in range(32):
        for half16, srcv in ((0, dtf), (1, acum)):
            bank = (2 * tt + half16) % 2
            mmv = inc_last(s_st)
            idx = 2 * tt + half16
            def fT(e, tt=tt, bank=bank, srcv=srcv, idx=idx):
                if idx >= 2:
                    e.wait_ge(s_ste.h, idx - 1)
                e.transpose(ps[0:128, bank, 0:16], srcv[:, tt * 128:(tt + 1) * 128],
                            idf[0:16, 0:16]).then_inc(s_st.h, 1)
            p.emit("tensor", fT)
            inc_last(s_ste)
            def fTe(e, tt=tt, bank=bank, mmv=mmv, half16=half16):
                e.wait_ge(s_st.h, mmv)
                e.copy(out=stkT[:, tt, 16 * half16:16 * half16 + 16],
                       in_=ps[:, bank, 0:16]).then_inc(s_ste.h, 1)
            p.emit("scalar", fTe)
    s_ac = p.sem("ac")
    wait(p, "sync", s_c, c1)
    dma(p, "sync", acf.rearrange("o (h t) -> (o h) t", h=16), acum[:, :], s_ac)
    s_sp = p.sem("sp"); s_spe = p.sem("spe")
    wait(p, "tensor", s_ste)
    def fsamp(e):
        inst = None
        for tt in range(32):
            inst = e.matmul(ps[0:1, 3, tt * 16:(tt + 1) * 16], onehb[:, :],
                            stkT[:, tt, 16:32], start=(tt == 0), stop=(tt == 31))
        inst.then_inc(s_sp.h, 1)
    p.emit("tensor", fsamp)
    sp1 = inc_last(s_sp)
    def fsampe(e, sp1=sp1):
        e.wait_ge(s_sp.h, sp1)
        e.copy(out=sampb[:, 0:512], in_=ps[0:1, 3, 0:512]).then_inc(s_spe.h, 1)
    p.emit("scalar", fsampe)
    spe1 = inc_last(s_spe)
    def fshift(e, spe1=spe1):
        e.wait_ge(s_spe.h, spe1)
        e.memset(sqscr[:, 512:1024], 0.0)
        e.memset(sqscr[:, 1024:1536], 0.0)
        e.memset(sampb[:, 512:528], 0.0)
        e.tensor_copy(out=sampb[:, 528:1024], in_=sampb[:, 0:496])
        e.tensor_tensor(out=sampb[:, 0:512], in0=sampb[:, 0:512],
                        in1=sampb[:, 512:1024], op=OP.subtract)
        e.tensor_scalar(out=sampb[:, 512:1024], in0=sampb[:, 512:1024],
                        scalar1=-1.0, scalar2=None, op0=OP.mult).then_inc(s_c.h, 1)
    p.emit("vector", fshift)
    c2 = inc_last(s_c)
    def fdch(e, c2=c2):
        e.wait_ge(s_c.h, c2)
        e.activation(out=sqscr[:, 0:512], in_=sqscr[:, 0:512], func=AF.Square)
        e.activation(out=sqscr[:, 0:512], in_=sqscr[:, 0:512], func=AF.Square)
        e.activation(out=sampb[:, 0:512], in_=sampb[:, 0:512],
                     func=AF.Exp).then_inc(s_cs.h, 1)
    p.emit("scalar", fdch)
    cs2 = inc_last(s_cs)
    p.emit("sync", lambda e, t=cs2: e.wait_ge(s_cs.h, t))
    dma(p, "sync", smp, sampb[:, :], s_ac)
    wait(p, "sync", s_ac)
    dma(p, "sync", dchrep, smp[0:1, 0:512].broadcast_to((128, 512)), s_ac)
    dma(p, "sync", nsprep, smp[0:1, 512:1024].broadcast_to((128, 512)), s_ac)
    ac_all = s_ac.n
    wait(p, "vector", s_ac)
    c3 = inc_last(s_c)
    def fneg(e):
        inst = None
        for tt in range(32):
            e.tensor_scalar(out=negAc[:, tt, :], in0=stkT[:, tt, 16:32],
                            scalar1=-1.0, scalar2=None, op0=OP.mult)
            inst = e.tensor_tensor(out=odT[:, tt, :], in0=stkT[:, tt, 16:32],
                                   in1=nsprep[:, tt * 16:(tt + 1) * 16], op=OP.add)
        inst.then_inc(s_c.h, 1)
    p.emit("vector", fneg)
    def fodx(e, c3=c3):
        e.wait_ge(s_c.h, c3)
        e.activation(out=odT[:, :, :], in_=odT[:, :, :], func=AF.Exp).then_inc(s_cs.h, 1)
    p.emit("scalar", fodx)
    cs3 = inc_last(s_cs)
    xsh = Ab(47104, 16384).rearrange("z (f t) -> z f t", f=8)
    xT = Ab(63488, 16384).rearrange("z (t c) -> z t c", t=16)
    xdtT = Ab(47104, 16384).rearrange("z (t c) -> z t c", t=16)
    S = Af(79872, 1024)
    Sb = Ab(81920, 1024)
    brow = Af(82944, 2048).rearrange("z (h t) -> z h t", h=16)
    mtall = Ab(87040, 2048).rearrange("z (h t) -> z h t", h=16)
    xddall = Ab(89088, 1024).rearrange("z (h t) -> z h t", h=16)
    gsb = Af(90112, 128)
    yring = Ab(90368, 1024).rearrange("z (r t) -> z r t", r=1)
    BT = Ab(91392, 2048).rearrange("z (t c) -> z t c", t=16)
    t0all = Af(87040, 1024).rearrange("z (h t) -> z h t", h=16)
    s_xl = p.sem("xl")
    s_br = p.sem("br")
    s_bm = p.sem("bm")
    s_e = p.sem("ee"); s_mt = p.sem("mt"); s_pe = p.sem("pe"); s_vv = p.sem("vv")
    s_yd = p.sem("yd"); s_xdt = p.sem("xdt")
    def fS0(e):
        e.memset(S[:, :], 0.0)
        e.memset(Sb[:, :], 0.0)
    p.emit("vector", fS0)
    gps = tp.pool[:, 2, 0:256].bitcast(F32)
    pso = tp.pool[:, 3, 0:256].bitcast(F32)
    wait(p, "gpsimd", s_ld)
    for half in range(2):
        hc0 = half * 16
        wait(p, "sync", s_xso)
        if half == 1:
            p.emit("sync", lambda e, t=s_pe.n: e.wait_ge(s_pe.h, t))
        xlw = dma(p, "sync", xsh[:, :, :],
                  xssp[:, half * 2048:(half + 1) * 2048]
                  .rearrange("(ft z) t -> z ft t", z=128), s_xl)
        wait(p, "tensor", s_xl, xlw)
        if half == 0:
            def wtcs(e, t=cs3):
                e.wait_ge(s_cs.h, t)
            p.emit("tensor", wtcs)
        for tt in range(16):
            gtt = hc0 + tt
            for ct in range(8):
                tp.emit(xsh[:, ct, tt * 128:(tt + 1) * 128],
                        xT[:, tt, ct * 128:(ct + 1) * 128], idb[:, :],
                        evac_eng="scalar")
            tp.emit(bcs[:, 0, gtt * 128:(gtt + 1) * 128], BT[:, tt, :], idb[:, :],
                    evac_eng="scalar")
        tp.drain(["vector"])
        wait(p, "vector", s_ste)
        xdtv = inc_last(s_xdt)
        def fxdt(e, hc0=hc0):
            inst = None
            for tt in range(16):
                for h in range(16):
                    inst = e.tensor_scalar(out=xdtT[:, tt, h * 64:(h + 1) * 64],
                                           in0=xT[:, tt, h * 64:(h + 1) * 64],
                                           scalar1=stkT[:, hc0 + tt, h:h + 1],
                                           scalar2=None, op0=OP.mult)
            inst.then_inc(s_xdt.h, 1)
        p.emit("vector", fxdt)
        wait(p, "tensor", s_xdt, xdtv)
        wait(p, "vector", s_ac, ac_all)
        for lc in range(16):
            c = hc0 + lc
            e_prev = s_e.n
            if e_prev > 0:
                p.emit("sync", lambda e, t=e_prev: e.wait_ge(s_e.h, t))
            brw = dma(p, "sync", brow[:, :, :],
                      bass.AP(acf_h, c * 128, [[0, 128], [4096, 16], [1, 128]]),
                      s_br)
            bmv = inc_last(s_bm)
            ydma_prev = 16 * c
            def fbm(e, brw=brw, ydma_prev=ydma_prev):
                e.wait_ge(s_br.h, brw)
                if ydma_prev > 0:
                    e.wait_ge(s_yd.h, ydma_prev)
                src2 = mnegb[:, :].rearrange("z t -> z () t").broadcast_to((128, 16, 128))
                e.tensor_tensor(out=brow[:, :, :], in0=brow[:, :, :], in1=src2,
                                op=OP.add)
                e.tensor_copy(out=Sb[:, :], in_=S[:, :]).then_inc(s_bm.h, 1)
            p.emit("vector", fbm)
            gv = inc_last(s_pe)
            vv_done = s_vv.n
            def fG(e, c=c, vv_done=vv_done):
                if vv_done > 0:
                    e.wait_ge(s_vv.h, vv_done)
                e.matmul(ps[:, 0, 0:128], bcs[:, 0, c * 128:(c + 1) * 128],
                         bcs[:, 1, c * 128:(c + 1) * 128],
                         start=True, stop=True).then_inc(s_pe.h, 1)
            p.emit("tensor", fG)
            ev = inc_last(s_e)
            def fES(e, c=c, gv=gv, bmv=bmv):
                e.wait_ge(s_pe.h, gv)
                e.copy(out=gsb[:, :], in_=ps[:, 0, 0:128])
                e.wait_ge(s_bm.h, bmv)
                inst = None
                for h in range(16):
                    inst = e.activation(out=brow[:, h, :], in_=brow[:, h, :],
                                        func=AF.Exp, bias=negAc[:, c, h:h + 1])
                inst.then_inc(s_e.h, 1)
            p.emit("scalar", fES)
            mtv = inc_last(s_mt)
            def fMt(e, c=c, lc=lc, ev=ev):
                e.wait_ge(s_e.h, ev)
                for h in range(16):
                    e.tensor_tensor(out=mtall[:, h, :], in0=brow[:, h, :],
                                    in1=gsb[:, :], op=OP.mult)
                inst = None
                for h in range(16):
                    inst = e.tensor_scalar(out=xddall[:, h, :],
                                           in0=xdtT[:, lc, h * 64:(h + 1) * 64],
                                           scalar1=brow[:, h, 127:128], scalar2=None,
                                           op0=OP.mult)
                inst.then_inc(s_mt.h, 1)
            p.emit("vector", fMt)
            pev = inc_last(s_pe)
            def fPE(e, c=c, lc=lc, mtv=mtv):
                e.wait_ge(s_mt.h, mtv)
                inst = None
                for h in range(16):
                    e.matmul(ps[:, h // 8, (h % 8) * 64:(h % 8) * 64 + 64],
                             mtall[:, h, :], xdtT[:, lc, h * 64:(h + 1) * 64],
                             start=True, stop=True)
                for h in range(16):
                    e.matmul(ps[:, 2 + h // 8, (h % 8) * 64:(h % 8) * 64 + 64],
                             BT[:, lc, :], xddall[:, h, :], start=True, stop=True)
                for h in range(16):
                    inst = e.matmul(
                        tp.pool[:, 2 + h // 8, 0:1024].bitcast(F32)[:, (h % 8) * 64:(h % 8) * 64 + 64],
                        bcs[:, 1, c * 128:(c + 1) * 128], Sb[:, h * 64:(h + 1) * 64],
                        start=True, stop=True)
                inst.then_inc(s_pe.h, 1)
            p.emit("tensor", fPE)
            vv = inc_last(s_vv)
            def fEV(e, c=c, lc=lc, pev=pev):
                e.wait_ge(s_pe.h, pev)
                for h in range(16):
                    e.scalar_tensor_tensor(out=t0all[:, h, :],
                                           in0=xT[:, lc, h * 64:(h + 1) * 64],
                                           scalar=drep[:, h:h + 1],
                                           in1=ps[:, h // 8, (h % 8) * 64:(h % 8) * 64 + 64],
                                           op0=OP.mult, op1=OP.add)
                for h in range(16):
                    e.scalar_tensor_tensor(
                        out=yring[:, 0, h * 64:(h + 1) * 64],
                        in0=tp.pool[:, 2 + h // 8, 0:1024].bitcast(F32)[:, (h % 8) * 64:(h % 8) * 64 + 64],
                        scalar=odT[:, c, h:h + 1], in1=t0all[:, h, :],
                        op0=OP.mult, op1=OP.add)
                inst = None
                for h in range(16):
                    inst = e.scalar_tensor_tensor(
                        out=S[:, h * 64:(h + 1) * 64], in0=S[:, h * 64:(h + 1) * 64],
                        scalar=dchrep[:, c * 16 + h:c * 16 + h + 1],
                        in1=ps[:, 2 + h // 8, (h % 8) * 64:(h % 8) * 64 + 64],
                        op0=OP.mult, op1=OP.add)
                inst.then_inc(s_vv.h, 1)
            p.emit("vector", fEV)
            yv = s_vv.n
            def fdy(e, c=c, yv=yv):
                e.wait_ge(s_vv.h, yv)
                e.dma_start(out=ysp[c * 128:(c + 1) * 128, :],
                            in_=yring[:, 0, :]).then_inc(s_yd.h, 16)
            p.emit("sync", fdy)
            s_yd.n += 16
    zscr = Ab(16384, 1024).rearrange("z (f t) -> z f t", f=8)
    zst = Ab(17408, 1024)
    yfin = Ab(18432, 2048).rearrange("z (r t) -> z r t", r=2)
    s_ze = p.sem("ze"); s_yl = p.sem("yl"); s_yf = p.sem("yf"); s_sq = p.sem("sq")
    wait(p, "sync", s_yd)
    wait(p, "sync", s_zo)
    for tt in range(32):
        zw = dma(p, "sync", zscr[:, :, :],
                 zsp[:, tt * 128:(tt + 1) * 128].rearrange("(ft z) t -> z ft t", z=128),
                 s_ze)
        yw = dma(p, "sync", yfin[:, tt % 2, :], ysp[tt * 128:(tt + 1) * 128, :], s_yl)
        wait(p, "tensor", s_ze, zw)
        for ct in range(8):
            tp.emit(zscr[:, ct, :], zst[:, ct * 128:(ct + 1) * 128], idb[:, :],
                    evac_eng="scalar", func=AF.Silu)
        tp.drain(["vector"])
        yfv = inc_last(s_yf)
        def fyf(e, tt=tt, yw=yw):
            e.wait_ge(s_yl.h, yw)
            e.tensor_tensor(out=yfin[:, tt % 2, :], in0=yfin[:, tt % 2, :],
                            in1=zst[:, :], op=OP.mult).then_inc(s_yf.h, 1)
        p.emit("vector", fyf)
        sqv = inc_last(s_sq)
        def fsq(e, tt=tt, yfv=yfv):
            e.wait_ge(s_yf.h, yfv)
            e.activation(out=sqscr[:, 0:1024], in_=yfin[:, tt % 2, :], func=AF.Square,
                         accum_out=ssqb[:, tt:tt + 1]).then_inc(s_sq.h, 1)
        p.emit("scalar", fsq)
        def fdo(e, tt=tt, sqv=sqv):
            e.wait_ge(s_sq.h, sqv)
            e.dma_start(out=yf_o[tt * 128:(tt + 1) * 128, :],
                        in_=yfin[:, tt % 2, :]).then_inc(s_out.h, 16)
        p.emit("sync", fdo)
        s_out.n += 16
        wait(p, "tensor", s_yf, s_yf.n)
    def fdsq(e):
        e.wait_ge(s_sq.h, 32)
        e.dma_start(out=ssq_o, in_=ssqb[:, :]).then_inc(s_out.h, 16)
    p.emit("sync", fdsq)
    s_out.n += 16
    wait(p, "sync", s_out)
    return p.finish()


def host_l0attn_inmaps(inputs):
    win = inputs["L0_attn_W_in"]
    cw = inputs["L0_attn_conv_w"]
    cb = inputs["L0_attn_conv_b"]
    mneg = np.triu(np.full((128, 128), -1e30, dtype=np.float32), 1).T
    oneh = np.zeros((128, 1), dtype=np.float32)
    oneh[127, 0] = 1.0
    identb = np.eye(128, dtype=BF)
    identf = np.eye(128, dtype=np.float32)
    maps = []
    for core in range(NCORE):
        b, hg = core // 4, core % 4
        z0 = 1024 * hg
        x0 = 4096 + 1024 * hg
        maps.append(dict(
            x_in=np.ascontiguousarray(inputs["L0_attn_u"][b], dtype=np.float32),
            wz=np.ascontiguousarray(win[:, z0:z0 + 1024]),
            wx=np.ascontiguousarray(win[:, x0:x0 + 1024]),
            wbc=np.ascontiguousarray(win[:, 8192:8448]),
            wdt=np.ascontiguousarray(win[:, 8448 + 16 * hg:8448 + 16 * hg + 16]),
            cwx=np.ascontiguousarray(cw[1024 * hg:1024 * hg + 1024].reshape(8, 128, 4).transpose(1, 0, 2)),
            cbx=np.ascontiguousarray(cb[1024 * hg:1024 * hg + 1024].reshape(8, 128).T),
            cwbc=np.ascontiguousarray(cw[4096:4352].reshape(2, 128, 4).transpose(1, 0, 2)),
            cbbc=np.ascontiguousarray(cb[4096:4352].reshape(2, 128).T),
            dtbv=np.ascontiguousarray(inputs["L0_attn_dt_bias"][16 * hg:16 * hg + 16].reshape(16, 1)),
            alog=np.ascontiguousarray(inputs["L0_attn_A_log"][16 * hg:16 * hg + 16].reshape(16, 1)),
            dpar=np.ascontiguousarray(inputs["L0_attn_D_param"][16 * hg:16 * hg + 16].reshape(1, 16)),
            mneg=np.ascontiguousarray(mneg), oneh=oneh,
            identb=identb, identf=identf, cst=CST))
    return maps


def kernel(**inputs):
    r1 = run_bass_kernel_spmd(build_l0attn(), host_l0attn_inmaps(inputs),
                              core_ids=list(range(8))).results
    yf_full = np.concatenate(
        [np.concatenate([r1[b * 4 + hg]["yf_o"] for hg in range(4)], axis=1)
         for b in range(2)], axis=0)
    ssq_full = np.stack(
        [np.concatenate([r1[b * 4 + hg]["ssq_o"].T.reshape(-1) for b in range(2)])
         for hg in range(4)], axis=1)
    x_base = np.ascontiguousarray(inputs["L0_attn_u"], dtype=np.float32).reshape(8192, DM)
    r2 = run_bass_kernel_spmd(build_l0tail(),
                              host_l0tail_inmaps(inputs, yf_full, ssq_full, x_base),
                              core_ids=list(range(8))).results
    x3_full = np.concatenate([r2[c]["x3_o"] for c in range(8)], axis=0)
    xlnf_full = np.ascontiguousarray(
        np.concatenate([r2[c]["x3lnf_o"] for c in range(8)], axis=1))
    r3 = run_bass_kernel_spmd(build_l1pre(), host_l1pre_inmaps(inputs, xlnf_full),
                              core_ids=list(range(8))).results
    parts = [r3[ds]["part_o"] for ds in range(8)]
    hs_full = np.ascontiguousarray(
        np.concatenate([r3[ds]["hs_o"] for ds in range(8)], axis=0))
    r4 = run_bass_kernel_spmd(build_l1mid(), host_l1mid_inmaps(inputs, parts),
                              core_ids=list(range(8))).results
    dt_full = np.ascontiguousarray(
        np.concatenate([r4[c]["dt_o"] for c in range(8)], axis=1))
    bt = np.ascontiguousarray(np.concatenate([r4[c]["bt_o"] for c in range(8)], axis=1))
    ct = np.ascontiguousarray(np.concatenate([r4[c]["ct_o"] for c in range(8)], axis=1))
    r5 = run_bass_kernel_spmd(build_l1scan(),
                              host_l1scan_inmaps(inputs, hs_full, dt_full, bt, ct,
                                                 xlnf_full),
                              core_ids=list(range(8))).results
    ytil_full = np.ascontiguousarray(
        np.concatenate([r5[ds]["ytil_o"] for ds in range(8)], axis=0))
    r6 = run_bass_kernel_spmd(build_l1tail(),
                              host_l1tail_inmaps(inputs, ytil_full, x3_full),
                              core_ids=list(range(8))).results
    out = np.concatenate([r6[c]["xout"] for c in range(8)], axis=0)
    return out.reshape(2, 4096, DM).astype(np.float32)


# version 02
# speedup vs baseline: 1.0052x; 1.0052x over previous
import sys
sys.path.insert(0, '/opt/trn_rl_repo')
import contextlib
import numpy as np
import ml_dtypes
import concourse.bass as bass
import concourse.mybir as mybir
from concourse.alu_op_type import AluOpType as OP
from concourse.bass_utils import run_bass_kernel_spmd

F32 = mybir.dt.float32
BF16 = mybir.dt.bfloat16
AF = mybir.ActivationFunctionType
BF = ml_dtypes.bfloat16

B, L, DM = 2, 4096, 2048
TPC = 1024
NCORE = 8

ENGINES = ("tensor", "vector", "scalar", "gpsimd", "sync")


class SemC:
    def __init__(self, h):
        self.h = h
        self.n = 0


class Prog:
    def __init__(self, name="prog"):
        self.nc = bass.Bass("TRN2", target_bir_lowering=False)
        self.arena_cols = 93440
        self.q = {e: [] for e in ENGINES}
        self.stack = contextlib.ExitStack()
        self._semi = 0
        self._shared = {}
        self.proj_prev = None
        self._tp = None
        self.prefix = ""

    def shared(self, key, fn):
        if key not in self._shared:
            self._shared[key] = fn()
        return self._shared[key]

    def main_pool(self):
        return self.shared("mainpool", lambda: self.psum("mainpool", [128, 4, 512], F32))

    def tp_pool(self):
        return self.shared("tppool", lambda: self.psum("tppool", [128, 4, 1024], BF16))

    def get_tp(self):
        if self._tp is None:
            self._tp = Transposer(self, "g")
        return self._tp

    def wbuf(self):
        return self.shared("wbuf", lambda: self._sbuf_raw("wbuf", [128, 2, 45, 128], BF16))

    def _sbuf_raw(self, name, shape, dtype):
        return self.stack.enter_context(self.nc.sbuf_tensor(name, list(shape), dtype))

    def arena(self):
        return self.shared("bigAR", lambda: self._sbuf_raw("bigAR", [128, self.arena_cols], BF16))

    def statb(self):
        return self.shared("statb", lambda: self._sbuf_raw("statb", [128, 256], F32))

    def sqscrb(self):
        return self.shared("sqscrb", lambda: self._sbuf_raw("sqscrb", [128, 2048], F32))

    def idbuf(self):
        return self.shared("idbuf", lambda: self._sbuf_raw("idbuf", [128, 128], BF16))

    def idfbuf(self):
        return self.shared("idfbuf", lambda: self._sbuf_raw("idfbuf", [128, 128], F32))

    def cstbuf(self):
        return self.shared("cstbuf", lambda: self._sbuf_raw("cstbuf", [128, 8], F32))

    def gscrbuf(self):
        return self.shared("gscrbuf", lambda: self._sbuf_raw("gscrbuf", [128, 2, 512], BF16))

    def sem(self, name=None):
        nm = self.prefix + (name or f"s{self._semi}")
        h = self.stack.enter_context(self.nc.semaphore(nm))
        self._semi += 1
        return SemC(h)

    def sbuf(self, name, shape, dtype):
        return self.stack.enter_context(
            self.nc.sbuf_tensor(self.prefix + name, list(shape), dtype))

    def psum(self, name, shape, dtype=F32):
        return self.stack.enter_context(
            self.nc.psum_tensor(self.prefix + name, list(shape), dtype))

    def dram(self, name, shape, dtype, kind=None):
        if kind:
            return self.nc.dram_tensor(self.prefix + name, list(shape), dtype, kind=kind)
        return self.nc.dram_tensor(self.prefix + name, list(shape), dtype)

    def emit(self, eng, fn):
        self.q[eng].append(fn)

    def finish(self):
        with self.nc.Block() as block:
            for e in ENGINES:
                fns = self.q[e]
                if not fns:
                    continue
                def runner(engine, fns=fns):
                    for f in fns:
                        f(engine)
                getattr(block, e)(runner)
        self.stack.close()
        return self.nc


def dma(p, qeng, out, in_, sem):
    sem.n += 16
    def f(e, out=out, in_=in_, s=sem.h):
        e.dma_start(out=out, in_=in_).then_inc(s, 16)
    p.emit(qeng, f)
    return sem.n


def wait(p, eng, sem, val=None):
    v = sem.n if val is None else val
    if v > 0:
        p.emit(eng, lambda e, s=sem.h, v=v: e.wait_ge(s, v))


def inc_last(sem, k=1):
    sem.n += k
    return sem.n


class Transposer:
    def __init__(self, p, name, dtype=BF16, nslots=4):
        self.p = p
        self.pool = p.tp_pool()
        self.s_mm = p.sem()
        self.s_ev = p.sem()
        self.nslots = nslots
        self.i = 0

    def emit(self, src_ap, dst_ap, ident, evac_eng="scalar", func=None, scale=None):
        p = self.p
        slot = self.i % self.nslots
        need = self.i - self.nslots + 1
        mmv = inc_last(self.s_mm)
        def ft(e, slot=slot, src=src_ap, need=need):
            if need > 0:
                e.wait_ge(self.s_ev.h, need)
            e.transpose(self.pool[:, slot, 0:128], src, ident).then_inc(self.s_mm.h, 1)
        p.emit("tensor", ft)
        inc_last(self.s_ev)
        def fe(e, slot=slot, dst=dst_ap, mmv=mmv, func=func, scale=scale):
            e.wait_ge(self.s_mm.h, mmv)
            if func is not None:
                kw = {}
                if scale is not None:
                    kw["scale"] = scale
                e.activation(out=dst, in_=self.pool[:, slot, 0:128], func=func,
                             **kw).then_inc(self.s_ev.h, 1)
            else:
                e.copy(out=dst, in_=self.pool[:, slot, 0:128]).then_inc(self.s_ev.h, 1)
        p.emit(evac_eng, fe)
        self.i += 1

    def drain(self, engines):
        for e in engines:
            wait(self.p, e, self.s_ev)


def transpose_mat(p, tp, src_tile_fn, dst_tile_fn, R, C, ident, evac_eng="scalar",
                  func=None, scale_fn=None):
    for rc in range(R // 128):
        for cc in range(C // 128):
            sc = scale_fn(rc, cc) if scale_fn else None
            tp.emit(src_tile_fn(rc, cc), dst_tile_fn(rc, cc), ident,
                    evac_eng=evac_eng, func=func, scale=sc)


def proj(p, name, K, M, T, w_dram, x_ap, epilog, epilog_eng="scalar",
         nbanks=4, tile_t=512, wq="gpsimd", nwbuf=2, w_pre=None, wbuf_ap=None):
    Kt, Mt = K // 128, M // 128
    ntc = (T + tile_t - 1) // tile_t
    ps = p.main_pool()
    if p.proj_prev is not None:
        wait(p, "tensor", p.proj_prev)
        wait(p, wq, p.proj_prev)
    s_mm = p.sem(f"{name}_mm")
    s_free = p.sem(f"{name}_free")
    s_w = [p.sem(f"{name}_w{i}") for i in range(nwbuf)]
    wb = wbuf_ap if wbuf_ap is not None else p.wbuf()
    assert wbuf_ap is not None or Kt <= 45
    for m in range(Mt):
        buf = m % nwbuf
        if m >= nwbuf:
            need = (m - nwbuf + 1) * ntc
            p.emit(wq, lambda e, t=need, s=s_mm.h: e.wait_ge(s, t))
        src = w_dram[:, m * 128:(m + 1) * 128].rearrange("(kt q) m -> q kt m", q=128)
        dma(p, wq, wb[:, buf, 0:Kt, :], src, s_w[buf])
    jobs = 0
    for m in range(Mt):
        buf = m % nwbuf
        wwait = 16 * (m // nwbuf + 1)
        for tc in range(ntc):
            tw = min(tile_t, T - tc * tile_t)
            bank = jobs % nbanks
            freewait = jobs - nbanks + 1
            mmv = inc_last(s_mm)
            def fmm(e, tw=tw, bank=bank, buf=buf, wwait=wwait, freewait=freewait,
                    tc=tc):
                e.wait_ge(s_w[buf].h, wwait)
                if freewait > 0:
                    e.wait_ge(s_free.h, freewait)
                inst = None
                for k in range(Kt):
                    inst = e.matmul(ps[:, bank, 0:tw], wb[:, buf, k, :],
                                    x_ap(k)[:, tc * tile_t:tc * tile_t + tw],
                                    start=(k == 0), stop=(k == Kt - 1))
                inst.then_inc(s_mm.h, 1)
            p.emit("tensor", fmm)
            inc_last(s_free)
            def fep(e, m=m, tc=tc, tw=tw, bank=bank, mmv=mmv):
                e.wait_ge(s_mm.h, mmv)
                last = epilog(e, m, tc, ps[:, bank, 0:tw])
                last.then_inc(s_free.h, 1)
            p.emit(epilog_eng, fep)
            jobs += 1
    p.proj_prev = s_free
    return s_free


def barrier(p, engines):
    ss = {e: p.sem() for e in engines}
    for e in engines:
        ss[e].n += 1
        p.emit(e, lambda en, s=ss[e].h: en.sem_inc(s, 1))
    for e in engines:
        for e2 in engines:
            if e2 != e:
                wait(p, e, ss[e2])


def ln_tile(p, x_ap, xln_ap, mv_ap, bn_ap, rstd_ap, nf, eps=1e-5, out2=None):
    def fv(e):
        nch = (nf + 511) // 512
        for c in range(nch):
            w = min(512, nf - c * 512)
            e.bn_stats(out=bn_ap[:, c * 6:(c + 1) * 6], in_=x_ap[:, c * 512:c * 512 + w])
        e.bn_aggr(out=mv_ap, in_=bn_ap[:, 0:6 * nch])
    p.emit("vector", fv)
    def fs(e):
        e.activation(out=rstd_ap, in_=mv_ap[:, 1:2], func=AF.Ln, bias=float(eps))
        e.activation(out=rstd_ap, in_=rstd_ap, func=AF.Exp, scale=-0.5)
    p.emit("scalar", fs)
    def fv2(e):
        e.tensor_scalar(out=xln_ap, in0=x_ap, scalar1=mv_ap[:, 0:1], scalar2=rstd_ap,
                        op0=OP.subtract, op1=OP.mult)
        if out2 is not None:
            e.tensor_scalar(out=out2, in0=x_ap, scalar1=mv_ap[:, 0:1], scalar2=rstd_ap,
                            op0=OP.subtract, op1=OP.mult)
    p.emit("vector", fv2)



def fence_all(p, scrv, scrs, scrg):
    fs = {e: p.sem("fence_" + e) for e in ("tensor", "vector", "scalar", "gpsimd")}
    for e, s in fs.items():
        inc_last(s)
    p.emit("tensor", lambda e, s=fs["tensor"].h: e.nop().then_inc(s, 1))
    p.emit("vector", lambda e, s=fs["vector"].h: e.memset(scrv, 0.0).then_inc(s, 1))
    p.emit("scalar", lambda e, s=fs["scalar"].h:
           e.activation(out=scrs, in_=scrs, func=AF.Copy).then_inc(s, 1))
    p.emit("gpsimd", lambda e, s=fs["gpsimd"].h: e.memset(scrg, 0.0).then_inc(s, 1))
    for e in ENGINES:
        for e2, s in fs.items():
            if e2 != e:
                wait(p, e, s)


def eng_fence(p, eng, sem, scr):
    inc_last(sem)
    if eng == "vector":
        p.emit(eng, lambda e, s=sem.h: e.memset(scr, 0.0).then_inc(s, 1))
    elif eng == "scalar":
        p.emit(eng, lambda e, s=sem.h: e.activation(out=scr, in_=scr, func=AF.Copy).then_inc(s, 1))
    elif eng == "gpsimd":
        p.emit(eng, lambda e, s=sem.h: e.memset(scr, 0.0).then_inc(s, 1))
    else:
        raise ValueError(eng)


def emit_ln8(p, src_ap_fn, stat, s_ph, s_sc, out_cb, eps=1e-5, nf=2048, ntiles=8,
             sqscr=None):
    inv = 1.0 / nf
    cols = {}
    for i in range(ntiles):
        c0 = 160 + (i % 8) * 6
        cols[i] = tuple(stat[:, c0 + j:c0 + j + 1] for j in range(6))
    def fs(e):
        def spacer():
            if ntiles < 4:
                e.activation(out=sqscr[:, 0:512], in_=sqscr[:, 0:512], func=AF.Square)
        for i in range(ntiles):
            sm, mean, ssq, m2, em, rstd = cols[i]
            e.activation(out=sqscr, in_=src_ap_fn(i), func=AF.Copy, accum_out=sm)
            e.activation(out=sqscr, in_=src_ap_fn(i), func=AF.Square, accum_out=ssq)
        spacer()
        for i in range(ntiles):
            sm, mean, ssq, m2, em, rstd = cols[i]
            e.activation(out=mean, in_=sm, func=AF.Copy, scale=inv)
        spacer()
        for i in range(ntiles):
            sm, mean, ssq, m2, em, rstd = cols[i]
            e.activation(out=m2, in_=mean, func=AF.Square)
        spacer()
        for i in range(ntiles):
            sm, mean, ssq, m2, em, rstd = cols[i]
            e.activation(out=em, in_=m2, func=AF.Copy, scale=-1.0, bias=float(eps))
        spacer()
        for i in range(ntiles):
            sm, mean, ssq, m2, em, rstd = cols[i]
            e.activation(out=rstd, in_=ssq, func=AF.Ln, scale=inv, bias=em)
        spacer()
        inst = None
        for i in range(ntiles):
            sm, mean, ssq, m2, em, rstd = cols[i]
            inst = e.activation(out=rstd, in_=rstd, func=AF.Exp, scale=-0.5)
        inst.then_inc(s_sc.h, 1)
    p.emit("scalar", fs)
    scv = inc_last(s_sc)
    for i in range(ntiles):
        sm, mean, ssq, m2, em, rstd = cols[i]
        def fv3(e, i=i, mean=mean, rstd=rstd, scv=scv):
            e.wait_ge(s_sc.h, scv)
            out_cb(e, i, mean, rstd)
        p.emit("vector", fv3)


def emit_ln_transpose(p, tp, ntiles, src_fn, scr, dstfm, idb, stat, s_ph, s_sc,
                      s_ln, eps=1e-5, nf=2048, sqscr=None):
    nct = nf // 128
    slot_state = {"done": {}}
    def cb(e, i, mean, rstd):
        prev_mm = slot_state["done"].get(i - 2, 0)
        if prev_mm > 0:
            e.wait_ge(tp.s_mm.h, prev_mm)
        e.tensor_scalar(out=scr[:, i % 2, :], in0=src_fn(i)[:, :], scalar1=mean,
                        scalar2=rstd, op0=OP.subtract, op1=OP.mult).then_inc(s_ln.h, 1)
    emit_ln8(p, src_fn, stat, s_ph, s_sc, cb, eps, nf=nf, ntiles=ntiles, sqscr=sqscr)
    for i in range(ntiles):
        inc_last(s_ln)
        wait(p, "tensor", s_ln, s_ln.n)
        for ct in range(nct):
            tp.emit(scr[:, i % 2, ct * 128:(ct + 1) * 128],
                    dstfm[:, ct, i * 128:(i + 1) * 128], idb[:, :], evac_eng="scalar")
        slot_state["done"][i] = tp.s_mm.n



def build_l0tail(P=None, fused=False, G=None, core=None):
    p = P if P is not None else Prog("l0tail")
    if P is not None:
        p.prefix = "t0_"
    else:
        p.arena_cols = 81920
    d = {}
    inlist = [("wout", [4096, DM], F32),
                     ("normw", [128, 32], F32), ("gwt", [DM, 5632], F32),
                     ("uwt", [DM, 5632], F32), ("dwt", [5632, DM], F32),
                     ("gs", [128, 44], F32), ("us", [128, 44], F32),
                     ("ds", [128, 16], F32), ("identb", [128, 128], BF16),
                     ("cst", [128, 8], F32), ("x_in", [TPC, DM], F32)]
    if not fused:
        inlist += [("yf_in", [TPC, 4096], BF16), ("ssq_in", [TPC, 4], F32)]
    for nm, shp, dt in inlist:
        d[nm] = p.dram(nm, shp, dt, "ExternalInput").ap()
    okind = None if fused else "ExternalOutput"
    x3_h = p.dram("x3_o", [TPC, DM], F32, okind)
    x3_o = x3_h.ap()
    x3lnf_h = p.dram("x3lnf_o", [DM, TPC], BF16, okind)
    x3lnf_o = x3lnf_h.ap()
    x2_spill = p.dram("x2_spill", [TPC, DM], F32)

    AR = p.arena()[:, 0:81920]
    def Ab(off, n):
        return AR[:, off:off + n]
    def Af(off, n):
        return AR[:, off:off + 2 * n].bitcast(F32)

    idb = p.idbuf()
    small = p.sbuf("small", [128, 160], F32)
    NW, GS, US, DS = 0, 32, 76, 120
    stat = p.statb()
    sqscr = p.sqscrb()

    s_ld = p.sem(); s_ph = p.sem(); s_sc = p.sem(); s_v = p.sem(); s_out = p.sem()

    cst = p.cstbuf()
    dma(p, "sync", cst[:, :], d["cst"], s_ld)
    dma(p, "sync", idb[:, :], d["identb"], s_ld)
    dma(p, "sync", small[:, NW:NW + 32], d["normw"], s_ld)
    dma(p, "sync", small[:, GS:GS + 44], d["gs"], s_ld)
    dma(p, "sync", small[:, US:US + 44], d["us"], s_ld)
    dma(p, "sync", small[:, DS:DS + 16], d["ds"], s_ld)
    yf = Ab(0, 32768).rearrange("q (t f) -> q t f", t=8)
    if not fused:
        dma(p, "sync", yf, d["yf_in"].rearrange("(t q) f -> q t f", q=128), s_ld)
        dma(p, "sync", stat[:, 0:32].rearrange("q (t c) -> q t c", t=8),
            d["ssq_in"].rearrange("(t q) c -> q t c", q=128), s_ld)
    else:
        ygat = G["ygat"].ap()
        sgat = G["sgat"].ap()
        wait(p, "sync", G["cc_yf"])
        for hg in range(4):
            dma(p, "sync", yf[:, :, hg * 1024:(hg + 1) * 1024],
                ygat[hg * 1024:(hg + 1) * 1024, :]
                .rearrange("(t q) f -> q t f", q=128), s_ld)
        for hg in range(4):
            dma(p, "sync", stat[:, 32 + 8 * hg:32 + 8 * hg + 8],
                sgat[hg * 128:(hg + 1) * 128, 0:8], s_ld)

    wait(p, "vector", s_ld)
    wait(p, "scalar", s_ld)
    def f_red(e):
        inst = None
        for i in range(8):
            if not fused:
                inst = e.reduce_sum(out=stat[:, 40 + i:41 + i],
                                    in_=stat[:, 4 * i:4 * i + 4],
                                    axis=mybir.AxisListType.X)
            else:
                inst = e.reduce_sum(out=stat[:, 40 + i:41 + i],
                                    in_=stat[:, 32:64].rearrange("q (g t) -> q t g", g=4)[:, i, :],
                                    axis=mybir.AxisListType.X)
        inst.then_inc(s_ph.h, 1)
    p.emit("vector", f_red)
    ph1 = inc_last(s_ph)
    def f_rr(e):
        e.wait_ge(s_ph.h, ph1)
        e.activation(out=stat[:, 56:64], in_=stat[:, 40:48], func=AF.Ln,
                     scale=1.0 / 4096.0, bias=cst[:, 0:1])
        e.activation(out=stat[:, 56:64], in_=stat[:, 56:64], func=AF.Exp,
                     scale=-0.5).then_inc(s_sc.h, 1)
    p.emit("scalar", f_rr)
    sc1 = inc_last(s_sc)
    def f_yhat(e):
        e.wait_ge(s_sc.h, sc1)
        for i in range(8):
            inst = e.tensor_scalar(out=yf[:, i, :], in0=yf[:, i, :],
                                   scalar1=stat[:, 56 + i:57 + i], scalar2=None,
                                   op0=OP.mult)
        inst.then_inc(s_v.h, 1)
    p.emit("vector", f_yhat)
    v1 = inc_last(s_v)

    yfm = Ab(32768, 32768).rearrange("q (f t) -> q f t", f=32)
    tp = p.get_tp()
    wait(p, "tensor", s_v, v1)
    wait(p, "scalar", s_ld)
    for rt in range(8):
        for ct in range(32):
            tp.emit(yf[:, rt, ct * 128:(ct + 1) * 128],
                    yfm[:, ct, rt * 128:(rt + 1) * 128], idb[:, :],
                    evac_eng="scalar", func=AF.Copy,
                    scale=small[:, NW + ct:NW + ct + 1])
    tp.drain(["tensor"])

    o1 = Ab(0, 16384).rearrange("q (f t) -> q f t", f=16)
    def ep_o1(e, m, tc, ps_ap):
        return e.copy(out=o1[:, m, tc * 512:tc * 512 + ps_ap.shape[-1]], in_=ps_ap)
    pr1 = proj(p, "wout", 4096, DM, TPC, d["wout"], lambda k: yfm[:, k, :], ep_o1)

    o1tm = Ab(16384, 16384).rearrange("q (t f) -> q t f", t=8)
    tp2 = p.get_tp()
    wait(p, "tensor", pr1)
    for rt in range(16):
        for ct in range(8):
            tp2.emit(o1[:, rt, ct * 128:(ct + 1) * 128],
                     o1tm[:, ct, rt * 128:(rt + 1) * 128], idb[:, :], evac_eng="scalar")
    xb = Af(32768, 16384).rearrange("q (t f) -> q t f", t=8)
    wait(p, "sync", pr1)
    dma(p, "sync", xb, d["x_in"].rearrange("(t q) f -> q t f", q=128), s_ld)
    wait(p, "vector", s_ld)
    tp2.drain(["vector"])

    lnscr = Ab(0, 4096).rearrange("q (a f) -> q a f", a=2)
    x2lnfm = Ab(4096, 16384).rearrange("q (f t) -> q f t", f=16)
    tp3 = p.get_tp()
    s_res = p.sem()
    def cb_res(e, i, mean, rstd):
        e.tensor_scalar(out=lnscr[:, i % 2, :], in0=o1tm[:, i, :], scalar1=mean,
                        scalar2=rstd, op0=OP.subtract, op1=OP.mult)
        e.tensor_tensor(out=xb[:, i, :], in0=xb[:, i, :], in1=lnscr[:, i % 2, :],
                        op=OP.add).then_inc(s_res.h, 1)
        inc_last(s_res, 0)
    emit_ln8(p, lambda i: o1tm[:, i, :], stat, s_ph, s_sc, cb_res, 1e-5, sqscr=sqscr[:, :])
    s_res.n += 8
    s_ln2 = p.sem()
    wait(p, "scalar", s_res)
    emit_ln_transpose(p, tp3, 8, lambda i: xb[:, i, :], lnscr, x2lnfm, idb,
                      stat, s_ph, s_sc, s_ln2, 1e-5, sqscr=sqscr[:, :])
    wait(p, "gpsimd", s_res)
    s_spill = p.sem()
    dma(p, "gpsimd", x2_spill.ap().rearrange("(t q) f -> q t f", q=128), xb, s_spill)
    tp3.drain(["tensor"])

    hdn = Ab(20480, 45056).rearrange("q (f t) -> q f t", f=44)
    gscr = p.gscrbuf()
    wait(p, "scalar", s_spill)
    wait(p, "vector", s_spill)
    def ffs(e, m, tc, ps1):
        return e.activation(out=gscr[:, (2 * m + tc) % 2, 0:ps1.shape[-1]], in_=ps1,
                            func=AF.Silu, scale=small[:, GS + m:GS + m + 1])
    def ffv(e, m, tc, ps1, ps2):
        tw = ps2.shape[-1]
        e.tensor_scalar(out=hdn[:, m, tc * 512:tc * 512 + tw], in0=ps2,
                        scalar1=small[:, US + m:US + m + 1], scalar2=None, op0=OP.mult)
        return e.tensor_tensor(out=hdn[:, m, tc * 512:tc * 512 + tw],
                               in0=hdn[:, m, tc * 512:tc * 512 + tw],
                               in1=gscr[:, (2 * m + tc) % 2, 0:tw], op=OP.mult)
    proj2(p, "gu", DM, 5632, TPC, d["gwt"], d["uwt"], lambda k: x2lnfm[:, k, :],
          (ffs, ffv))

    o2 = Ab(65536, 16384).rearrange("q (f t) -> q f t", f=16)
    def ep_o2(e, m, tc, ps_ap):
        return e.activation(out=o2[:, m, tc * 512:tc * 512 + ps_ap.shape[-1]],
                            in_=ps_ap, func=AF.Copy, scale=small[:, DS + m:DS + m + 1])
    pr3 = proj(p, "down", 5632, DM, TPC, d["dwt"], lambda k: hdn[:, k, :], ep_o2)

    o2tm = Ab(0, 16384).rearrange("q (t f) -> q t f", t=8)
    tp4 = p.get_tp()
    wait(p, "tensor", pr3)
    for rt in range(16):
        for ct in range(8):
            tp4.emit(o2[:, rt, ct * 128:(ct + 1) * 128],
                     o2tm[:, ct, rt * 128:(rt + 1) * 128], idb[:, :], evac_eng="scalar")
    xb2 = Af(20480, 16384).rearrange("q (t f) -> q t f", t=8)
    wait(p, "sync", pr3)
    wait(p, "sync", s_spill)
    dma(p, "sync", xb2, x2_spill.ap().rearrange("(t q) f -> q t f", q=128), s_ld)
    wait(p, "vector", s_ld)
    tp4.drain(["vector"])
    lnscr2 = Ab(16384, 4096).rearrange("q (a f) -> q a f", a=2)
    x3lnfm = Ab(53248, 16384).rearrange("q (f t) -> q f t", f=16)
    tp5 = p.get_tp()
    s_res2 = p.sem(); s_ln3 = p.sem()
    def cb_res2(e, i, mean, rstd):
        e.tensor_scalar(out=lnscr2[:, i % 2, :], in0=o2tm[:, i, :], scalar1=mean,
                        scalar2=rstd, op0=OP.subtract, op1=OP.mult)
        e.tensor_tensor(out=xb2[:, i, :], in0=xb2[:, i, :], in1=lnscr2[:, i % 2, :],
                        op=OP.add).then_inc(s_res2.h, 1)
    emit_ln8(p, lambda i: o2tm[:, i, :], stat, s_ph, s_sc, cb_res2, 1e-5, sqscr=sqscr[:, :])
    s_res2.n += 8
    wait(p, "scalar", s_res2)
    for i in range(8):
        def fdma(e, i=i):
            e.wait_ge(s_res2.h, i + 1)
            e.dma_start(out=x3_o.rearrange("(t q) f -> q t f", q=128)[:, i, :],
                        in_=xb2[:, i, :]).then_inc(s_out.h, 16)
        p.emit("sync", fdma)
        s_out.n += 16
    emit_ln_transpose(p, tp5, 8, lambda i: xb2[:, i, :], lnscr2, x3lnfm, idb,
                      stat, s_ph, s_sc, s_ln3, 1e-5, sqscr=sqscr[:, :])
    tp5.drain(["sync"])
    dma(p, "sync", x3lnf_o.rearrange("(ft q) t -> q ft t", q=128),
        x3lnfm[:, :, :], s_out)
    wait(p, "sync", s_out)
    return p.finish()


CST = np.tile(np.array([1e-5, 1e-6, 1.0, 0.0, 0.5, 2.0, 0.0, 0.0],
                        dtype=np.float32), (128, 1))


def host_l0tail_inmaps(inputs, yf_full, ssq_full, x_base):
    identb = np.eye(128, dtype=BF)
    wout = np.ascontiguousarray(inputs["L0_attn_W_out"], dtype=np.float32)
    normw = np.ascontiguousarray(inputs["L0_attn_norm_w"].reshape(32, 128).T)
    gwt = np.ascontiguousarray(inputs["L0_ffn_gate_w"].T)
    uwt = np.ascontiguousarray(inputs["L0_ffn_up_w"].T)
    dwt = np.ascontiguousarray(inputs["L0_ffn_down_w"].T)
    gs = np.ascontiguousarray(inputs["L0_ffn_gate_s"].reshape(44, 128).T)
    us = np.ascontiguousarray(inputs["L0_ffn_up_s"].reshape(44, 128).T)
    ds = np.ascontiguousarray(inputs["L0_ffn_down_s"].reshape(16, 128).T)
    yfr = yf_full.reshape(NCORE, TPC, 4096)
    sqr = ssq_full.reshape(NCORE, TPC, 4)
    xr = x_base.reshape(NCORE, TPC, DM)
    maps = []
    for c in range(NCORE):
        maps.append(dict(x_in=np.ascontiguousarray(xr[c]),
                         yf_in=np.ascontiguousarray(yfr[c]),
                         ssq_in=np.ascontiguousarray(sqr[c]),
                         wout=wout, normw=normw, gwt=gwt, uwt=uwt, dwt=dwt,
                         gs=gs, us=us, ds=ds, identb=identb, cst=CST))
    return maps


def proj2(p, name, K, M, T, w1_dram, w2_dram, x_ap, epilog,
          tile_t=512, wq="gpsimd", nwbuf=2):
    Kt, Mt = K // 128, M // 128
    ntc = (T + tile_t - 1) // tile_t
    ps = p.main_pool()
    if p.proj_prev is not None:
        wait(p, "tensor", p.proj_prev)
        wait(p, wq, p.proj_prev)
    s_mm = p.sem(f"{name}_mm"); s_free = p.sem(f"{name}_free"); s_scd = p.sem(f"{name}_scd")
    s_w = [p.sem(f"{name}_w{i}") for i in range(nwbuf)]
    wb = p.wbuf()
    assert 2 * Kt <= 45
    for m in range(Mt):
        buf = m % nwbuf
        if m >= nwbuf:
            need = (m - nwbuf + 1) * ntc
            p.emit(wq, lambda e, t=need, s=s_mm.h: e.wait_ge(s, t))
        for wi, wd in enumerate((w1_dram, w2_dram)):
            src = wd[:, m * 128:(m + 1) * 128].rearrange("(kt q) m -> q kt m", q=128)
            dma(p, wq, wb[:, buf, wi * Kt:(wi + 1) * Kt, :], src, s_w[buf])
    jobs = 0
    fs_cb, fv_cb = epilog
    for m in range(Mt):
        buf = m % nwbuf
        wwait = 32 * (m // nwbuf + 1)
        for tc in range(ntc):
            tw = min(tile_t, T - tc * tile_t)
            pair = jobs % 2
            freewait = jobs - 2 + 1
            mmv = inc_last(s_mm)
            def fmm(e, tw=tw, pair=pair, buf=buf, wwait=wwait, freewait=freewait, tc=tc):
                e.wait_ge(s_w[buf].h, wwait)
                if freewait > 0:
                    e.wait_ge(s_free.h, freewait)
                inst = None
                for wi in range(2):
                    for k in range(Kt):
                        inst = e.matmul(ps[:, 2 * pair + wi, 0:tw], wb[:, buf, wi * Kt + k, :],
                                        x_ap(k)[:, tc * tile_t:tc * tile_t + tw],
                                        start=(k == 0), stop=(k == Kt - 1))
                inst.then_inc(s_mm.h, 1)
            p.emit("tensor", fmm)
            scv = inc_last(s_scd)
            gfree = jobs - 1
            def ffs(e, m=m, tc=tc, tw=tw, pair=pair, mmv=mmv, gfree=gfree):
                e.wait_ge(s_mm.h, mmv)
                if gfree > 0:
                    e.wait_ge(s_free.h, gfree)
                last = fs_cb(e, m, tc, ps[:, 2 * pair, 0:tw])
                last.then_inc(s_scd.h, 1)
            p.emit("scalar", ffs)
            inc_last(s_free)
            def ffv(e, m=m, tc=tc, tw=tw, pair=pair, scv=scv):
                e.wait_ge(s_scd.h, scv)
                last = fv_cb(e, m, tc, ps[:, 2 * pair, 0:tw], ps[:, 2 * pair + 1, 0:tw])
                last.then_inc(s_free.h, 1)
            p.emit("vector", ffv)
            jobs += 1
    p.proj_prev = s_free
    return s_free


def build_l1tail(P=None, fused=False, G=None):
    p = P if P is not None else Prog("l1tail")
    if P is not None:
        p.prefix = "t1_"
    else:
        p.arena_cols = 81920
    d = {}
    inlist = [("wout1", [4096, DM], F32), ("w1t", [DM, 8192], F32),
              ("w3t", [DM, 8192], F32), ("w2t", [8192, DM], F32),
              ("identb", [128, 128], BF16), ("h128", [128, 128], BF16)]
    if not fused:
        inlist += [("x3_in", [TPC, DM], F32), ("ytil_in", [4096, TPC], BF16)]
    for nm, shp, dt in inlist:
        d[nm] = p.dram(nm, shp, dt, "ExternalInput").ap()
    if fused:
        d["x3_in"] = G["x3_h"].ap()
        d["ytil_in"] = G["ytex"].ap()
    xout = p.dram("xout", [TPC, DM], F32, "ExternalOutput").ap()
    x4_spill = p.dram("x4_spill", [TPC, DM], F32)

    AR = p.arena()[:, 0:81920]
    def Ab(off, n):
        return AR[:, off:off + n]
    def Af(off, n):
        return AR[:, off:off + 2 * n].bitcast(F32)
    idb = p.idbuf()
    h128 = p.sbuf("h128s", [128, 128], BF16)
    stat = p.statb()
    sqscr = p.sqscrb()
    spare = p.sbuf("spare", [128, 512], BF16)
    gscr = p.gscrbuf()

    s_ld = p.sem(); s_ph = p.sem(); s_sc = p.sem(); s_out = p.sem()

    dma(p, "sync", idb[:, :], d["identb"], s_ld)
    dma(p, "sync", h128[:, :], d["h128"], s_ld)
    ytil = Ab(0, 32768).rearrange("q (f t) -> q f t", f=32)
    dma(p, "sync", ytil, d["ytil_in"].rearrange("(f q) t -> q f t", q=128), s_ld)
    wait(p, "tensor", s_ld)

    o1 = Ab(49152, 16384).rearrange("q (f t) -> q f t", f=16)
    def ep_o1(e, m, tc, ps_ap):
        return e.copy(out=o1[:, m, tc * 512:tc * 512 + ps_ap.shape[-1]], in_=ps_ap)
    pr1 = proj(p, "wo1", 4096, DM, TPC, d["wout1"], lambda k: ytil[:, k, :], ep_o1)

    o1tm = Ab(65536, 16384).rearrange("q (t f) -> q t f", t=8)
    tp = p.get_tp()
    wait(p, "tensor", pr1)
    for rt in range(16):
        for ct in range(8):
            tp.emit(o1[:, rt, ct * 128:(ct + 1) * 128],
                    o1tm[:, ct, rt * 128:(rt + 1) * 128], idb[:, :], evac_eng="scalar")
    xb = Af(0, 16384).rearrange("q (t f) -> q t f", t=8)
    wait(p, "sync", pr1)
    dma(p, "sync", xb, d["x3_in"].rearrange("(t q) f -> q t f", q=128), s_ld)
    wait(p, "vector", s_ld)
    wait(p, "scalar", s_ld)
    tp.drain(["vector", "scalar"])
    lnscr = Ab(49152, 4096).rearrange("q (a f) -> q a f", a=2)
    x4lnfm = Ab(32768, 16384).rearrange("q (f t) -> q f t", f=16)
    s_res = p.sem()
    def cb_res(e, i, mean, rstd):
        e.tensor_scalar(out=lnscr[:, i % 2, :], in0=o1tm[:, i, :], scalar1=mean,
                        scalar2=rstd, op0=OP.subtract, op1=OP.mult)
        e.tensor_tensor(out=xb[:, i, :], in0=xb[:, i, :], in1=lnscr[:, i % 2, :],
                        op=OP.add).then_inc(s_res.h, 1)
    emit_ln8(p, lambda i: o1tm[:, i, :], stat, s_ph, s_sc, cb_res, 1e-5, sqscr=sqscr[:, :])
    s_res.n += 8
    wait(p, "scalar", s_res)
    s_ln2 = p.sem()
    emit_ln_transpose(p, tp, 8, lambda i: xb[:, i, :], lnscr, x4lnfm, idb,
                      stat, s_ph, s_sc, s_ln2, 1e-5, sqscr=sqscr[:, :])
    wait(p, "gpsimd", s_res)
    s_spill = p.sem()
    dma(p, "gpsimd", x4_spill.ap().rearrange("(t q) f -> q t f", q=128), xb, s_spill)
    tp.drain(["tensor"])

    o2 = Ab(49152, 16384).rearrange("q (f t) -> q f t", f=16)
    wb2 = Ab(65536, 16384).rearrange("q (b k m) -> q b k m", b=2, k=64)
    s_h128 = p.sem("s_h128")
    s_bfly = p.sem("s_bfly")
    wait(p, "scalar", s_spill)
    wait(p, "vector", s_spill)
    for half in range(2):
        t0c = half * 512
        hdn = Ab(0, 32768).rearrange("q (f t) -> q f t", f=64)
        def ffs(e, m, tc, ps1):
            return e.activation(out=gscr[:, (2 * m + tc) % 2, 0:ps1.shape[-1]],
                                in_=ps1, func=AF.Silu)
        def ffv(e, m, tc, ps1, ps2, hdn=hdn):
            tw = ps2.shape[-1]
            return e.tensor_tensor(out=hdn[:, m, 0:tw], in0=ps2,
                                   in1=gscr[:, (2 * m + tc) % 2, 0:tw], op=OP.mult)
        proj2(p, f"gu{half}", DM, 8192, 512, d["w1t"][:, :], d["w3t"][:, :],
              lambda k, t0c=t0c: x4lnfm[:, k, t0c:t0c + 512], (ffs, ffv), tile_t=512)
        ps = p.main_pool()
        wait(p, "tensor", p.proj_prev)
        s_mm1 = p.sem(f"h128mm{half}"); s_ev1 = p.sem(f"h128ev{half}")
        for a in range(64):
            bank = a % 4
            mmv = inc_last(s_mm1)
            need = a - 4 + 1
            def fmm(e, a=a, bank=bank, need=need, s_ev1=s_ev1, s_mm1=s_mm1, hdn=hdn):
                if need > 0:
                    e.wait_ge(s_ev1.h, need)
                e.matmul(ps[:, bank, 0:512], h128[:, :], hdn[:, a, :],
                         start=True, stop=True).then_inc(s_mm1.h, 1)
            p.emit("tensor", fmm)
            inc_last(s_ev1)
            def fev(e, a=a, bank=bank, mmv=mmv, s_ev1=s_ev1, s_mm1=s_mm1, hdn=hdn):
                e.wait_ge(s_mm1.h, mmv)
                e.activation(out=hdn[:, a, :], in_=ps[:, bank, 0:512], func=AF.Copy,
                             scale=1.4142135623730951).then_inc(s_ev1.h, 1)
            p.emit("scalar", fev)
        p.proj_prev = s_ev1
        pos = list(range(64))
        spare_slot = 64
        def slot_ap(s, hdn=hdn):
            if s == 64:
                return spare[:, :]
            return hdn[:, s, :]
        wait(p, "vector", s_ev1)
        nbf = 0
        for st in range(6):
            h = 1 << st
            for base in range(0, 64, 2 * h):
                for off in range(h):
                    i, j = base + off, base + off + h
                    A, Bs = pos[i], pos[j]
                    e_diff = spare_slot
                    def fb(e, A=A, Bs=Bs, e_diff=e_diff, hdn=hdn):
                        e.tensor_tensor(out=slot_ap(e_diff), in0=slot_ap(A),
                                        in1=slot_ap(Bs), op=OP.subtract)
                        e.tensor_tensor(out=slot_ap(A), in0=slot_ap(A),
                                        in1=slot_ap(Bs), op=OP.add)
                    p.emit("vector", fb)
                    nbf += 1
                    pos[j] = e_diff
                    spare_slot = Bs
        def fbl(e):
            e.memset(spare[:, 0:1], 0.0)
        inc_last(s_bfly)
        def fbl2(e):
            e.memset(stat[:, 255:256], 0.0)
            e.tensor_scalar(out=stat[:, 255:256], in0=stat[:, 255:256], scalar1=1.0,
                            scalar2=None, op0=OP.mult).then_inc(s_bfly.h, 1)
        p.emit("vector", fbl2)
        def ep_o2(e, m, tc, ps_ap, t0c=t0c):
            return e.copy(out=o2[:, m, t0c:t0c + ps_ap.shape[-1]], in_=ps_ap)
        wait(p, "tensor", s_bfly)
        pos_fixed = list(pos)
        proj(p, f"w2_{half}", 8192, DM, 512, d["w2t"],
             lambda k, pf=pos_fixed: slot_ap(pf[k]), ep_o2, wbuf_ap=wb2)

    o2tm = Ab(0, 16384).rearrange("q (t f) -> q t f", t=8)
    wait(p, "tensor", p.proj_prev)
    for rt in range(16):
        for ct in range(8):
            tp.emit(o2[:, rt, ct * 128:(ct + 1) * 128],
                    o2tm[:, ct, rt * 128:(rt + 1) * 128], idb[:, :], evac_eng="scalar")
    xb2 = Af(16384, 16384).rearrange("q (t f) -> q t f", t=8)
    wait(p, "sync", p.proj_prev)
    wait(p, "sync", s_spill)
    dma(p, "sync", xb2, x4_spill.ap().rearrange("(t q) f -> q t f", q=128), s_ld)
    wait(p, "vector", s_ld)
    wait(p, "scalar", s_ld)
    tp.drain(["vector", "scalar"])
    lnscr2 = Ab(49152, 4096).rearrange("q (a f) -> q a f", a=2)
    s_res2 = p.sem()
    def cb_res2(e, i, mean, rstd):
        e.tensor_scalar(out=lnscr2[:, i % 2, :], in0=o2tm[:, i, :], scalar1=mean,
                        scalar2=rstd, op0=OP.subtract, op1=OP.mult)
        e.tensor_tensor(out=xb2[:, i, :], in0=xb2[:, i, :], in1=lnscr2[:, i % 2, :],
                        op=OP.add).then_inc(s_res2.h, 1)
    emit_ln8(p, lambda i: o2tm[:, i, :], stat, s_ph, s_sc, cb_res2, 1e-5, sqscr=sqscr[:, :])
    s_res2.n += 8
    for i in range(8):
        def fdma(e, i=i):
            e.wait_ge(s_res2.h, i + 1)
            e.dma_start(out=xout.rearrange("(t q) f -> q t f", q=128)[:, i, :],
                        in_=xb2[:, i, :]).then_inc(s_out.h, 16)
        p.emit("sync", fdma)
        s_out.n += 16
    wait(p, "sync", s_out)
    if P is None:
        return p.finish()
    return None


def hadamard(n):
    h = np.array([[1.0]], dtype=np.float32)
    while h.shape[0] < n:
        h = np.block([[h, h], [h, -h]])
    return h


def host_l1tail_inmaps(inputs, ytil_full, x3_full):
    identb = np.eye(128, dtype=BF)
    h128 = (hadamard(128) * (2.0 ** -7)).astype(BF)
    wout1 = np.ascontiguousarray(inputs["L1_attn_out_proj_w"], dtype=np.float32)
    w1t = np.ascontiguousarray(inputs["L1_ffn_W1"].T)
    w3t = np.ascontiguousarray(inputs["L1_ffn_W3"].T)
    w2t = np.ascontiguousarray(inputs["L1_ffn_W2"].T)
    maps = []
    for c in range(NCORE):
        maps.append(dict(
            x3_in=np.ascontiguousarray(x3_full[c * TPC:(c + 1) * TPC]),
            ytil_in=np.ascontiguousarray(ytil_full[:, c * TPC:(c + 1) * TPC]),
            wout1=wout1, w1t=w1t, w3t=w3t, w2t=w2t, identb=identb, h128=h128))
    return maps


def build_l1pre(P=None, fused=False, G=None):
    p = P if P is not None else Prog("l1pre")
    if P is not None:
        p.prefix = "p1_"
    d = {}
    inlist = [("wh", [DM, 512], F32), ("cw", [128, 4, 4], F32),
              ("cb", [128, 4], F32), ("wxp", [512, 160], F32)]
    if not fused:
        inlist.append(("xlnf", [DM, 8192], BF16))
    for nm, shp, dt in inlist:
        d[nm] = p.dram(nm, shp, dt, "ExternalInput").ap()
    if fused:
        d["xlnf"] = None
        xg = G["xlnfgat"].ap()
    okind = None if fused else "ExternalOutput"
    hs_h = p.dram("hs_o", [512, 8192], BF16, okind)
    hs_o = hs_h.ap()
    part_h = p.dram("part_o", [8192, 160], F32, okind)
    part_o = part_h.ap()

    s_ld = p.sem(); s_out = p.sem(); s_conv = p.sem(); s_hs = p.sem()
    cwb = p.sbuf("cwb", [128, 4, 4], F32)
    cbb = p.sbuf("cbb", [128, 4], F32)
    wxpb = p.sbuf("wxpb", [128, 4, 160], BF16)
    s_ldg = p.sem()
    dma(p, "sync", cwb[:, :, :], d["cw"], s_ld)
    dma(p, "sync", cbb[:, :], d["cb"], s_ld)
    dma(p, "gpsimd", wxpb[:, :, :],
        d["wxp"].rearrange("(kt q) c -> q kt c", q=128), s_ldg)

    AR = p.arena()
    hid = AR[:, 0:32792].rearrange("q (mt b t) -> q mt b t", mt=4, b=2)
    hs = AR[:, 32800:65568].rearrange("q (mt t) -> q mt t", mt=4)
    xch = AR[:, 65568:81952].rearrange("q (r k t) -> q r k t", r=2, k=16)
    cs = AR[:, 81952:90144].bitcast(F32)
    partb = AR[:, 90144:92704].rearrange("q (s c) -> q s c", s=16).bitcast(F32) \
        if False else AR[:, 90144:92704].bitcast(F32).rearrange("q (s c) -> q s c", s=8)

    def fz(e):
        for mt in range(4):
            for b in range(2):
                e.memset(hid[:, mt, b, 0:3], 0.0)
        e.memset(cs[:, 0:1], 0.0)
    p.emit("vector", fz)

    ps = p.main_pool()
    s_mm = p.sem("inp_mm"); s_ev = p.sem("inp_ev"); s_x = [p.sem(f"x{i}") for i in range(2)]
    whb = p.sbuf("whb", [128, 16, 4, 128], BF16)
    dma(p, "gpsimd", whb[:, :, :, :],
        d["wh"].rearrange("(kt q) (mt m) -> q kt mt m", q=128, m=128), s_ldg)
    jobs = 0
    for tc in range(16):
        buf = tc % 2
        if tc >= 2:
            need = (tc - 1) * 4
            p.emit("sync", lambda e, t=need, s=s_mm.h: e.wait_ge(s, t))
        if not fused:
            dma(p, "sync", xch[:, buf, :, :],
                d["xlnf"][:, tc * 512:(tc + 1) * 512]
                .rearrange("(kt q) t -> q kt t", q=128), s_x[buf])
        else:
            rr, cc0 = (tc * 512) // 1024, (tc * 512) % 1024
            dma(p, "sync", xch[:, buf, :, :],
                xg[rr * 2048:(rr + 1) * 2048, cc0:cc0 + 512]
                .rearrange("(kt q) t -> q kt t", q=128), s_x[buf])
        b, l0 = divmod(tc * 512, 4096)
        for mt in range(4):
            bank = jobs % 4
            mmv = inc_last(s_mm)
            freewait = jobs - 4 + 1
            xwait = 16 * (tc // 2 + 1)
            def fmm(e, mt=mt, bank=bank, buf=buf, xwait=xwait, freewait=freewait,
                    first=(jobs == 0)):
                e.wait_ge(s_x[buf].h, xwait)
                if freewait > 0:
                    e.wait_ge(s_ev.h, freewait)
                if first:
                    e.wait_ge(s_ldg.h, 32)
                inst = None
                for k in range(16):
                    inst = e.matmul(ps[:, bank, 0:512], whb[:, k, mt, :],
                                    xch[:, buf, k, :], start=(k == 0), stop=(k == 15))
                inst.then_inc(s_mm.h, 1)
            p.emit("tensor", fmm)
            inc_last(s_ev)
            def fev(e, mt=mt, bank=bank, mmv=mmv, b=b, l0=l0):
                e.wait_ge(s_mm.h, mmv)
                e.copy(out=hid[:, mt, b, 3 + l0:3 + l0 + 512],
                       in_=ps[:, bank, 0:512]).then_inc(s_ev.h, 1)
            p.emit("scalar", fev)
            jobs += 1
    wait(p, "vector", s_ev)
    wait(p, "vector", s_ld)
    wait(p, "scalar", s_ld)
    for mt in range(4):
        for b in range(2):
            cv = inc_last(s_conv)
            def fconv(e, mt=mt, b=b):
                e.tensor_scalar(out=cs[:, :], in0=hid[:, mt, b, 0:4096],
                                scalar1=cwb[:, mt, 0:1], scalar2=None, op0=OP.mult)
                inst = None
                for w in range(1, 4):
                    inst = e.scalar_tensor_tensor(out=cs[:, :], in0=hid[:, mt, b, w:w + 4096],
                                                  scalar=cwb[:, mt, w:w + 1], in1=cs[:, :],
                                                  op0=OP.mult, op1=OP.add)
                inst.then_inc(s_conv.h, 1)
            p.emit("vector", fconv)
            hv = inc_last(s_hs)
            def fsilu(e, mt=mt, b=b, cv=cv):
                e.wait_ge(s_conv.h, cv)
                e.activation(out=hs[:, mt, b * 4096:(b + 1) * 4096], in_=cs[:, :],
                             func=AF.Silu, bias=cbb[:, mt:mt + 1]).then_inc(s_hs.h, 1)
            p.emit("scalar", fsilu)
            wait(p, "vector", s_hs, hv)
    wait(p, "sync", s_hs)
    dma(p, "sync", hs_o.rearrange("(mt q) t -> q mt t", q=128), hs[:, :, :], s_out)
    s_mm2 = p.sem("xp_mm"); s_ev2 = p.sem("xp_ev"); s_pout = p.sem("xp_out")
    wait(p, "tensor", s_hs)
    wait(p, "tensor", s_ldg, 32)
    for tt in range(64):
        slot = tt % 8
        mmv = inc_last(s_mm2)
        def fmm2(e, tt=tt, slot=slot):
            if tt >= 8:
                e.wait_ge(s_pout.h, 16 * (tt - 7))
            inst = None
            for k in range(4):
                inst = e.matmul(ps[:, tt % 4, 0:160], hs[:, k, tt * 128:(tt + 1) * 128],
                                wxpb[:, k, :], start=(k == 0), stop=(k == 3))
            inst.then_inc(s_mm2.h, 1)
        p.emit("tensor", fmm2)
        inc_last(s_ev2)
        def fev2(e, tt=tt, slot=slot, mmv=mmv):
            e.wait_ge(s_mm2.h, mmv)
            e.copy(out=partb[:, slot, :], in_=ps[:, tt % 4, 0:160]).then_inc(s_ev2.h, 1)
        p.emit("scalar", fev2)
        def fdma(e, tt=tt, slot=slot, ev=s_ev2.n):
            e.wait_ge(s_ev2.h, ev)
            e.dma_start(out=part_o.rearrange("(tt q) c -> q tt c", q=128)[:, tt, :],
                        in_=partb[:, slot, :]).then_inc(s_pout.h, 16)
        p.emit("sync", fdma)
        s_pout.n += 16
    wait(p, "sync", s_pout)
    wait(p, "sync", s_out)
    if G is not None:
        G["hs_h"] = hs_h
        G["part_h"] = part_h
        G["p1_out_sems"] = (s_pout, s_out)
    if P is None:
        return p.finish()
    return None


def host_l1pre_inmaps(inputs, xlnf_full):
    wfull = inputs["L1_attn_in_proj_w"]
    cwf = inputs["L1_attn_conv_w"][:, 0, :]
    cbf = inputs["L1_attn_conv_b"]
    xpf = inputs["L1_attn_x_proj_w"]
    maps = []
    for ds in range(NCORE):
        r0 = 512 * ds
        maps.append(dict(
            xlnf=xlnf_full,
            wh=np.ascontiguousarray(wfull[:, r0:r0 + 512], dtype=np.float32),
            cw=np.ascontiguousarray(cwf[r0:r0 + 512].reshape(4, 128, 4).transpose(1, 0, 2)),
            cb=np.ascontiguousarray(cbf[r0:r0 + 512].reshape(4, 128).T),
            wxp=np.ascontiguousarray(xpf[r0:r0 + 512], dtype=np.float32)))
    return maps


def build_l1mid(P=None, fused=False, G=None):
    p = P if P is not None else Prog("l1mid")
    if P is not None:
        p.prefix = "m1_"
    d = {}
    inlist = [("dtw", [128, 4096], F32), ("dtb", [128, 32], F32),
              ("identb", [128, 128], BF16), ("cst", [128, 8], F32)]
    if not fused:
        inlist.append(("part8", [8, TPC, 160], F32))
    for nm, shp, dt in inlist:
        d[nm] = p.dram(nm, shp, dt, "ExternalInput").ap()
    okind = None if fused else "ExternalOutput"
    dt_h = p.dram("dt_o", [4096, TPC], F32, okind)
    dt_o = dt_h.ap()
    bt_h = p.dram("bt_o", [16, TPC], F32, okind)
    bt_o = bt_h.ap()
    ct_h = p.dram("ct_o", [16, TPC], F32, okind)
    ct_o = ct_h.ap()

    s_ld = p.sem(); s_ldg = p.sem(); s_ph = p.sem(); s_sc = p.sem(); s_out = p.sem()
    AR = p.arena()
    if not fused:
        pb = AR[:, 0:20480].bitcast(F32).rearrange("q (tt j c) -> q tt j c", tt=8, j=8)
    ssm = AR[:, 69632:72192].bitcast(F32).rearrange("q (t c) -> q t c", t=8)
    cst = p.cstbuf()
    idb = p.idbuf()
    stat = p.statb()
    sqscr = p.sqscrb()
    dtwb = AR[:, 65536:69632].rearrange("q (m c) -> q m c", m=32)
    dtbb = p.sbuf("dtbb", [128, 32], F32)
    dtrn = AR[:, 72192:73216].rearrange("q (t c) -> q t c", t=8)
    bcn = AR[:, 73216:75264].bitcast(F32).rearrange("q (t c) -> q t c", t=8)
    dtrfm = AR[:, 75264:76288]
    bct = AR[:, 76288:80384].bitcast(F32).rearrange("q (c t) -> q c t", c=2)[0:16, :, :]
    dto = AR[:, 0:65536].bitcast(F32).rearrange("q (m t) -> q m t", m=32)
    lscr = AR[:, 80384:82432].bitcast(F32).rearrange("q (r t) -> q r t", r=2)

    if not fused:
        for j in range(8):
            dma(p, "sync", pb[:, :, j, :],
                d["part8"][j, :, :].rearrange("(tt q) c -> q tt c", q=128), s_ld)
    else:
        wait(p, "sync", G["cc_part"])
        dma(p, "sync", ssm[:, :, :],
            G["partsc"].ap().rearrange("(tt q) c -> q tt c", q=128), s_ld)
    dma(p, "sync", cst[:, :], d["cst"], s_ld)
    dma(p, "sync", idb[:, :], d["identb"], s_ld)
    dma(p, "sync", dtbb[:, :], d["dtb"], s_ld)
    dma(p, "gpsimd", dtwb[:, :, :], d["dtw"].rearrange("q (mt m) -> q mt m", m=128), s_ldg)

    wait(p, "vector", s_ld)
    wait(p, "scalar", s_ld)
    s_sum = p.sem("sum")
    def fpad(e):
        e.memset(bcn[:, :, :], 0.0)
    p.emit("vector", fpad)
    def fsum(e):
        inst = None
        for tt in range(8):
            e.tensor_tensor(out=ssm[:, tt, :], in0=pb[:, tt, 0, :], in1=pb[:, tt, 1, :],
                            op=OP.add)
            for j in range(2, 8):
                inst = e.tensor_tensor(out=ssm[:, tt, :], in0=ssm[:, tt, :],
                                       in1=pb[:, tt, j, :], op=OP.add)
        inst.then_inc(s_sum.h, 1)
    p.emit("vector", fsum)
    s_sum.n += 1
    wait(p, "scalar", s_sum)
    s_rms = p.sem("rms")
    def frms(e):
        for tt in range(8):
            e.activation(out=sqscr[:, 0:128], in_=ssm[:, tt, 0:128], func=AF.Square,
                         accum_out=stat[:, tt:tt + 1])
            e.activation(out=sqscr[:, 0:16], in_=ssm[:, tt, 128:144], func=AF.Square,
                         accum_out=stat[:, 8 + tt:9 + tt])
            e.activation(out=sqscr[:, 0:16], in_=ssm[:, tt, 144:160], func=AF.Square,
                         accum_out=stat[:, 16 + tt:17 + tt])
        for g, n in ((0, 128.0), (8, 16.0), (16, 16.0)):
            for tt in range(8):
                e.activation(out=stat[:, 24 + g + tt:25 + g + tt],
                             in_=stat[:, g + tt:g + tt + 1], func=AF.Ln,
                             scale=1.0 / n, bias=cst[:, 1:2])
        inst = None
        for c in range(24):
            inst = e.activation(out=stat[:, 24 + c:25 + c], in_=stat[:, 24 + c:25 + c],
                                func=AF.Exp, scale=-0.5)
        inst.then_inc(s_rms.h, 1)
    p.emit("scalar", frms)
    s_rms.n += 1
    wait(p, "vector", s_rms)
    s_n = p.sem("norm")
    def fnorm(e):
        inst = None
        for tt in range(8):
            e.tensor_scalar(out=dtrn[:, tt, :], in0=ssm[:, tt, 0:128],
                            scalar1=stat[:, 24 + tt:25 + tt], scalar2=None, op0=OP.mult)
            e.tensor_scalar(out=bcn[:, tt, 0:16], in0=ssm[:, tt, 128:144],
                            scalar1=stat[:, 32 + tt:33 + tt], scalar2=None, op0=OP.mult)
            inst = e.tensor_scalar(out=bcn[:, tt, 32:48], in0=ssm[:, tt, 144:160],
                                   scalar1=stat[:, 40 + tt:41 + tt], scalar2=None,
                                   op0=OP.mult)
        inst.then_inc(s_n.h, 1)
    p.emit("vector", fnorm)
    s_n.n += 1
    tp = p.get_tp()
    wait(p, "tensor", s_n)
    for tt in range(8):
        tp.emit(dtrn[:, tt, :], dtrfm[:, tt * 128:(tt + 1) * 128], idb[:, :])
    identf = p.sbuf("identf", [128, 128], F32)
    s_id = p.sem("id")
    def fid(e):
        e.copy(out=identf[:, :], in_=idb[:, :]).then_inc(s_id.h, 1)
    p.emit("scalar", fid)
    s_id.n += 1
    ps = p.main_pool()
    s_bt = p.sem("bt"); s_bte = p.sem("bte")
    wait(p, "tensor", s_id)
    for tt in range(8):
        bank = tt % 2
        mmv = inc_last(s_bt)
        def fT(e, tt=tt, bank=bank):
            if tt >= 2:
                e.wait_ge(s_bte.h, 2 * (tt - 1))
            e.transpose(ps[0:128, bank, 0:128], bcn[:, tt, 0:128],
                        identf[:, :]).then_inc(s_bt.h, 1)
        p.emit("tensor", fT)
        inc_last(s_bte, 2)
        def fTe(e, tt=tt, bank=bank, mmv=mmv):
            e.wait_ge(s_bt.h, mmv)
            e.copy(out=bct[:, 0, tt * 128:(tt + 1) * 128],
                   in_=ps[0:16, bank, 0:128]).then_inc(s_bte.h, 1)
            e.copy(out=bct[:, 1, tt * 128:(tt + 1) * 128],
                   in_=ps[32:48, bank, 0:128]).then_inc(s_bte.h, 1)
        p.emit("scalar", fTe)
    wait(p, "sync", s_bte)
    wait(p, "tensor", tp.s_ev)
    s_mm3 = p.sem("dt_mm"); s_ev3 = p.sem("dt_ev"); s_v3 = p.sem("dt_v")
    wait(p, "tensor", s_ldg)
    jobs = 0
    for m in range(32):
        for tc in range(2):
            bank = jobs % 4
            mmv = inc_last(s_mm3)
            def fmm3(e, m=m, tc=tc, bank=bank, jobs=jobs, bte=s_bte.n):
                if jobs < 4:
                    e.wait_ge(s_bte.h, bte)
                if jobs >= 4:
                    e.wait_ge(s_v3.h, jobs - 3)
                e.matmul(ps[:, bank, 0:512], dtwb[:, m, :],
                         dtrfm[:, tc * 512:(tc + 1) * 512],
                         start=True, stop=True).then_inc(s_mm3.h, 1)
            p.emit("tensor", fmm3)
            scv = inc_last(s_ev3)
            def fev3(e, m=m, tc=tc, bank=bank, mmv=mmv, jobs=jobs):
                e.wait_ge(s_mm3.h, mmv)
                slot = jobs % 2
                e.activation(out=dto[:, m, tc * 512:(tc + 1) * 512], in_=ps[:, bank, 0:512],
                             func=AF.Relu, bias=dtbb[:, m:m + 1])
                e.activation(out=lscr[:, slot, :], in_=ps[:, bank, 0:512], func=AF.Abs,
                             bias=dtbb[:, m:m + 1])
                e.activation(out=lscr[:, slot, :], in_=lscr[:, slot, :], func=AF.Exp,
                             scale=-1.0)
                e.activation(out=lscr[:, slot, :], in_=lscr[:, slot, :], func=AF.Ln,
                             bias=cst[:, 2:3]).then_inc(s_ev3.h, 1)
            p.emit("scalar", fev3)
            inc_last(s_v3)
            def fv3(e, m=m, tc=tc, scv=scv, jobs=jobs):
                e.wait_ge(s_ev3.h, scv)
                slot = jobs % 2
                e.tensor_tensor(out=dto[:, m, tc * 512:(tc + 1) * 512],
                                in0=dto[:, m, tc * 512:(tc + 1) * 512],
                                in1=lscr[:, slot, :], op=OP.add).then_inc(s_v3.h, 1)
            p.emit("vector", fv3)
            jobs += 1
    wait(p, "sync", s_v3)
    dma(p, "sync", dt_o.rearrange("(mt q) t -> q mt t", q=128), dto[:, :, :], s_out)
    dma(p, "sync", bt_o, bct[0:16, 0, :], s_out)
    dma(p, "sync", ct_o, bct[0:16, 1, :], s_out)
    wait(p, "sync", s_out)
    if G is not None:
        G["dt_h"] = dt_h
        G["bt_h"] = bt_h
        G["ct_h"] = ct_h
        G["m1_out_sem"] = s_out
    if P is None:
        return p.finish()
    return None


def host_l1mid_inmaps(inputs, parts):
    dtw = np.ascontiguousarray(inputs["L1_attn_dt_proj_w"], dtype=np.float32)
    dtb = np.ascontiguousarray(inputs["L1_attn_dt_proj_b"].reshape(32, 128).T)
    identb = np.eye(128, dtype=BF)
    maps = []
    for c in range(NCORE):
        p8 = np.stack([parts[j][c * TPC:(c + 1) * TPC] for j in range(8)], axis=0)
        maps.append(dict(part8=np.ascontiguousarray(p8), dtw=dtw, dtb=dtb,
                         identb=identb, cst=CST))
    return maps


def build_l1scan(P=None, fused=False, G=None):
    p = P if P is not None else Prog("l1scan")
    if P is not None:
        p.prefix = "c1_"
    d = {}
    inlist = [("alog", [128, 4, 16], F32), ("dp", [128, 4], F32),
              ("wg", [DM, 512], F32)]
    if not fused:
        inlist += [("hs_in", [512, 8192], BF16), ("dt_in", [512, 8192], F32),
                   ("bt_in", [16, 8192], F32), ("ct_in", [16, 8192], F32),
                   ("xlnf", [DM, 8192], BF16)]
    for nm, shp, dt in inlist:
        d[nm] = p.dram(nm, shp, dt, "ExternalInput").ap()
    if fused:
        d["hs_in"] = G["hs_h"].ap()
        xg5 = G["xlnfgat"].ap()
    okind = None if fused else "ExternalOutput"
    yt_h = p.dram("ytil_o", [512, 8192], BF16, okind)
    ytil_o = yt_h.ap()

    s_ld = p.sem(); s_ldg = p.sem(); s_out = p.sem()
    AR = p.arena()
    hs = AR[:, 0:32768].rearrange("q (mt t) -> q mt t", mt=4)
    Ab_ = p.sbuf("Abuf", [128, 4, 16], F32)
    dpb = p.sbuf("dpb", [128, 4], F32)
    wgb = AR[:, 32768:40960].rearrange("q (k mt c) -> q k mt c", k=16, mt=4)
    dtb_ = AR[:, 40960:49152].bitcast(F32)
    dtu = AR[:, 49152:57344].bitcast(F32)
    bbc = AR[:, 57344:61440].bitcast(F32).rearrange("q (r t) -> q r t", r=1)
    cbc = AR[:, 61440:65536].bitcast(F32).rearrange("q (r t) -> q r t", r=1)
    asc = AR[:, 65536:69632].bitcast(F32)
    bsc = AR[:, 69632:73728].bitcast(F32)
    ssc = AR[:, 73728:77824].bitcast(F32).rearrange("q (r t) -> q r t", r=1)
    carry = p.sbuf("carry", [128, 16], F32)
    tmp = AR[:, 77824:81920].bitcast(F32)
    yb = AR[:, 81920:86016].bitcast(F32)
    gsl = AR[:, 86016:88064]
    xch = AR[:, 57344:65536].rearrange("q (r k t) -> q r k t", r=1, k=16)
    ytb = AR[:, 69632:71680]

    dma(p, "sync", Ab_[:, :, :], d["alog"], s_ld)
    dma(p, "sync", dpb[:, :], d["dp"], s_ld)
    dma(p, "sync", hs[:, :, :], d["hs_in"].rearrange("(mt q) t -> q mt t", q=128), s_ld)
    dma(p, "gpsimd", wgb[:, :, :, :],
        d["wg"].rearrange("(kt q) (mt m) -> q kt mt m", q=128, m=128), s_ldg)
    s_A = p.sem("A")
    wait(p, "scalar", s_ld)
    def fA(e):
        e.activation(out=Ab_[:, :, :], in_=Ab_[:, :, :], func=AF.Exp, scale=1.0)
        e.activation(out=Ab_[:, :, :], in_=Ab_[:, :, :], func=AF.Copy,
                     scale=-1.0).then_inc(s_A.h, 1)
    p.emit("scalar", fA)
    s_A.n += 1
    wait(p, "vector", s_ld)

    ps = p.main_pool()
    s_dt = p.sem("dt")
    s_v = p.sem("v")
    s_s = p.sem("s")
    s_b = [p.sem(f"bb{i}") for i in range(2)]
    s_g = p.sem("g")
    s_gm = p.sem("gm")
    s_yo = p.sem("yo")
    nv = 0
    ns = 0
    it = 0
    for dtile in range(4):
        for b in range(2):
            wait(p, "sync", s_s)
            if not fused:
                dwait = dma(p, "sync", dtb_[:, :],
                            d["dt_in"][dtile * 128:(dtile + 1) * 128,
                                       b * 4096:(b + 1) * 4096], s_dt)
            else:
                dtex = G["dtex"].ap()
                for rr in range(4 * b, 4 * b + 4):
                    dwait = dma(p, "sync", dtb_[:, (rr % 4) * 1024:(rr % 4) * 1024 + 1024],
                                dtex[rr * 512 + dtile * 128:rr * 512 + (dtile + 1) * 128, :],
                                s_dt)
            nv = inc_last(s_v)
            def fdtu(e, dtile=dtile, b=b, dwait=dwait):
                e.wait_ge(s_dt.h, dwait)
                e.tensor_tensor(out=dtu[:, :], in0=dtb_[:, :],
                                in1=hs[:, dtile, b * 4096:(b + 1) * 4096],
                                op=OP.mult).then_inc(s_v.h, 1)
            p.emit("vector", fdtu)
            for half in range(2):
                off = half * 2048
                gt0 = b * 4096 + off
                for n in range(16):
                    par = 0
                    vprev = s_v.n
                    p.emit("sync", lambda e, t=vprev: e.wait_ge(s_v.h, t))
                    if not fused:
                        bw = dma(p, "sync", bbc[:, par, :],
                                 d["bt_in"][n:n + 1, gt0:gt0 + 2048]
                                 .broadcast_to((128, 2048)), s_b[par])
                        cw = dma(p, "sync", cbc[:, par, :],
                                 d["ct_in"][n:n + 1, gt0:gt0 + 2048]
                                 .broadcast_to((128, 2048)), s_b[par])
                    else:
                        btg = G["btgat"].ap()
                        ctg = G["ctgat"].ap()
                        r0 = gt0 // 1024
                        for ri in range(2):
                            dma(p, "sync", bbc[:, par, ri * 1024:(ri + 1) * 1024],
                                btg[(r0 + ri) * 16 + n:(r0 + ri) * 16 + n + 1, :]
                                .broadcast_to((128, 1024)), s_b[par])
                            dma(p, "sync", cbc[:, par, ri * 1024:(ri + 1) * 1024],
                                ctg[(r0 + ri) * 16 + n:(r0 + ri) * 16 + n + 1, :]
                                .broadcast_to((128, 1024)), s_b[par])
                        bw = s_b[par].n - 16
                    prev_scan = s_v.n - 1
                    ns = inc_last(s_s)
                    def fa(e, dtile=dtile, n=n, off=off, nv=nv, ns=ns, prev_scan=prev_scan):
                        e.wait_ge(s_v.h, max(nv, prev_scan))
                        e.activation(out=asc[:, :], in_=dtb_[:, off:off + 2048],
                                     func=AF.Exp,
                                     scale=Ab_[:, dtile, n:n + 1]).then_inc(s_s.h, 1)
                    p.emit("scalar", fa)
                    nv_prev_scan = s_v.n
                    nbc = 32 if not fused else 64
                    def fb(e, par=par, off=off, bw=bw, ns=ns, nbc=nbc):
                        e.wait_ge(s_b[par].h, bw + 16)
                        e.tensor_tensor(out=bsc[:, :], in0=dtu[:, off:off + 2048],
                                        in1=bbc[:, par, :], op=OP.mult)
                        e.wait_ge(s_s.h, ns)
                    p.emit("vector", fb)
                    init = 0.0 if half == 0 else carry[:, n:n + 1]
                    nv = inc_last(s_v, 2)
                    def fscan(e, init=init, par=par, n=n, half=half):
                        e.tensor_tensor_scan(out=ssc[:, 0, :], data0=asc[:, :],
                                             data1=bsc[:, :], initial=init,
                                             op0=OP.mult, op1=OP.add)
                        if half == 0:
                            e.tensor_copy(out=carry[:, n:n + 1], in_=ssc[:, 0, 2047:2048])
                        e.tensor_scalar(out=ssc[:, 0, 0:1], in0=ssc[:, 0, 0:1],
                                        scalar1=1.0, scalar2=None,
                                        op0=OP.mult).then_inc(s_v.h, 1)
                        e.tensor_tensor(out=tmp[:, :], in0=ssc[:, 0, :],
                                        in1=cbc[:, par, :], op=OP.mult)
                        if n == 0:
                            e.tensor_copy(out=yb[:, :], in_=tmp[:, :])
                            e.engine_nop() if False else None
                            inst = e.tensor_scalar(out=yb[:, 0:1], in0=yb[:, 0:1],
                                                   scalar1=1.0, scalar2=None, op0=OP.mult)
                        else:
                            inst = e.tensor_tensor(out=yb[:, :], in0=yb[:, :],
                                                   in1=tmp[:, :], op=OP.add)
                        inst.then_inc(s_v.h, 1)
                    p.emit("vector", fscan)
                vnow = s_v.n
                for tc in range(4):
                    xbuf = 0
                    gprev = s_gm.n
                    def fgw(e, t=gprev, tc=tc, vnow=vnow):
                        if t > 0:
                            e.wait_ge(s_gm.h, t)
                        if tc == 0:
                            e.wait_ge(s_v.h, vnow)
                    p.emit("sync", fgw)
                    if not fused:
                        xw = dma(p, "sync", xch[:, xbuf, :, :],
                                 d["xlnf"][:, gt0 + tc * 512:gt0 + (tc + 1) * 512]
                                 .rearrange("(kt q) t -> q kt t", q=128), s_ld)
                    else:
                        gg = gt0 + tc * 512
                        rr2, cc2 = gg // 1024, gg % 1024
                        xw = dma(p, "sync", xch[:, xbuf, :, :],
                                 xg5[rr2 * 2048:(rr2 + 1) * 2048, cc2:cc2 + 512]
                                 .rearrange("(kt q) t -> q kt t", q=128), s_ld)
                    bank = tc % 2
                    gmv = inc_last(s_gm)
                    def fgm(e, dtile=dtile, tc=tc, xbuf=xbuf, bank=bank, xw=xw,
                            it=it, gmv=gmv):
                        e.wait_ge(s_ld.h, xw)
                        if gmv > 2:
                            e.wait_ge(s_g.h, gmv - 2)
                        if gmv == 1:
                            e.wait_ge(s_ldg.h, 16)
                        inst = None
                        for k in range(16):
                            inst = e.matmul(ps[:, bank, 0:512], wgb[:, k, dtile, :],
                                            xch[:, xbuf, k, :], start=(k == 0),
                                            stop=(k == 15))
                        inst.then_inc(s_gm.h, 1)
                    p.emit("tensor", fgm)
                    inc_last(s_g)
                    def fge(e, tc=tc, bank=bank, gmv=gmv):
                        e.wait_ge(s_gm.h, gmv)
                        e.activation(out=gsl[:, tc * 512:(tc + 1) * 512],
                                     in_=ps[:, bank, 0:512],
                                     func=AF.Silu).then_inc(s_g.h, 1)
                    p.emit("scalar", fge)
                yv = inc_last(s_v)
                gev = s_g.n
                def fcomp(e, dtile=dtile, b=b, off=off, gev=gev):
                    e.scalar_tensor_tensor(out=tmp[:, :],
                                           in0=hs[:, dtile, b * 4096 + off:b * 4096 + off + 2048],
                                           scalar=dpb[:, dtile:dtile + 1], in1=yb[:, :],
                                           op0=OP.mult, op1=OP.add)
                    e.wait_ge(s_g.h, gev)
                    e.tensor_tensor(out=ytb[:, :], in0=tmp[:, :], in1=gsl[:, :],
                                    op=OP.mult).then_inc(s_v.h, 1)
                p.emit("vector", fcomp)
                def fout(e, dtile=dtile, gt0=gt0, yv=yv):
                    e.wait_ge(s_v.h, yv)
                    e.dma_start(out=ytil_o[dtile * 128:(dtile + 1) * 128,
                                           gt0:gt0 + 2048],
                                in_=ytb[:, :]).then_inc(s_yo.h, 16)
                p.emit("sync", fout)
                s_yo.n += 16
                yo_now = s_yo.n
                def fg2(e, yo_now=yo_now):
                    e.wait_ge(s_yo.h, yo_now)
                p.emit("vector", fg2)
                it += 1
    wait(p, "sync", s_yo)
    if G is not None:
        G["yt_h"] = yt_h
        G["c1_out_sem"] = s_yo
    if P is None:
        return p.finish()
    return None


def host_l1scan_inmaps(inputs, hs_full, dt_full, bt, ct, xlnf_full):
    wfull = inputs["L1_attn_in_proj_w"]
    alog = inputs["L1_attn_A_log"]
    dpar = inputs["L1_attn_D_param"]
    maps = []
    for ds in range(NCORE):
        r0 = 512 * ds
        maps.append(dict(
            hs_in=np.ascontiguousarray(hs_full[r0:r0 + 512]),
            dt_in=np.ascontiguousarray(dt_full[r0:r0 + 512]),
            bt_in=bt, ct_in=ct,
            alog=np.ascontiguousarray(alog[r0:r0 + 512].reshape(4, 128, 16).transpose(1, 0, 2)),
            dp=np.ascontiguousarray(dpar[r0:r0 + 512].reshape(4, 128).T),
            wg=np.ascontiguousarray(wfull[:, 4096 + r0:4096 + r0 + 512], dtype=np.float32),
            xlnf=xlnf_full))
    return maps


def build_l0attn(debug=False, P=None, fused=False, G=None):
    p = P if P is not None else Prog("l0attn")
    if P is not None:
        p.prefix = "a0_"
    d = {}
    for nm, shp, dt in [("x_in", [4096, DM], F32), ("wz", [DM, 1024], F32),
                        ("wx", [DM, 1024], F32), ("wbc", [DM, 256], F32),
                        ("wdt", [DM, 16], F32), ("cwx", [128, 8, 4], F32),
                        ("cbx", [128, 8], F32), ("cwbc", [128, 2, 4], F32),
                        ("cbbc", [128, 2], F32), ("dtbv", [16, 1], F32),
                        ("alog", [16, 1], F32), ("dpar", [1, 16], F32),
                        ("mneg", [128, 128], F32), ("oneh", [128, 1], F32),
                        ("identb", [128, 128], BF16), ("identf", [128, 128], F32),
                        ("cst", [128, 8], F32)]:
        d[nm] = p.dram(nm, shp, dt, "ExternalInput").ap()
    okind = None if fused else "ExternalOutput"
    yf_h = p.dram("yf_o", [4096, 1024], BF16, okind)
    yf_o = yf_h.ap()
    ssq_h = p.dram("ssq_o", [128, 32], F32, okind)
    ssq_o = ssq_h.ap()
    kk = "ExternalOutput" if debug else None
    acf_h = p.dram("acf", [1, 65536], F32, kk)
    acf = acf_h.ap()
    smp_h = p.dram("smp", [1, 1024], F32, kk)
    smp = smp_h.ap()
    zsp_h = p.dram("zsp", [1024, 4096], BF16, kk)
    zsp = zsp_h.ap()
    ysp_h = p.dram("ysp", [4096, 1024], BF16, kk)
    ysp = ysp_h.ap()
    xssp_h = p.dram("xssp", [1024, 4096], BF16, kk)
    xssp = xssp_h.ap()

    cwxb = p.sbuf("cwxb", [128, 8, 4], F32)
    cbxb = p.sbuf("cbxb", [128, 8], F32)
    cwbcb = p.sbuf("cwbcb", [128, 2, 4], F32)
    cbbcb = p.sbuf("cbbcb", [128, 2], F32)
    dtbb = p.sbuf("dtbb", [16, 1], F32)
    alogb = p.sbuf("alogb", [16, 2], F32)
    mnegb = p.sbuf("mnegb", [128, 128], F32)
    onehb = p.sbuf("onehb", [128, 1], F32)
    idb = p.idbuf()
    idf = p.idfbuf()
    cstb = p.cstbuf()
    stat = p.statb()
    drep = p.sbuf("drep", [128, 16], F32)
    sampb = p.sbuf("sampb", [1, 1024], F32)
    ssqb = p.sbuf("ssqb", [128, 32], F32)

    AR = p.arena()
    def Ab(off, n):
        return AR[:, off:off + n]
    def Af(off, n):
        return AR[:, off:off + 2 * n].bitcast(F32)

    s_ld = p.sem("ld"); s_out = p.sem("out")
    for nm, dst in [("cwx", cwxb[:, :, :]), ("cbx", cbxb[:, :]),
                    ("cwbc", cwbcb[:, :, :]), ("cbbc", cbbcb[:, :]),
                    ("dtbv", dtbb[:, :]), ("alog", alogb[:, 0:1]),
                    ("mneg", mnegb[:, :]), ("oneh", onehb[:, :]),
                    ("identb", idb[:, :]), ("identf", idf[:, :]),
                    ("cst", cstb[:, :])]:
        dma(p, "sync", dst, d[nm], s_ld)
    dma(p, "sync", drep[:, :], d["dpar"].broadcast_to((128, 16)), s_ld)

    sqscr = p.sqscrb()
    dtpre = Af(76800, 4096)[0:16, :]
    xlq = Ab(0, 16384).rearrange("z (k t) -> z k t", k=16)
    xcv = Ab(16384, 32800).rearrange("z (f t) -> z f t", f=8)
    bccv = Ab(49184, 8200).rearrange("z (f t) -> z f t", f=2)
    xring = Af(57384, 4096).rearrange("z (r t) -> z r t", r=2)
    lnscr = Ab(65576, 4096).rearrange("z (r t) -> z r t", r=2)
    zring = Ab(69672, 2048).rearrange("z (r t) -> z r t", r=4)
    wb = Ab(71720, 4096).rearrange("z (r k m) -> z r k m", r=2, k=16)
    ps = p.main_pool()
    tp = p.get_tp()

    s_x = p.sem("x"); s_ph = p.sem("ph"); s_sc = p.sem("sc"); s_ln = p.sem("ln")
    s_w = [p.sem(f"w{i}") for i in range(2)]
    s_mm = p.sem("mm"); s_ev = p.sem("ev"); s_zo = p.sem("zo")
    wait(p, "vector", s_ld)
    wait(p, "scalar", s_ld)
    def fz0(e):
        e.memset(Ab(16384, 41000), 0.0)
    p.emit("vector", fz0)
    wjobs = ([("z", d["wz"], m, 128) for m in range(8)]
             + [("x", d["wx"], m, 128) for m in range(8)]
             + [("bc", d["wbc"], m, 128) for m in range(2)]
             + [("dt", d["wdt"], 0, 16)])
    mwi = 0
    job = 0
    for q in range(4):
        for i in range(8):
            tt = q * 8 + i
            prev_ln = s_ln.n - 1
            if prev_ln > 0:
                p.emit("sync", lambda e, t=prev_ln: e.wait_ge(s_ln.h, t))
            xw = dma(p, "sync", xring[:, i % 2, :],
                     d["x_in"][tt * 128:(tt + 1) * 128, :], s_x)
            wait(p, "vector", s_x, xw)
            wait(p, "scalar", s_x, xw)
            def cb(e, ii, mean, rstd, i=i):
                e.tensor_scalar(out=lnscr[:, i % 2, :], in0=xring[:, i % 2, :],
                                scalar1=mean, scalar2=rstd, op0=OP.subtract,
                                op1=OP.mult).then_inc(s_ln.h, 1)
            emit_ln8(p, lambda ii, i=i: xring[:, i % 2, :], stat, s_ph, s_sc, cb,
                     1e-5, ntiles=1, sqscr=sqscr[:, :])
            inc_last(s_ln)
            wait(p, "tensor", s_ln, s_ln.n)
            for ct in range(16):
                tp.emit(lnscr[:, i % 2, ct * 128:(ct + 1) * 128],
                        xlq[:, ct, i * 128:(i + 1) * 128], idb[:, :], evac_eng="scalar")
        tp.drain(["tensor"])
        for (kind, wap, m, mcols) in wjobs:
            buf = mwi % 2
            if mwi >= 2:
                need = (mwi - 1) * 2
                p.emit("gpsimd", lambda e, t=need, s=s_mm.h: e.wait_ge(s, t))
            dma(p, "gpsimd", wb[:, buf, 0:16, 0:mcols],
                wap[:, m * mcols:(m + 1) * mcols]
                .rearrange("(kt z) c -> z kt c", z=128), s_w[buf])
            wwait = 16 * (mwi // 2 + 1)
            for tc in range(2):
                bank = job % 4
                mmv = inc_last(s_mm)
                freewait = job - 4 + 1
                prows = 128 if mcols == 128 else 16
                def fmm(e, buf=buf, bank=bank, wwait=wwait, freewait=freewait,
                        mcols=mcols, tc=tc, prows=prows):
                    e.wait_ge(s_w[buf].h, wwait)
                    if freewait > 0:
                        e.wait_ge(s_ev.h, freewait)
                    inst = None
                    for k in range(16):
                        inst = e.matmul(ps[0:prows, bank, 0:512],
                                        wb[:, buf, k, 0:mcols],
                                        xlq[:, k, tc * 512:(tc + 1) * 512],
                                        start=(k == 0), stop=(k == 15))
                    inst.then_inc(s_mm.h, 1)
                p.emit("tensor", fmm)
                col0 = q * 1024 + tc * 512
                inc_last(s_ev)
                if kind == "z":
                    zi = job
                    def fevz(e, bank=bank, mmv=mmv, zi=zi, zon=s_zo.n):
                        e.wait_ge(s_mm.h, mmv)
                        if zon >= 64:
                            e.wait_ge(s_zo.h, zon - 48)
                        e.copy(out=zring[:, zi % 4, :],
                               in_=ps[:, bank, 0:512]).then_inc(s_ev.h, 1)
                    p.emit("scalar", fevz)
                    evn = s_ev.n
                    def fdz(e, m=m, col0=col0, evn=evn, zi=zi):
                        e.wait_ge(s_ev.h, evn)
                        e.dma_start(out=zsp[m * 128:(m + 1) * 128, col0:col0 + 512],
                                    in_=zring[:, zi % 4, :]).then_inc(s_zo.h, 16)
                    p.emit("sync", fdz)
                    s_zo.n += 16
                elif kind == "x":
                    def fev(e, m=m, bank=bank, mmv=mmv, col0=col0):
                        e.wait_ge(s_mm.h, mmv)
                        e.copy(out=xcv[:, m, 3 + col0:3 + col0 + 512],
                               in_=ps[:, bank, 0:512]).then_inc(s_ev.h, 1)
                    p.emit("scalar", fev)
                elif kind == "bc":
                    def fev(e, m=m, bank=bank, mmv=mmv, col0=col0):
                        e.wait_ge(s_mm.h, mmv)
                        e.copy(out=bccv[:, m, 3 + col0:3 + col0 + 512],
                               in_=ps[:, bank, 0:512]).then_inc(s_ev.h, 1)
                    p.emit("scalar", fev)
                else:
                    def fev(e, bank=bank, mmv=mmv, col0=col0):
                        e.wait_ge(s_mm.h, mmv)
                        e.copy(out=dtpre[:, col0:col0 + 512],
                               in_=ps[0:16, bank, 0:512]).then_inc(s_ev.h, 1)
                    p.emit("scalar", fev)
                job += 1
            mwi += 1
    cs = Af(57384, 4096)
    xsr = Ab(0, 8192).rearrange("z (r t) -> z r t", r=2)
    bcs = Ab(8192, 8192).rearrange("z (f t) -> z f t", f=2)
    s_conv = p.sem("conv"); s_hs = p.sem("hs"); s_xso = p.sem("xso")
    wait(p, "vector", s_ev)
    wait(p, "scalar", s_ev)
    for grp in range(2):
        n_ft = 8 if grp == 0 else 2
        for ft in range(n_ft):
            src_t, cvb, cbv = ((xcv, cwxb, cbxb) if grp == 0 else (bccv, cwbcb, cbbcb))
            cv = inc_last(s_conv)
            def fconv(e, ft=ft, src_t=src_t, cvb=cvb, xon=s_xso.n):
                if xon >= 32:
                    e.wait_ge(s_xso.h, xon - 16)
                e.tensor_scalar(out=cs[:, :], in0=src_t[:, ft, 0:4096],
                                scalar1=cvb[:, ft, 0:1], scalar2=None, op0=OP.mult)
                inst = None
                for w in range(1, 4):
                    inst = e.scalar_tensor_tensor(out=cs[:, :],
                                                  in0=src_t[:, ft, w:w + 4096],
                                                  scalar=cvb[:, ft, w:w + 1],
                                                  in1=cs[:, :], op0=OP.mult, op1=OP.add)
                inst.then_inc(s_conv.h, 1)
            p.emit("vector", fconv)
            hv = inc_last(s_hs)
            if grp == 0:
                def fsilu(e, ft=ft, cv=cv, cbv=cbv):
                    e.wait_ge(s_conv.h, cv)
                    e.activation(out=xsr[:, ft % 2, :], in_=cs[:, :], func=AF.Silu,
                                 bias=cbv[:, ft:ft + 1]).then_inc(s_hs.h, 1)
                p.emit("scalar", fsilu)
                def fxo(e, ft=ft, hv=hv):
                    e.wait_ge(s_hs.h, hv)
                    e.dma_start(out=xssp[ft * 128:(ft + 1) * 128, :],
                                in_=xsr[:, ft % 2, :]).then_inc(s_xso.h, 16)
                p.emit("sync", fxo)
                s_xso.n += 16
            else:
                def fsilu(e, ft=ft, cv=cv, cbv=cbv):
                    e.wait_ge(s_conv.h, cv)
                    e.activation(out=bcs[:, ft, :], in_=cs[:, :], func=AF.Silu,
                                 bias=cbv[:, ft:ft + 1]).then_inc(s_hs.h, 1)
                p.emit("scalar", fsilu)
            wait(p, "vector", s_hs, hv)
    dtf = Af(16384, 4096)[0:16, :]
    zer = Af(24576, 4096)[0:16, :]
    acum = Af(32768, 4096)[0:16, :]
    stkT = Af(40960, 1024).rearrange("z (t c) -> z t c", t=32)
    negAc = Af(43008, 512).rearrange("z (t c) -> z t c", t=32)
    odT = Af(44032, 512).rearrange("z (t c) -> z t c", t=32)
    nsprep = Af(45056, 512)
    dchrep = Af(46080, 512)
    s_c = p.sem("cc"); s_cs = p.sem("ccs")
    s_ev_total = s_ev.n
    def fsp(e):
        e.wait_ge(s_ev.h, s_ev_total)
        e.activation(out=alogb[:, 1:2], in_=alogb[:, 0:1], func=AF.Exp)
        e.activation(out=alogb[:, 1:2], in_=alogb[:, 1:2], func=AF.Copy, scale=-1.0)
        e.activation(out=dtf[:, :], in_=dtpre[:, :], func=AF.Relu, bias=dtbb[:, 0:1])
        e.activation(out=zer[:, :], in_=dtpre[:, :], func=AF.Abs, bias=dtbb[:, 0:1])
        e.activation(out=zer[:, :], in_=zer[:, :], func=AF.Exp, scale=-1.0)
        e.activation(out=zer[:, :], in_=zer[:, :], func=AF.Ln,
                     bias=cstb[0:16, 2:3]).then_inc(s_cs.h, 1)
    p.emit("scalar", fsp)
    cs1 = inc_last(s_cs)
    c1 = inc_last(s_c)
    def fdt(e, cs1=cs1):
        e.wait_ge(s_cs.h, cs1)
        e.tensor_tensor(out=dtf[:, :], in0=dtf[:, :], in1=zer[:, :], op=OP.add)
        e.tensor_scalar(out=acum[:, :], in0=dtf[:, :], scalar1=alogb[:, 1:2],
                        scalar2=None, op0=OP.mult)
        e.memset(zer[:, :], 0.0)
        e.memset(sampb[:, :], 0.0)
        e.memset(sqscr[:, 0:512], 0.0)
        e.tensor_tensor_scan(out=acum[:, :], data0=acum[:, :], data1=zer[:, :],
                             initial=0.0, op0=OP.add, op1=OP.add)
        for _ in range(6):
            e.tensor_copy(out=sqscr[:, 1024:2048], in_=sqscr[:, 0:1024])
        e.tensor_copy(out=sqscr[:, 1024:2048],
                      in_=sqscr[:, 0:1024]).then_inc(s_c.h, 1)
    p.emit("vector", fdt)
    s_st = p.sem("st"); s_ste = p.sem("ste")
    wait(p, "tensor", s_c, c1)
    for tt in range(32):
        for half16, srcv in ((0, dtf), (1, acum)):
            bank = (2 * tt + half16) % 2
            mmv = inc_last(s_st)
            idx = 2 * tt + half16
            def fT(e, tt=tt, bank=bank, srcv=srcv, idx=idx):
                if idx >= 2:
                    e.wait_ge(s_ste.h, idx - 1)
                e.transpose(ps[0:128, bank, 0:16], srcv[:, tt * 128:(tt + 1) * 128],
                            idf[0:16, 0:16]).then_inc(s_st.h, 1)
            p.emit("tensor", fT)
            inc_last(s_ste)
            def fTe(e, tt=tt, bank=bank, mmv=mmv, half16=half16):
                e.wait_ge(s_st.h, mmv)
                e.copy(out=stkT[:, tt, 16 * half16:16 * half16 + 16],
                       in_=ps[:, bank, 0:16]).then_inc(s_ste.h, 1)
            p.emit("scalar", fTe)
    s_ac = p.sem("ac")
    wait(p, "sync", s_c, c1)
    dma(p, "sync", acf.rearrange("o (h t) -> (o h) t", h=16), acum[:, :], s_ac)
    s_sp = p.sem("sp"); s_spe = p.sem("spe")
    wait(p, "tensor", s_ste)
    def fsamp(e):
        inst = None
        for tt in range(32):
            inst = e.matmul(ps[0:1, 3, tt * 16:(tt + 1) * 16], onehb[:, :],
                            stkT[:, tt, 16:32], start=(tt == 0), stop=(tt == 31))
        inst.then_inc(s_sp.h, 1)
    p.emit("tensor", fsamp)
    sp1 = inc_last(s_sp)
    def fsampe(e, sp1=sp1):
        e.wait_ge(s_sp.h, sp1)
        e.copy(out=sampb[:, 0:512], in_=ps[0:1, 3, 0:512]).then_inc(s_spe.h, 1)
    p.emit("scalar", fsampe)
    spe1 = inc_last(s_spe)
    def fshift(e, spe1=spe1):
        e.wait_ge(s_spe.h, spe1)
        e.memset(sqscr[:, 512:1024], 0.0)
        e.memset(sqscr[:, 1024:1536], 0.0)
        e.memset(sampb[:, 512:528], 0.0)
        e.tensor_copy(out=sampb[:, 528:1024], in_=sampb[:, 0:496])
        e.tensor_tensor(out=sampb[:, 0:512], in0=sampb[:, 0:512],
                        in1=sampb[:, 512:1024], op=OP.subtract)
        e.tensor_scalar(out=sampb[:, 512:1024], in0=sampb[:, 512:1024],
                        scalar1=-1.0, scalar2=None, op0=OP.mult).then_inc(s_c.h, 1)
    p.emit("vector", fshift)
    c2 = inc_last(s_c)
    def fdch(e, c2=c2):
        e.wait_ge(s_c.h, c2)
        e.activation(out=sqscr[:, 0:512], in_=sqscr[:, 0:512], func=AF.Square)
        e.activation(out=sqscr[:, 0:512], in_=sqscr[:, 0:512], func=AF.Square)
        e.activation(out=sampb[:, 0:512], in_=sampb[:, 0:512],
                     func=AF.Exp).then_inc(s_cs.h, 1)
    p.emit("scalar", fdch)
    cs2 = inc_last(s_cs)
    p.emit("sync", lambda e, t=cs2: e.wait_ge(s_cs.h, t))
    dma(p, "sync", smp, sampb[:, :], s_ac)
    wait(p, "sync", s_ac)
    dma(p, "sync", dchrep, smp[0:1, 0:512].broadcast_to((128, 512)), s_ac)
    dma(p, "sync", nsprep, smp[0:1, 512:1024].broadcast_to((128, 512)), s_ac)
    ac_all = s_ac.n
    wait(p, "vector", s_ac)
    c3 = inc_last(s_c)
    def fneg(e):
        inst = None
        for tt in range(32):
            e.tensor_scalar(out=negAc[:, tt, :], in0=stkT[:, tt, 16:32],
                            scalar1=-1.0, scalar2=None, op0=OP.mult)
            inst = e.tensor_tensor(out=odT[:, tt, :], in0=stkT[:, tt, 16:32],
                                   in1=nsprep[:, tt * 16:(tt + 1) * 16], op=OP.add)
        inst.then_inc(s_c.h, 1)
    p.emit("vector", fneg)
    def fodx(e, c3=c3):
        e.wait_ge(s_c.h, c3)
        e.activation(out=odT[:, :, :], in_=odT[:, :, :], func=AF.Exp).then_inc(s_cs.h, 1)
    p.emit("scalar", fodx)
    cs3 = inc_last(s_cs)
    xsh = Ab(47104, 16384).rearrange("z (f t) -> z f t", f=8)
    xT = Ab(63488, 16384).rearrange("z (t c) -> z t c", t=16)
    xdtT = Ab(47104, 16384).rearrange("z (t c) -> z t c", t=16)
    S = Af(79872, 1024)
    Sb = Ab(81920, 1024)
    brow = Af(82944, 2048).rearrange("z (h t) -> z h t", h=16)
    mtall = Ab(87040, 2048).rearrange("z (h t) -> z h t", h=16)
    xddall = Ab(89088, 1024).rearrange("z (h t) -> z h t", h=16)
    gsb = Af(90112, 128)
    yring = Ab(90368, 1024).rearrange("z (r t) -> z r t", r=1)
    BT = Ab(91392, 2048).rearrange("z (t c) -> z t c", t=16)
    t0all = Af(87040, 1024).rearrange("z (h t) -> z h t", h=16)
    s_xl = p.sem("xl")
    s_br = p.sem("br")
    s_bm = p.sem("bm")
    s_e = p.sem("ee"); s_mt = p.sem("mt"); s_pe = p.sem("pe"); s_vv = p.sem("vv")
    s_yd = p.sem("yd"); s_xdt = p.sem("xdt")
    def fS0(e):
        e.memset(S[:, :], 0.0)
        e.memset(Sb[:, :], 0.0)
    p.emit("vector", fS0)
    gps = tp.pool[:, 2, 0:256].bitcast(F32)
    pso = tp.pool[:, 3, 0:256].bitcast(F32)
    wait(p, "gpsimd", s_ld)
    for half in range(2):
        hc0 = half * 16
        wait(p, "sync", s_xso)
        if half == 1:
            p.emit("sync", lambda e, t=s_pe.n: e.wait_ge(s_pe.h, t))
        xlw = dma(p, "sync", xsh[:, :, :],
                  xssp[:, half * 2048:(half + 1) * 2048]
                  .rearrange("(ft z) t -> z ft t", z=128), s_xl)
        wait(p, "tensor", s_xl, xlw)
        if half == 0:
            def wtcs(e, t=cs3):
                e.wait_ge(s_cs.h, t)
            p.emit("tensor", wtcs)
        for tt in range(16):
            gtt = hc0 + tt
            for ct in range(8):
                tp.emit(xsh[:, ct, tt * 128:(tt + 1) * 128],
                        xT[:, tt, ct * 128:(ct + 1) * 128], idb[:, :],
                        evac_eng="scalar")
            tp.emit(bcs[:, 0, gtt * 128:(gtt + 1) * 128], BT[:, tt, :], idb[:, :],
                    evac_eng="scalar")
        tp.drain(["vector"])
        wait(p, "vector", s_ste)
        xdtv = inc_last(s_xdt)
        def fxdt(e, hc0=hc0):
            inst = None
            for tt in range(16):
                for h in range(16):
                    inst = e.tensor_scalar(out=xdtT[:, tt, h * 64:(h + 1) * 64],
                                           in0=xT[:, tt, h * 64:(h + 1) * 64],
                                           scalar1=stkT[:, hc0 + tt, h:h + 1],
                                           scalar2=None, op0=OP.mult)
            inst.then_inc(s_xdt.h, 1)
        p.emit("vector", fxdt)
        wait(p, "tensor", s_xdt, xdtv)
        wait(p, "vector", s_ac, ac_all)
        for lc in range(16):
            c = hc0 + lc
            e_prev = s_e.n
            if e_prev > 0:
                p.emit("sync", lambda e, t=e_prev: e.wait_ge(s_e.h, t))
            brw = dma(p, "sync", brow[:, :, :],
                      bass.AP(acf_h, c * 128, [[0, 128], [4096, 16], [1, 128]]),
                      s_br)
            bmv = inc_last(s_bm)
            ydma_prev = 16 * c
            def fbm(e, brw=brw, ydma_prev=ydma_prev):
                e.wait_ge(s_br.h, brw)
                if ydma_prev > 0:
                    e.wait_ge(s_yd.h, ydma_prev)
                src2 = mnegb[:, :].rearrange("z t -> z () t").broadcast_to((128, 16, 128))
                e.tensor_tensor(out=brow[:, :, :], in0=brow[:, :, :], in1=src2,
                                op=OP.add)
                e.tensor_copy(out=Sb[:, :], in_=S[:, :]).then_inc(s_bm.h, 1)
            p.emit("vector", fbm)
            gv = inc_last(s_pe)
            vv_done = s_vv.n
            def fG(e, c=c, vv_done=vv_done):
                if vv_done > 0:
                    e.wait_ge(s_vv.h, vv_done)
                e.matmul(ps[:, 0, 0:128], bcs[:, 0, c * 128:(c + 1) * 128],
                         bcs[:, 1, c * 128:(c + 1) * 128],
                         start=True, stop=True).then_inc(s_pe.h, 1)
            p.emit("tensor", fG)
            ev = inc_last(s_e)
            def fES(e, c=c, gv=gv, bmv=bmv):
                e.wait_ge(s_pe.h, gv)
                e.copy(out=gsb[:, :], in_=ps[:, 0, 0:128])
                e.wait_ge(s_bm.h, bmv)
                inst = None
                for h in range(16):
                    inst = e.activation(out=brow[:, h, :], in_=brow[:, h, :],
                                        func=AF.Exp, bias=negAc[:, c, h:h + 1])
                inst.then_inc(s_e.h, 1)
            p.emit("scalar", fES)
            mtv = inc_last(s_mt)
            def fMt(e, c=c, lc=lc, ev=ev):
                e.wait_ge(s_e.h, ev)
                for h in range(16):
                    e.tensor_tensor(out=mtall[:, h, :], in0=brow[:, h, :],
                                    in1=gsb[:, :], op=OP.mult)
                inst = None
                for h in range(16):
                    inst = e.tensor_scalar(out=xddall[:, h, :],
                                           in0=xdtT[:, lc, h * 64:(h + 1) * 64],
                                           scalar1=brow[:, h, 127:128], scalar2=None,
                                           op0=OP.mult)
                inst.then_inc(s_mt.h, 1)
            p.emit("vector", fMt)
            pev = inc_last(s_pe)
            def fPE(e, c=c, lc=lc, mtv=mtv):
                e.wait_ge(s_mt.h, mtv)
                inst = None
                for h in range(16):
                    e.matmul(ps[:, h // 8, (h % 8) * 64:(h % 8) * 64 + 64],
                             mtall[:, h, :], xdtT[:, lc, h * 64:(h + 1) * 64],
                             start=True, stop=True)
                for h in range(16):
                    e.matmul(ps[:, 2 + h // 8, (h % 8) * 64:(h % 8) * 64 + 64],
                             BT[:, lc, :], xddall[:, h, :], start=True, stop=True)
                for h in range(16):
                    inst = e.matmul(
                        tp.pool[:, 2 + h // 8, 0:1024].bitcast(F32)[:, (h % 8) * 64:(h % 8) * 64 + 64],
                        bcs[:, 1, c * 128:(c + 1) * 128], Sb[:, h * 64:(h + 1) * 64],
                        start=True, stop=True)
                inst.then_inc(s_pe.h, 1)
            p.emit("tensor", fPE)
            vv = inc_last(s_vv)
            def fEV(e, c=c, lc=lc, pev=pev):
                e.wait_ge(s_pe.h, pev)
                for h in range(16):
                    e.scalar_tensor_tensor(out=t0all[:, h, :],
                                           in0=xT[:, lc, h * 64:(h + 1) * 64],
                                           scalar=drep[:, h:h + 1],
                                           in1=ps[:, h // 8, (h % 8) * 64:(h % 8) * 64 + 64],
                                           op0=OP.mult, op1=OP.add)
                for h in range(16):
                    e.scalar_tensor_tensor(
                        out=yring[:, 0, h * 64:(h + 1) * 64],
                        in0=tp.pool[:, 2 + h // 8, 0:1024].bitcast(F32)[:, (h % 8) * 64:(h % 8) * 64 + 64],
                        scalar=odT[:, c, h:h + 1], in1=t0all[:, h, :],
                        op0=OP.mult, op1=OP.add)
                inst = None
                for h in range(16):
                    inst = e.scalar_tensor_tensor(
                        out=S[:, h * 64:(h + 1) * 64], in0=S[:, h * 64:(h + 1) * 64],
                        scalar=dchrep[:, c * 16 + h:c * 16 + h + 1],
                        in1=ps[:, 2 + h // 8, (h % 8) * 64:(h % 8) * 64 + 64],
                        op0=OP.mult, op1=OP.add)
                inst.then_inc(s_vv.h, 1)
            p.emit("vector", fEV)
            yv = s_vv.n
            def fdy(e, c=c, yv=yv):
                e.wait_ge(s_vv.h, yv)
                e.dma_start(out=ysp[c * 128:(c + 1) * 128, :],
                            in_=yring[:, 0, :]).then_inc(s_yd.h, 16)
            p.emit("sync", fdy)
            s_yd.n += 16
    zscr = Ab(16384, 1024).rearrange("z (f t) -> z f t", f=8)
    zst = Ab(17408, 1024)
    yfin = Ab(18432, 2048).rearrange("z (r t) -> z r t", r=2)
    s_ze = p.sem("ze"); s_yl = p.sem("yl"); s_yf = p.sem("yf"); s_sq = p.sem("sq")
    wait(p, "sync", s_yd)
    wait(p, "sync", s_zo)
    for tt in range(32):
        zw = dma(p, "sync", zscr[:, :, :],
                 zsp[:, tt * 128:(tt + 1) * 128].rearrange("(ft z) t -> z ft t", z=128),
                 s_ze)
        yw = dma(p, "sync", yfin[:, tt % 2, :], ysp[tt * 128:(tt + 1) * 128, :], s_yl)
        wait(p, "tensor", s_ze, zw)
        for ct in range(8):
            tp.emit(zscr[:, ct, :], zst[:, ct * 128:(ct + 1) * 128], idb[:, :],
                    evac_eng="scalar", func=AF.Silu)
        tp.drain(["vector"])
        yfv = inc_last(s_yf)
        def fyf(e, tt=tt, yw=yw):
            e.wait_ge(s_yl.h, yw)
            e.tensor_tensor(out=yfin[:, tt % 2, :], in0=yfin[:, tt % 2, :],
                            in1=zst[:, :], op=OP.mult).then_inc(s_yf.h, 1)
        p.emit("vector", fyf)
        sqv = inc_last(s_sq)
        def fsq(e, tt=tt, yfv=yfv):
            e.wait_ge(s_yf.h, yfv)
            e.activation(out=sqscr[:, 0:1024], in_=yfin[:, tt % 2, :], func=AF.Square,
                         accum_out=ssqb[:, tt:tt + 1]).then_inc(s_sq.h, 1)
        p.emit("scalar", fsq)
        def fdo(e, tt=tt, sqv=sqv):
            e.wait_ge(s_sq.h, sqv)
            e.dma_start(out=yf_o[tt * 128:(tt + 1) * 128, :],
                        in_=yfin[:, tt % 2, :]).then_inc(s_out.h, 16)
        p.emit("sync", fdo)
        s_out.n += 16
        wait(p, "tensor", s_yf, s_yf.n)
    def fdsq(e):
        e.wait_ge(s_sq.h, 32)
        e.dma_start(out=ssq_o, in_=ssqb[:, :]).then_inc(s_out.h, 16)
    p.emit("sync", fdsq)
    s_out.n += 16
    wait(p, "sync", s_out)
    if G is not None:
        G["yf_h"] = yf_h
        G["ssq_h"] = ssq_h
        G["a0_out_sem"] = s_out
    if P is None:
        return p.finish()
    return None


def host_l0attn_inmaps(inputs):
    win = inputs["L0_attn_W_in"]
    cw = inputs["L0_attn_conv_w"]
    cb = inputs["L0_attn_conv_b"]
    mneg = np.triu(np.full((128, 128), -1e30, dtype=np.float32), 1).T
    oneh = np.zeros((128, 1), dtype=np.float32)
    oneh[127, 0] = 1.0
    identb = np.eye(128, dtype=BF)
    identf = np.eye(128, dtype=np.float32)
    maps = []
    for core in range(NCORE):
        b, hg = core // 4, core % 4
        z0 = 1024 * hg
        x0 = 4096 + 1024 * hg
        maps.append(dict(
            x_in=np.ascontiguousarray(inputs["L0_attn_u"][b], dtype=np.float32),
            wz=np.ascontiguousarray(win[:, z0:z0 + 1024]),
            wx=np.ascontiguousarray(win[:, x0:x0 + 1024]),
            wbc=np.ascontiguousarray(win[:, 8192:8448]),
            wdt=np.ascontiguousarray(win[:, 8448 + 16 * hg:8448 + 16 * hg + 16]),
            cwx=np.ascontiguousarray(cw[1024 * hg:1024 * hg + 1024].reshape(8, 128, 4).transpose(1, 0, 2)),
            cbx=np.ascontiguousarray(cb[1024 * hg:1024 * hg + 1024].reshape(8, 128).T),
            cwbc=np.ascontiguousarray(cw[4096:4352].reshape(2, 128, 4).transpose(1, 0, 2)),
            cbbc=np.ascontiguousarray(cb[4096:4352].reshape(2, 128).T),
            dtbv=np.ascontiguousarray(inputs["L0_attn_dt_bias"][16 * hg:16 * hg + 16].reshape(16, 1)),
            alog=np.ascontiguousarray(inputs["L0_attn_A_log"][16 * hg:16 * hg + 16].reshape(16, 1)),
            dpar=np.ascontiguousarray(inputs["L0_attn_D_param"][16 * hg:16 * hg + 16].reshape(1, 16)),
            mneg=np.ascontiguousarray(mneg), oneh=oneh,
            identb=identb, identf=identf, cst=CST))
    return maps


def kernel(**inputs):
    r1 = run_bass_kernel_spmd(build_l0attn(), host_l0attn_inmaps(inputs),
                              core_ids=list(range(8))).results
    yf_full = np.concatenate(
        [np.concatenate([r1[b * 4 + hg]["yf_o"] for hg in range(4)], axis=1)
         for b in range(2)], axis=0)
    ssq_full = np.stack(
        [np.concatenate([r1[b * 4 + hg]["ssq_o"].T.reshape(-1) for b in range(2)])
         for hg in range(4)], axis=1)
    x_base = np.ascontiguousarray(inputs["L0_attn_u"], dtype=np.float32).reshape(8192, DM)
    r2 = run_bass_kernel_spmd(build_l0tail(),
                              host_l0tail_inmaps(inputs, yf_full, ssq_full, x_base),
                              core_ids=list(range(8))).results
    x3_full = np.concatenate([r2[c]["x3_o"] for c in range(8)], axis=0)
    xlnf_full = np.ascontiguousarray(
        np.concatenate([r2[c]["x3lnf_o"] for c in range(8)], axis=1))
    r3 = run_bass_kernel_spmd(build_l1pre(), host_l1pre_inmaps(inputs, xlnf_full),
                              core_ids=list(range(8))).results
    parts = [r3[ds]["part_o"] for ds in range(8)]
    hs_full = np.ascontiguousarray(
        np.concatenate([r3[ds]["hs_o"] for ds in range(8)], axis=0))
    r4 = run_bass_kernel_spmd(build_l1mid(), host_l1mid_inmaps(inputs, parts),
                              core_ids=list(range(8))).results
    dt_full = np.ascontiguousarray(
        np.concatenate([r4[c]["dt_o"] for c in range(8)], axis=1))
    bt = np.ascontiguousarray(np.concatenate([r4[c]["bt_o"] for c in range(8)], axis=1))
    ct = np.ascontiguousarray(np.concatenate([r4[c]["ct_o"] for c in range(8)], axis=1))
    r5 = run_bass_kernel_spmd(build_l1scan(),
                              host_l1scan_inmaps(inputs, hs_full, dt_full, bt, ct,
                                                 xlnf_full),
                              core_ids=list(range(8))).results
    ytil_full = np.ascontiguousarray(
        np.concatenate([r5[ds]["ytil_o"] for ds in range(8)], axis=0))
    r6 = run_bass_kernel_spmd(build_l1tail(),
                              host_l1tail_inmaps(inputs, ytil_full, x3_full),
                              core_ids=list(range(8))).results
    out = np.concatenate([r6[c]["xout"] for c in range(8)], axis=0)
    return out.reshape(2, 4096, DM).astype(np.float32)


def build_fused():
    p = Prog("fused")
    G = {}
    G["ygat"] = p.dram("ygat", [4096, 1024], BF16)
    G["sgat"] = p.dram("sgat", [512, 8], F32)
    G["xlnfgat"] = p.dram("xlnfgat", [16384, 1024], BF16)
    G["partsc"] = p.dram("partsc", [1024, 160], F32)
    G["dtex"] = p.dram("dtex", [4096, 1024], F32)
    G["btgat"] = p.dram("btgat", [128, 1024], F32)
    G["ctgat"] = p.dram("ctgat", [128, 1024], F32)
    G["ytex"] = p.dram("ytex", [4096, 1024], BF16)
    G["cc_yf"] = p.sem("cc_yf")
    G["cc_xlnf"] = p.sem("cc_xlnf")
    G["cc_part"] = p.sem("cc_part")
    G["cc_dt"] = p.sem("cc_dt")
    G["cc_yt"] = p.sem("cc_yt")
    scrv = p.sbuf("fscrv", [1, 8], F32)
    scrs = p.sbuf("fscrs", [1, 8], F32)
    scrg = p.sbuf("fscrg", [1, 8], F32)
    GRP4 = [[0, 1, 2, 3], [4, 5, 6, 7]]
    GRP8 = [list(range(8))]

    build_l0attn(P=p, fused=True, G=G)
    so = G["a0_out_sem"]
    def cc1(e, t=so.n):
        e.wait_ge(so.h, t)
        e.collective_compute("AllToAll", OP.bypass, replica_groups=GRP4,
                             ins=[G["yf_h"].ap().opt()],
                             outs=[G["ygat"].ap().opt()]).then_inc(G["cc_yf"].h, 1)
        e.collective_compute(
            "AllToAll", OP.bypass, replica_groups=GRP4,
            ins=[G["ssq_h"].ap().rearrange("z (r t) -> (r z) t", r=4).opt()],
            outs=[G["sgat"].ap().opt()]).then_inc(G["cc_yf"].h, 1)
    p.emit("gpsimd", cc1)
    G["cc_yf"].n += 2
    fence_all(p, scrv[:, :], scrs[:, :], scrg[:, :])

    p.prefix = "t0_"
    build_l0tail(P=p, fused=True, G=G)
    so = G["t0_out_sem"]
    def cc2(e, t=so.n):
        e.wait_ge(so.h, t)
        e.collective_compute("AllGather", OP.bypass, replica_groups=GRP8,
                             ins=[G["xlnf_h"].ap().opt()],
                             outs=[G["xlnfgat"].ap().opt()]).then_inc(G["cc_xlnf"].h, 1)
    p.emit("gpsimd", cc2)
    G["cc_xlnf"].n += 1
    fence_all(p, scrv[:, :], scrs[:, :], scrg[:, :])
    wait(p, "sync", G["cc_xlnf"])

    build_l1pre(P=p, fused=True, G=G)
    sp_, so_ = G["p1_out_sems"]
    def cc3(e, t1=sp_.n, t2=so_.n):
        e.wait_ge(sp_.h, t1)
        e.collective_compute("ReduceScatter", OP.add, replica_groups=GRP8,
                             ins=[G["part_h"].ap().opt()],
                             outs=[G["partsc"].ap().opt()]).then_inc(G["cc_part"].h, 1)
    p.emit("gpsimd", cc3)
    G["cc_part"].n += 1
    fence_all(p, scrv[:, :], scrs[:, :], scrg[:, :])

    build_l1mid(P=p, fused=True, G=G)
    so = G["m1_out_sem"]
    def cc4(e, t=so.n):
        e.wait_ge(so.h, t)
        e.collective_compute(
            "AllToAll", OP.bypass, replica_groups=GRP8,
            ins=[G["dt_h"].ap().rearrange("(r d) t -> (r d) t", r=8).opt()],
            outs=[G["dtex"].ap().opt()]).then_inc(G["cc_dt"].h, 1)
        e.collective_compute("AllGather", OP.bypass, replica_groups=GRP8,
                             ins=[G["bt_h"].ap().opt()],
                             outs=[G["btgat"].ap().opt()]).then_inc(G["cc_dt"].h, 1)
        e.collective_compute("AllGather", OP.bypass, replica_groups=GRP8,
                             ins=[G["ct_h"].ap().opt()],
                             outs=[G["ctgat"].ap().opt()]).then_inc(G["cc_dt"].h, 1)
    p.emit("gpsimd", cc4)
    G["cc_dt"].n += 3
    fence_all(p, scrv[:, :], scrs[:, :], scrg[:, :])
    wait(p, "sync", G["cc_dt"])

    build_l1scan(P=p, fused=True, G=G)
    so = G["c1_out_sem"]
    def cc5(e, t=so.n):
        e.wait_ge(so.h, t)
        e.collective_compute(
            "AllToAll", OP.bypass, replica_groups=GRP8,
            ins=[G["yt_h"].ap().rearrange("d (r t) -> (r d) t", r=8).opt()],
            outs=[G["ytex"].ap().opt()]).then_inc(G["cc_yt"].h, 1)
    p.emit("gpsimd", cc5)
    G["cc_yt"].n += 1
    fence_all(p, scrv[:, :], scrs[:, :], scrg[:, :])
    wait(p, "sync", G["cc_yt"])

    build_l1tail(P=p, fused=True, G=G)
    return p.finish()


def host_fused_inmaps(inputs):
    m1 = host_l0attn_inmaps(inputs)
    x_base = np.ascontiguousarray(inputs["L0_attn_u"], dtype=np.float32).reshape(8192, DM)
    m2 = host_l0tail_inmaps(inputs, np.zeros((8192, 4096), BF),
                            np.zeros((8192, 4), np.float32), x_base)
    m3 = host_l1pre_inmaps(inputs, np.zeros((2048, 8192), BF))
    m4 = host_l1mid_inmaps(inputs, [np.zeros((8192, 160), np.float32)] * 8)
    m5 = host_l1scan_inmaps(inputs, np.zeros((4096, 8192), BF),
                            np.zeros((4096, 8192), np.float32),
                            np.zeros((16, 8192), np.float32),
                            np.zeros((16, 8192), np.float32),
                            np.zeros((2048, 8192), BF))
    m6 = host_l1tail_inmaps(inputs, np.zeros((4096, 8192), BF),
                            np.zeros((8192, 2048), np.float32))
    drop = {"yf_in", "ssq_in", "xlnf", "part8", "hs_in", "dt_in", "bt_in",
            "ct_in", "x3_in", "ytil_in"}
    maps = []
    for c in range(8):
        mm = {}
        for pref, m in [("a0_", m1[c]), ("t0_", m2[c]), ("p1_", m3[c]),
                        ("m1_", m4[c]), ("c1_", m5[c]), ("t1_", m6[c])]:
            for k, v in m.items():
                if k in drop:
                    continue
                mm[pref + k] = v
        maps.append(mm)
    return maps
